# version 21
# speedup vs baseline: 1.1865x; 1.0480x over previous
import numpy as np
from contextlib import ExitStack, contextmanager

import concourse.bass as bass
import concourse.mybir as mybir
from concourse.bass_utils import run_bass_kernel_spmd

F32 = mybir.dt.float32
BF16 = mybir.dt.bfloat16
I32 = mybir.dt.int32
AF = mybir.ActivationFunctionType
ALU = mybir.AluOpType
AX = mybir.AxisListType
IOoA = bass.IndirectOffsetOnAxis

NCORE = 8
NTOK = 16384
D = 1024
H, DK = 16, 64
TSH = NTOK // NCORE
RBIG = 12582912.0
EXP, CAP = 8, 2560
NSLOT = 2 * NTOK
TCH = 128
NCH = 4096 // TCH
LSEQ = 4096
B = 4


ENGINES = ("v", "s", "t", "g", "q")


class P:
    def __init__(self, name="prog", fused=False):
        self.nc = bass.Bass("TRN2", target_bir_lowering=False)
        self.ctx = ExitStack()
        self.ops = []
        self.inputs = {}
        self.outputs = {}
        self._name_i = 0
        self.fused = fused
        self.wire = {}
        self.internals = {}
        self.tensors = {}
        self.prefix = ""
        self._scopes = [self.ctx]
        self.final_outputs = set()

    def din(self, name, shape, dt=F32):
        if name in self.wire:
            return self.wire[name]
        if name in self.internals:
            return self.internals[name]
        if name in self.inputs:
            return self.inputs[name]
        ap = self.nc.dram_tensor(name, list(shape), dt, kind="ExternalInput").ap()
        self.inputs[name] = ap
        return ap

    def dout(self, name, shape, dt=F32):
        if name in self.wire:
            return self.wire[name]
        if self.fused and name not in self.final_outputs:
            return self.dram(name, shape, dt)
        ap = self.nc.dram_tensor(name, list(shape), dt, kind="ExternalOutput").ap()
        self.outputs[name] = ap
        return ap

    def dram(self, name, shape, dt=F32):
        if name in self.internals:
            return self.internals[name]
        t = self.nc.dram_tensor(name, list(shape), dt)
        self.tensors[name] = t
        self.internals[name] = t.ap()
        return self.internals[name]

    def sb(self, name, shape, dt=F32):
        return self._scopes[-1].enter_context(
            self.nc.sbuf_tensor(self.prefix + name, list(shape), dt))

    def ps(self, name, shape, dt=F32):
        return self._scopes[-1].enter_context(
            self.nc.psum_tensor(self.prefix + name, list(shape), dt))

    @contextmanager
    def scope(self, prefix):
        st = ExitStack()
        self._scopes.append(st)
        old = self.prefix
        self.prefix = prefix
        try:
            yield
        finally:
            self.prefix = old
            self._scopes.pop()
            self.barrier()
            st.close()

    def barrier(self):
        self.ops.append(("B", None, (), (), "b"))

    def swdma(self, mk, reads, writes):
        def fn(e, sem):
            for arm in e.switch_core_id(n=128):
                mk(e, arm.logical % NCORE).then_inc(sem, 16)
        pf = self.prefix
        self.ops.append(("g", fn, tuple(pf + b for b in reads),
                         tuple(pf + b for b in writes), "w"))

    def coll(self, kind, alu, in_name, out_name, out_shape, dt, groups=None):
        groups = groups if groups is not None else [list(range(NCORE))]
        tin = self.tensors[in_name]
        self.dram(out_name, out_shape, dt)
        tout = self.tensors[out_name]
        self.op("g", lambda e: e.collective_compute(
            kind, alu, replica_groups=groups,
            ins=[tin.ap().opt()], outs=[tout.ap().opt()]),
            [in_name], [out_name])
        return self.internals[out_name]

    def op(self, eng, fn, reads, writes, kind="c"):
        pf = self.prefix
        self.ops.append((eng, fn, tuple(pf + b for b in reads),
                         tuple(pf + b for b in writes), kind))

    def dma(self, eng, out, in_, reads, writes, **kw):
        self.op(eng, lambda e: e.dma_start(out=out, in_=in_, **kw), reads, writes, "d")

    def emit(self):
        nc = self.nc
        sems = {}
        for e in ENGINES:
            sems[e + "c"] = self.ctx.enter_context(nc.semaphore(e + "c"))
            sems[e + "d"] = self.ctx.enter_context(nc.semaphore(e + "d"))
        cnt = {k: 0 for k in sems}
        last_w = {}
        readers = {}
        waits_per_op = []
        final_out_waits = {}
        cur_bar = {}
        for (eng, fn, reads, writes, kind) in self.ops:
            if kind == "b":
                cur_bar = {k: v for k, v in cnt.items() if v > 0}
                waits_per_op.append(None)
                continue
            semkey = eng + ("d" if kind in ("d", "w") else "c")
            amt = 16 if kind in ("d", "w") else 1
            waits = {}

            def need(sk, c):
                if c > 0 and not (sk == semkey and kind == "p"):
                    waits[sk] = max(waits.get(sk, 0), c)
            for sk, c in cur_bar.items():
                need(sk, c)
            for b in reads:
                if b in last_w:
                    need(*last_w[b])
            for b in writes:
                if b in last_w:
                    need(*last_w[b])
                for sk, c in readers.get(b, {}).items():
                    need(sk, c)
            cnt[semkey] += amt
            waits_per_op.append(waits)
            for b in reads:
                readers.setdefault(b, {})[semkey] = cnt[semkey]
            for b in writes:
                last_w[b] = (semkey, cnt[semkey])
                readers[b] = {}
        totals = dict(cnt)

        block = self.ctx.enter_context(nc.Block())
        per_eng = {e: [] for e in ENGINES}
        for i, (eng, fn, reads, writes, kind) in enumerate(self.ops):
            if kind == "b":
                continue
            per_eng[eng].append((i, fn, kind, waits_per_op[i]))
        emitted_waits = {e: {} for e in ENGINES}

        def mk_section(eng):
            ops = per_eng[eng]

            def section(handle):
                ew = emitted_waits[eng]
                run = {k: 0 for k in sems}
                for (i, fn, kind, waits) in ops:
                    for sk, c in waits.items():
                        if sk == eng + "c" and kind != "d":
                            pass
                        if ew.get(sk, 0) >= c:
                            continue
                        handle.wait_ge(sems[sk], c)
                        ew[sk] = c
                    semkey = eng + ("d" if kind in ("d", "w") else "c")
                    if kind == "w":
                        fn(handle, sems[semkey])
                    else:
                        inst = fn(handle)
                        inst.then_inc(sems[semkey], 16 if kind == "d" else 1)
                    ew[semkey] = max(ew.get(semkey, 0), 0)
            return section

        for eng in ENGINES:
            if per_eng[eng]:
                getattr(block, {"v": "vector", "s": "scalar", "t": "tensor",
                                "g": "gpsimd", "q": "sync"}[eng])(mk_section(eng))
        def fin(handle):
            for sk, tot in totals.items():
                if tot > 0 and not (sk == "qc"):
                    handle.wait_ge(sems[sk], tot)
        block.sync(fin)
        return nc

    def run(self, in_maps, **kw):
        nc = self.emit()
        res = run_bass_kernel_spmd(nc, in_maps, core_ids=list(range(len(in_maps))), **kw)
        self.ctx.close()
        return res.results


def emit_LA(p):
    x = p.din("x", [TSH, D])
    iden = p.din("iden", [128, 128], BF16)
    xn_tm = p.dout("xn_tm", [TSH, D])
    xn_tmb = p.dout("xn_tmb", [TSH, D], BF16)
    xn_fmb = p.dout("xn_fmb", [D, TSH], BF16)
    NT = TSH // 128
    sid = p.sb("sid", [128, 128], BF16)
    p.dma("q", sid[:], iden, ["iden"], ["sid"])
    bufs = []
    for k in range(2):
        bufs.append(dict(
            xt=p.sb(f"xt{k}", [128, D]), xm=p.sb(f"xm{k}", [128, D]),
            sq=p.sb(f"sq{k}", [128, D]), xnb=p.sb(f"xnb{k}", [128, D], BF16),
            mc=p.sb(f"mc{k}", [128, 1]), vc=p.sb(f"vc{k}", [128, 1]),
            trb=p.sb(f"trb{k}", [128, D], BF16), ptr=p.ps(f"ptr{k}", [128, 128], BF16)))
    for i in range(NT):
        b = bufs[i % 2]
        ki = f"_{i%2}"
        xt, xm, sq, xnb, mc, vc, trb, ptr = (b[n] for n in ("xt", "xm", "sq", "xnb", "mc", "vc", "trb", "ptr"))
        p.dma("q", xt[:], x[i*128:(i+1)*128, :], ["x"], ["xt"+ki])
        p.op("v", lambda e, xt=xt, mc=mc: e.reduce_sum(mc[:], xt[:], AX.X), ["xt"+ki], ["mc"+ki])
        p.op("v", lambda e, mc=mc: e.tensor_scalar(mc[:], mc[:], 1.0/D, None, ALU.mult), ["mc"+ki], ["mc"+ki])
        p.op("v", lambda e, xt=xt, xm=xm, mc=mc: e.tensor_scalar(xm[:], xt[:], mc[:], None, ALU.subtract), ["xt"+ki, "mc"+ki], ["xm"+ki])
        p.op("s", lambda e, xm=xm, sq=sq, vc=vc: e.activation(sq[:], xm[:], AF.Square, accum_out=vc[:]), ["xm"+ki], ["sq"+ki, "vc"+ki])
        p.op("v", lambda e, vc=vc: e.tensor_scalar(vc[:], vc[:], 1.0/D, 1e-5, ALU.mult, ALU.add), ["vc"+ki], ["vc"+ki])
        p.op("s", lambda e, vc=vc: e.activation(vc[:], vc[:], AF.Sqrt), ["vc"+ki], ["vc"+ki])
        p.op("v", lambda e, vc=vc: e.reciprocal(vc[:], vc[:]), ["vc"+ki], ["vc"+ki])
        p.op("v", lambda e, xm=xm, vc=vc: e.tensor_scalar(xm[:], xm[:], vc[:], None, ALU.mult), ["xm"+ki, "vc"+ki], ["xm"+ki])
        p.op("v", lambda e, xm=xm, xnb=xnb: e.tensor_copy(xnb[:], xm[:]), ["xm"+ki], ["xnb"+ki])
        p.dma("q", xn_tm[i*128:(i+1)*128, :], xm[:], ["xm"+ki], ["xn_tm"])
        p.dma("q", xn_tmb[i*128:(i+1)*128, :], xnb[:], ["xnb"+ki], ["xn_tmb"])
        for f in range(8):
            p.op("t", lambda e, xnb=xnb, ptr=ptr, f=f: e.transpose(ptr[:], xnb[:, f*128:(f+1)*128], sid[:]),
                 ["xnb"+ki, "sid"], ["ptr"+ki])
            p.op("v", lambda e, ptr=ptr, trb=trb, f=f: e.tensor_copy(trb[:, f*128:(f+1)*128], ptr[:]),
                 ["ptr"+ki], ["trb"+ki])
        p.dma("g", xn_fmb.rearrange("(f p) n -> p f n", p=128)[:, :, i*128:(i+1)*128],
              trb.rearrange("p (f n) -> p f n", f=8), ["trb"+ki], ["xn_fmb"])


def emit_LB1(p):
    xn8 = p.din("xn_fmb8", [NCORE * D, TSH], BF16)
    xnv = xn8.rearrange("(blk t pp) n -> pp blk t n", blk=NCORE, t=8)
    wq = p.din("wq", [D, 128])
    wk = p.din("wk", [D, 128])
    wv = p.din("wv", [D, 128])
    iden = p.din("iden", [128, 128], BF16)
    q_fmb = p.dout("q_fmb", [128, NTOK], BF16)
    k_fmb = p.dout("k_fmb", [128, NTOK], BF16)
    kn_fmb = p.dout("kn_fmb", [128, NTOK], BF16)
    v_fmb = p.dout("v_fmb", [128, NTOK], BF16)
    qT = p.dout("qT", [2, NTOK, 64], BF16)
    knT = p.dout("knT", [2, NTOK, 64], BF16)
    vTf = p.dout("vTf", [2, NTOK, 64])
    vTb = p.dout("vTb", [2, NTOK, 64], BF16)

    idenf = p.din("idenf", [128, 128])
    sid = p.sb("sid", [128, 128], BF16)
    sidf = p.sb("sidf", [128, 128])
    p.dma("q", sid[:], iden, ["iden"], ["sid"])
    p.dma("q", sidf[:], idenf, ["idenf"], ["sidf"])
    wbf = {}
    for nm, ap in (("wq", wq), ("wk", wk), ("wv", wv)):
        wf = p.sb(nm + "f", [128, 8 * 128])
        wb = p.sb(nm + "b", [128, 8 * 128], BF16)
        p.dma("q", wf.rearrange("p (t c) -> p t c", t=8), ap.rearrange("(t p) c -> p t c", p=128),
              [nm], [nm + "f"])
        p.op("v", lambda e, wf=wf, wb=wb: e.tensor_copy(wb[:], wf[:]), [nm + "f"], [nm + "b"])
        wbf[nm] = wb

    xnb = p.sb("xnb", [128, 8 * LSEQ], BF16)
    pp = [p.ps(f"pp{i}", [128, 512]) for i in range(2)]
    pev = p.sb("pev", [128, 512])
    pevb = p.sb("pevb", [128, 512], BF16)
    ptT = p.ps("ptT", [128, 64])
    ptTb = p.ps("ptTb", [128, 64], BF16)
    tT = p.sb("tT", [128, 64])
    tTb = p.sb("tTb", [128, 64], BF16)
    mc = p.sb("mc", [128, 1]); vc = p.sb("vc", [128, 1]); sqs = p.sb("sqs", [128, 64])
    pback = p.ps("pback", [64, 128], BF16)
    knc = p.sb("knc", [128, 512], BF16)

    for b in range(B):
        for hb in range(2):
            p.dma("q", xnb.rearrange("p (t n) -> p t n", t=8)[:, :, hb * TSH:(hb + 1) * TSH],
                  xnv[:, 2 * b + hb, :, :], ["xn_fmb8"], ["xnb"])
        for nm, fmout, wantT in (("wq", q_fmb, "qT"), ("wk", k_fmb, "kT"),
                                 ("wv", v_fmb, "vT")):
            wb = wbf[nm]
            for nchunk in range(8):
                n0 = nchunk * 512
                sl = nchunk % 2
                for kt in range(8):
                    p.op("t", lambda e, wb=wb, kt=kt, n0=n0, sl=sl:
                         e.matmul(pp[sl][:], wb[:, kt * 128: (kt + 1) * 128],
                                  xnb[:, kt * LSEQ + n0: kt * LSEQ + n0 + 512],
                                  start=(kt == 0), stop=(kt == 7)),
                         [nm + "b", "xnb"], [f"pp{sl}"], "p" if kt else "c")
                p.op("s", lambda e, sl=sl: e.copy(pev[:], pp[sl][:]), [f"pp{sl}"], ["pev"])
                p.op("v", lambda e: e.tensor_copy(pevb[:], pev[:]), ["pev"], ["pevb"])
                p.dma("q", fmout[:, b * LSEQ + n0: b * LSEQ + n0 + 512],
                      pevb[:], ["pevb"], [nm + "fm"])
                for cc in range(4):
                    tok0 = b * LSEQ + n0 + cc * 128
                    for hh in range(2):
                        if nm == "wq":
                            p.op("t", lambda e, cc=cc, hh=hh: e.transpose(
                                ptTb[:], pevb[hh * 64:(hh + 1) * 64, cc * 128:(cc + 1) * 128],
                                sid[hh * 64:(hh + 1) * 64, hh * 64:(hh + 1) * 64]),
                                 ["pevb", "sid"], ["ptTb"])
                            p.op("v", lambda e: e.tensor_copy(tTb[:], ptTb[:]), ["ptTb"], ["tTb"])
                            p.dma("q", qT[hh, tok0:tok0 + 128, :], tTb[:], ["tTb"], ["qT"])
                        elif nm == "wv":
                            p.op("t", lambda e, cc=cc, hh=hh: e.transpose(
                                ptTb[:], pevb[hh * 64:(hh + 1) * 64, cc * 128:(cc + 1) * 128],
                                sid[hh * 64:(hh + 1) * 64, hh * 64:(hh + 1) * 64]),
                                 ["pevb", "sid"], ["ptTb"])
                            p.op("v", lambda e: e.tensor_copy(tTb[:], ptTb[:]), ["ptTb"], ["tTb"])
                            p.op("v", lambda e: e.tensor_copy(tT[:], ptTb[:]), ["ptTb"], ["tT"])
                            p.dma("q", vTb[hh, tok0:tok0 + 128, :], tTb[:], ["tTb"], ["vTb"])
                            p.dma("g", vTf[hh, tok0:tok0 + 128, :], tT[:], ["tT"], ["vTf"])
                        else:
                            p.op("t", lambda e, cc=cc, hh=hh: e.transpose(
                                ptT[:], pev[hh * 64:(hh + 1) * 64, cc * 128:(cc + 1) * 128],
                                sidf[hh * 64:(hh + 1) * 64, hh * 64:(hh + 1) * 64]),
                                 ["pev", "sidf"], ["ptT"])
                            p.op("v", lambda e: e.tensor_copy(tT[:], ptT[:]), ["ptT"], ["tT"])
                            p.op("v", lambda e: e.reduce_sum(mc[:], tT[:], AX.X), ["tT"], ["mc"])
                            p.op("v", lambda e: e.tensor_scalar(mc[:], mc[:], 1.0 / 64, None, ALU.mult), ["mc"], ["mc"])
                            p.op("v", lambda e: e.tensor_scalar(tT[:], tT[:], mc[:], None, ALU.subtract), ["tT", "mc"], ["tT"])
                            p.op("s", lambda e: e.activation(sqs[:], tT[:], AF.Square, accum_out=vc[:]), ["tT"], ["sqs", "vc"])
                            p.op("v", lambda e: e.tensor_scalar(vc[:], vc[:], 1.0 / 64, 1e-5, ALU.mult, ALU.add), ["vc"], ["vc"])
                            p.op("s", lambda e: e.activation(vc[:], vc[:], AF.Sqrt), ["vc"], ["vc"])
                            p.op("v", lambda e: e.reciprocal(vc[:], vc[:]), ["vc"], ["vc"])
                            p.op("v", lambda e: e.tensor_scalar(tT[:], tT[:], vc[:], None, ALU.mult), ["tT", "vc"], ["tT"])
                            p.op("v", lambda e: e.tensor_copy(tTb[:], tT[:]), ["tT"], ["tTb"])
                            p.dma("q", knT[hh, tok0:tok0 + 128, :], tTb[:], ["tTb"], ["knT"])
                            p.op("t", lambda e: e.transpose(pback[:], tTb[:], sid[:]), ["tTb", "sid"], ["pback"])
                            p.op("v", lambda e, cc=cc, hh=hh: e.tensor_copy(
                                knc[hh * 64:(hh + 1) * 64, cc * 128:(cc + 1) * 128], pback[:]),
                                 ["pback"], ["knc"])
                if nm == "wk":
                    p.dma("g", kn_fmb[:, b * LSEQ + n0: b * LSEQ + n0 + 512],
                          knc[:], ["knc"], ["kn_fmb"])


def emit_LB2(p):
    qf = p.din("q_fmb", [128, NTOK], BF16)
    kf = p.din("k_fmb", [128, NTOK], BF16)
    vtb = p.din("vTb", [2, NTOK, 64], BF16)
    cmask = p.din("cmask", [4, 128, 512], BF16)
    ysf = p.dout("ysoft_fm", [128, NTOK], BF16)
    qs = p.sb("qs", [64, LSEQ], BF16)
    ks = p.sb("ks", [64, LSEQ], BF16)
    vte = p.sb("vte", [128, 32 * 65], BF16)
    msk = p.sb("msk", [4, 128 * 512 // 128], BF16)
    cm = p.sb("cm", [128, 4 * 512], BF16)
    ones1 = p.sb("ones1", [1, 64])
    rs = p.sb("rs", [1, 512])
    rsb = p.sb("rsb", [64, 512])
    yo = p.sb("yo", [64, 512])
    pS = [p.ps(f"pS{i}", [128, 512]) for i in range(2)]
    pt = [p.sb(f"pt{i}", [128, 512], BF16) for i in range(2)]
    pY = p.ps("pY", [65, 512])
    pR = p.ps("pR", [64, 512])
    p.op("v", lambda e: e.memset(ones1[:], 1.0), [], ["ones1"])
    p.dma("q", cm.rearrange("p (m n) -> p m n", m=4), cmask.rearrange("m p n -> p m n"), ["cmask"], ["cm"])
    for hh in range(2):
        for b in range(B):
            t0 = b * LSEQ
            p.dma("q", qs[:], qf[hh * 64:(hh + 1) * 64, t0:t0 + LSEQ], ["q_fmb"], ["qs"])
            p.dma("q", ks[:], kf[hh * 64:(hh + 1) * 64, t0:t0 + LSEQ], ["k_fmb"], ["ks"])
            p.dma("g", vte.rearrange("p (k c) -> p k c", k=32)[:, :, 0:64],
                  vtb[hh, t0:t0 + LSEQ, :].rearrange("(k p) c -> p k c", p=128),
                  ["vTb"], ["vte"])
            p.op("v", lambda e: e.memset(vte.rearrange("p (k c) -> p k c", k=32)[:, :, 64:65], 1.0),
                 ["vte"], ["vte"])
            for qc in range(8):
                nkt = 4 * qc + 4

                def emit_S(kt, qc=qc):
                    i = kt % 2
                    p.op("t", lambda e, kt=kt, qc=qc, i=i:
                         e.matmul(pS[i][:], ks[:, kt * 128:(kt + 1) * 128],
                                  qs[:, qc * 512:(qc + 1) * 512], start=True, stop=True),
                         ["ks", "qs"], [f"pS{i}"])
                    p.op("s", lambda e, i=i: e.activation(pt[i][:], pS[i][:], AF.Exp, scale=0.125),
                         [f"pS{i}"], [f"pt{i}"])
                    if kt >= 4 * qc:
                        moff = kt - 4 * qc
                        p.op("v", lambda e, i=i, moff=moff:
                             e.tensor_mul(pt[i][:], pt[i][:], cm[:, moff * 512:(moff + 1) * 512]),
                             [f"pt{i}", "cm"], [f"pt{i}"])

                def emit_PV(kt, nkt=nkt):
                    i = kt % 2
                    p.op("t", lambda e, kt=kt, i=i, nkt=nkt:
                         e.matmul(pY[:], vte[:, kt * 65:(kt + 1) * 65], pt[i][:],
                                  start=(kt == 0), stop=(kt == nkt - 1)),
                         ["vte", f"pt{i}"], ["pY"], "p" if kt else "c")

                emit_S(0)
                for kt in range(1, nkt):
                    emit_S(kt)
                    emit_PV(kt - 1)
                emit_PV(nkt - 1)
                p.op("v", lambda e: e.reciprocal(rs[:], pY[64:65, :]), ["pY"], ["rs"])
                p.op("t", lambda e: e.matmul(pR[:], ones1[:], rs[:], start=True, stop=True),
                     ["ones1", "rs"], ["pR"])
                p.op("s", lambda e: e.copy(rsb[:], pR[:]), ["pR"], ["rsb"])
                p.op("v", lambda e: e.tensor_mul(yo[:], pY[0:64, :], rsb[:]), ["pY", "rsb"], ["yo"])
                p.dma("g", ysf[hh * 64:(hh + 1) * 64, t0 + qc * 512: t0 + (qc + 1) * 512],
                      yo[:], ["yo"], ["ysf"])


def _emit_tables(p, wltm, vgs, vt0, btau, bgm, cch, gam, alphap, icthr, idenf):
    out = {}
    sidf = p.sb("sidf_t", [64, 64])
    p.dma("q", sidf[:], idenf[0:64, 0:64], ["idenf"], ["sidf_t"])
    wl = p.sb("wl", [64, 2 * 64])
    p.dma("q", wl.rearrange("p (h e) -> p h e", h=2), wltm.rearrange("h d e -> d h e"),
          ["wltm"], ["wl"])
    cols = {}
    for nm, ap in (("vgs", vgs), ("vt0", vt0), ("btau", btau), ("bgm", bgm),
                   ("cch", cch), ("gam", gam)):
        t = p.sb("col_" + nm, [64, 2])
        p.dma("q", t[:, 0:1], ap[0:1].unsqueeze(0).to_broadcast([64, 1]), [nm], ["col_" + nm])
        p.dma("q", t[:, 1:2], ap[1:2].unsqueeze(0).to_broadcast([64, 1]), [nm], ["col_" + nm])
        cols[nm] = t
    alc = p.sb("col_al", [64, 1])
    icc = p.sb("col_ic", [64, 1])
    p.dma("q", alc[:], alphap[0:1].unsqueeze(0).to_broadcast([64, 1]), ["alphap"], ["col_al"])
    p.dma("q", icc[:], icthr[0:1].unsqueeze(0).to_broadcast([64, 1]), ["icthr"], ["col_ic"])
    r = p.sb("r_t", [64, 2 * 64])
    c = p.sb("c_t", [64, 2 * 64])
    t1 = p.sb("t1_t", [64, 64]); t2 = p.sb("t2_t", [64, 64]); t3 = p.sb("t3_t", [64, 64])
    t4 = p.sb("t4_t", [64, 64]); t5 = p.sb("t5_t", [64, 64])
    rcp = p.sb("rcp_t", [64, 1])
    for h in range(2):
        wlh = wl[:, h * 64:(h + 1) * 64]
        hh_ = h
        def V(fn, reads, writes):
            p.op("v", fn, reads, writes)
        def S(fn, reads, writes):
            p.op("s", fn, reads, writes)
        V(lambda e, wlh=wlh, h=hh_: e.tensor_scalar(t1[:], wlh, cols["vgs"][:, h:h+1], cols["vt0"][:, h:h+1], ALU.add, ALU.subtract), ["wl", "col_vgs", "col_vt0"], ["t1"])
        S(lambda e: e.activation(t2[:], t1[:], AF.Abs), ["t1"], ["t2"])
        S(lambda e: e.activation(t2[:], t2[:], AF.Exp, scale=-1.0), ["t2"], ["t2"])
        V(lambda e: e.tensor_scalar(t2[:], t2[:], 1.0, None, ALU.add), ["t2"], ["t2"])
        S(lambda e: e.activation(t2[:], t2[:], AF.Ln), ["t2"], ["t2"])
        S(lambda e: e.activation(t3[:], t1[:], AF.Relu), ["t1"], ["t3"])
        V(lambda e: e.tensor_add(t2[:], t2[:], t3[:]), ["t2", "t3"], ["t2"])
        V(lambda e, h=hh_: e.tensor_scalar(t3[:], t2[:], cols["btau"][:, h:h+1], None, ALU.mult), ["t2", "col_btau"], ["t3"])
        V(lambda e, h=hh_: e.reciprocal(rcp[:], cols["cch"][:, h:h+1]), ["col_cch"], ["rcp"])
        V(lambda e: e.tensor_scalar(t5[:], t3[:], rcp[:], None, ALU.mult), ["t3", "rcp"], ["t5"])
        S(lambda e: e.activation(t5[:], t5[:], AF.Exp, scale=-1.0), ["t5"], ["t5"])
        V(lambda e, h=hh_: e.tensor_copy(r[:, h*64:(h+1)*64], t5[:]), ["t5"], ["r_t"])
        S(lambda e: e.activation(t4[:], t1[:], AF.Sigmoid), ["t1"], ["t4"])
        V(lambda e: e.tensor_mul(t2[:], t2[:], t4[:]), ["t2", "t4"], ["t2"])
        V(lambda e, h=hh_: e.tensor_scalar(t2[:], t2[:], cols["bgm"][:, h:h+1], None, ALU.mult), ["t2", "col_bgm"], ["t2"])
        V(lambda e: e.tensor_scalar(t3[:], t3[:], icc[:], None, ALU.subtract), ["t3", "col_ic"], ["t3"])
        V(lambda e: e.tensor_scalar(t3[:], t3[:], alc[:], None, ALU.mult), ["t3", "col_al"], ["t3"])
        S(lambda e: e.activation(t3[:], t3[:], AF.Tanh), ["t3"], ["t3"])
        V(lambda e: e.tensor_mul(t2[:], t2[:], t3[:]), ["t2", "t3"], ["t2"])
        V(lambda e, h=hh_: e.tensor_scalar(c[:, h*64:(h+1)*64], t2[:], cols["gam"][:, h:h+1], None, ALU.mult), ["t2", "col_gam"], ["c_t"])
    out["r"] = r; out["c"] = c; out["sidf"] = sidf; out["cols"] = cols
    return out


def emit_LB3a(p):
    knT = p.din("knT", [2, NTOK, 64], BF16)
    vTf = p.din("vTf", [2, NTOK, 64])
    wltm = p.din("wltm", [2, 64, 64])
    vgs = p.din("vgs", [2]); vt0 = p.din("vt0", [2]); btau = p.din("btau", [2])
    bgm = p.din("bgm", [2]); cch = p.din("cch", [2]); gam = p.din("gam", [2])
    alphap = p.din("alphap", [1]); icthr = p.din("icthr", [1])
    idenf = p.din("idenf", [128, 128])
    iden = p.din("iden", [128, 128], BF16)
    FbH = p.dout("FbH", [2, B, NCH, 4096], BF16)
    rpb = p.dout("rpb", [2, 64, 128 * 64], BF16)
    ccb = p.dout("ccb", [2, 64, 64])
    sstH = p.dout("sstH", [2, B, NCH, 4096], BF16)

    tb = _emit_tables(p, wltm, vgs, vt0, btau, bgm, cch, gam, alphap, icthr, idenf)
    r, c = tb["r"], tb["c"]
    sid = p.sb("sid", [128, 128], BF16)
    p.dma("q", sid[:], iden, ["iden"], ["sid"])
    PS = p.sb("PS", [64, 128 * 64])
    PSb = p.sb("PSb", [64, 128 * 64], BF16)
    RcPb = p.sb("RcPb", [64, 128 * 64], BF16)
    r128T = p.sb("r128T", [64, 2 * 64])
    pr128 = p.ps("pr128", [64, 64])
    rcb_hbm = p.dout("rcb", [2, 64, 128 * 64], BF16)
    for h in range(2):
        rh = r[:, h * 64:(h + 1) * 64]
        ch = c[:, h * 64:(h + 1) * 64]
        p.op("v", lambda e, rh=rh: e.tensor_copy(PS[:, 0:64], rh), ["r_t"], ["PS"])
        for j in range(1, 128):
            p.op("v", lambda e, j=j, rh=rh: e.tensor_mul(PS[:, j*64:(j+1)*64], PS[:, (j-1)*64:j*64], rh),
                 ["PS", "r_t"], ["PS"])
        p.op("v", lambda e: e.tensor_copy(PSb[:], PS[:]), ["PS"], ["PSb"])
        p.dma("q", rpb[h], PSb[:], ["PSb"], ["rpb"])
        p.dma("q", ccb[h], ch, ["c_t"], ["ccb"])
        for j in range(128):
            if j == 127:
                p.op("v", lambda e, ch=ch: e.tensor_copy(RcPb[:, 127*64:128*64], ch), ["c_t"], ["RcPb"])
            else:
                p.op("v", lambda e, j=j, ch=ch: e.tensor_mul(RcPb[:, j*64:(j+1)*64], PS[:, (126-j)*64:(127-j)*64], ch),
                     ["PS", "c_t"], ["RcPb"])
        p.dma("q", rcb_hbm[h], RcPb[:], ["RcPb"], ["rcb"])
        p.op("t", lambda e: e.transpose(pr128[:], PS[:, 127*64:128*64], tb["sidf"][:]),
             ["PS", "sidf_t"], ["pr128"])
        p.op("v", lambda e, h=h: e.tensor_copy(r128T[:, h*64:(h+1)*64], pr128[:]), ["pr128"], ["r128T"])
    rrep = p.sb("rrep", [128, 2 * 32 * 32])
    for h in range(2):
        for dt in range(32):
            for half in range(2):
                d = 2 * dt + half
                p.op("v", lambda e, h=h, dt=dt, half=half, d=d:
                     e.tensor_copy(rrep[half*64:(half+1)*64, (h*32+dt)*32:(h*32+dt+1)*32],
                                   r128T[:, h*64+d:h*64+d+1].to_broadcast([64, 32])),
                     ["r128T"], ["rrep"])
    Rcs = p.sb("Rcs", [128, 2 * 64 * 64], BF16)
    p.dma("q", Rcs.rearrange("p (h d e) -> p h d e", h=2, d=64),
          rcb_hbm.rearrange("h d (j e) -> j h d e", e=64), ["rcb"], ["Rcs"])
    ones128 = p.sb("ones128", [128, 1], BF16)
    p.op("v", lambda e: e.memset(ones128[:], 1.0), [], ["ones128"])
    knTs = p.sb("knTs", [128, 32 * 64], BF16)
    vTs = p.sb("vTs", [128, 32 * 64])
    kv1 = p.sb("kv1", [128, 2048], BF16)
    kv2 = [p.sb(f"kv2{i}", [128, 2048], BF16) for i in range(2)]
    psA = p.ps("psA", [128, 1024])
    ptb = p.ps("ptb", [128, 128], BF16)
    srow = p.sb("srow", [1, 4096], BF16)
    sstS = p.sb("sstS", [32, 4096], BF16)
    Stile = p.sb("Stile", [128, 32])
    Fall = p.sb("Fall", [128, 32])
    FallB = p.sb("FallB", [128, 32], BF16)
    fT = p.sb("fT", [32, 128], BF16)
    for hh in range(2):
        for b in range(B):
            t0 = b * LSEQ
            p.dma("q", knTs.rearrange("p (m e) -> p m e", m=32),
                  knT[hh, t0:t0 + LSEQ, :].rearrange("(m p) e -> p m e", p=128),
                  ["knT"], ["knTs"])
            p.dma("g", vTs.rearrange("p (m e) -> p m e", m=32),
                  vTf[hh, t0:t0 + LSEQ, :].rearrange("(m p) e -> p m e", p=128),
                  ["vTf"], ["vTs"])
            for m in range(NCH):
                for half in range(2):
                    d0 = half * 32
                    p.op("v", lambda e, m=m, hh=hh, d0=d0: e.tensor_tensor(
                        kv1[:].rearrange("p (a b) -> p a b", a=32),
                        knTs[:, m*64:(m+1)*64].unsqueeze(1).to_broadcast([128, 32, 64]),
                        Rcs[:, (hh*64+d0)*64:(hh*64+d0+32)*64].rearrange("p (a b) -> p a b", a=32),
                        ALU.mult), ["knTs", "Rcs"], ["kv1"])
                    i2 = half
                    p.op("v", lambda e, m=m, d0=d0, i2=i2: e.tensor_tensor(
                        kv2[i2][:].rearrange("p (a b) -> p a b", a=32),
                        kv1[:].rearrange("p (a b) -> p a b", a=32),
                        vTs[:, m*64+d0:m*64+d0+32].unsqueeze(2).to_broadcast([128, 32, 64]),
                        ALU.mult), ["kv1", "vTs"], [f"kv2{i2}"])
                    for dgrp in range(4):
                        p.op("t", lambda e, dgrp=dgrp, i2=i2: e.matmul(
                            psA[0:1, (dgrp % 2)*512:(dgrp % 2 + 1)*512],
                            ones128[:], kv2[i2][:, dgrp*512:(dgrp+1)*512], start=True, stop=True),
                             ["ones128", f"kv2{i2}"], ["psW2"])
                        if dgrp % 2 == 1:
                            p.op("s", lambda e, half=half, dgrp=dgrp: e.copy(
                                srow[:, half*2048+(dgrp//2)*1024:half*2048+(dgrp//2)*1024+1024],
                                psA[0:1, :]), ["psW2"], ["srow"])
                p.dma("q", sstS[m:m+1, :], srow[:], ["srow"], ["sstS"])
            for dt in range(32):
                p.op("t", lambda e, dt=dt: e.transpose(ptb[0:128, 0:32], sstS[:, dt*128:(dt+1)*128], sid[0:32, 0:32]),
                     ["sstS", "sid"], ["ptb"])
                p.op("v", lambda e: e.tensor_copy(Stile[:], ptb[0:128, 0:32]), ["ptb"], ["Stile"])
                p.op("v", lambda e, hh=hh, dt=dt:
                     e.tensor_tensor_scan(Fall[:], rrep[:, (hh*32+dt)*32:(hh*32+dt+1)*32], Stile[:],
                                          0.0, ALU.mult, ALU.add),
                     ["rrep", "Stile"], ["Fall"])
                p.op("v", lambda e: e.tensor_copy(FallB[:], Fall[:]), ["Fall"], ["FallB"])
                p.op("t", lambda e: e.transpose(ptb[0:32, 0:128], FallB[:], sid[:]),
                     ["FallB", "sid"], ["ptb2"])
                p.op("v", lambda e: e.tensor_copy(fT[:], ptb[0:32, 0:128]), ["ptb2"], ["fT"])
                p.dma("q", FbH[hh, b][:, dt*128:(dt+1)*128], fT[:], ["fT"], ["FbH"])


def emit_LB3b(p):
    qfm = p.din("q_fmb", [128, NTOK], BF16)
    knfm = p.din("kn_fmb", [128, NTOK], BF16)
    vfm = p.din("v_fmb", [128, NTOK], BF16)
    qT = p.din("qT", [2, NTOK, 64], BF16)
    FbH = p.din("FbH", [2, B, NCH, 4096], BF16)
    rpb = p.din("rpb", [2, 64, 128 * 64], BF16)
    ccb = p.din("ccb", [2, 64, 64])
    wltm = p.din("wltm", [2, 64, 64])
    gate = p.din("gate", [2])
    ysf = p.din("ysoft_fm", [128, NTOK], BF16)
    idenf = p.din("idenf", [128, 128])
    iden = p.din("iden", [128, 128], BF16)
    y0 = p.dout("y0_fmb", [128, NTOK], BF16)

    sid = p.sb("sid", [128, 128], BF16)
    sidf = p.sb("sidf", [128, 128])
    p.dma("q", sid[:], iden, ["iden"], ["sid"])
    p.dma("q", sidf[:], idenf, ["idenf"], ["sidf"])
    Rp = p.sb("Rp", [128, 2 * 4096], BF16)
    p.dma("q", Rp.rearrange("p (h d e) -> p h d e", h=2, d=64),
          rpb.rearrange("h d (j e) -> j h d e", e=64), ["rpb"], ["Rp"])
    Pblk = p.sb("Pblk", [128, 128 * 128], BF16)
    p.op("v", lambda e: e.memset(Pblk[:], 0.0), [], ["Pblk"])
    rpn = p.sb("rpn", [64, 128 * 64], BF16)
    ccbf = p.sb("ccbf", [64, 2 * 64], BF16)
    pl = p.sb("pl", [64, 64], BF16)
    pt1 = p.ps("pt1", [64, 64], BF16)
    p.dma("g", ccbf.rearrange("p (h e) -> p h e", h=2), ccb.rearrange("h d e -> d h e"),
          ["ccb"], ["ccbf"])
    for h in range(2):
        p.dma("q", rpn[:], rpb[h], ["rpb"], ["rpn"])
        for l in range(128):
            if l == 0:
                p.op("v", lambda e, h=h: e.tensor_copy(pl[:], ccbf[:, h*64:(h+1)*64]), ["ccbf"], ["pl"])
            else:
                p.op("v", lambda e, l=l, h=h: e.tensor_mul(pl[:], rpn[:, (l-1)*64:l*64], ccbf[:, h*64:(h+1)*64]),
                     ["rpn", "ccbf"], ["pl"])
            p.op("t", lambda e: e.transpose(pt1[:], pl[:], sid[0:64, 0:64]), ["pl", "sid"], ["pt1"])
            p.op("v", lambda e, l=l, h=h: e.tensor_copy(Pblk[h*64:(h+1)*64, l*128+h*64:l*128+(h+1)*64], pt1[:]),
                 ["pt1"], ["Pblk"])
    wl = p.sb("wl2", [64, 2 * 64])
    wlb = p.sb("wlb", [64, 64], BF16)
    wlT = p.sb("wlT", [128, 128], BF16)
    ptf = p.ps("ptf", [64, 64])
    p.dma("q", wl.rearrange("p (h e) -> p h e", h=2), wltm.rearrange("h d e -> d h e"),
          ["wltm"], ["wl2"])
    p.op("v", lambda e: e.memset(wlT[:], 0.0), [], ["wlT"])
    for h in range(2):
        p.op("t", lambda e, h=h: e.transpose(ptf[:], wl[:, h*64:(h+1)*64], sidf[0:64, 0:64]),
             ["wl2", "sidf"], ["ptf"])
        p.op("v", lambda e: e.tensor_copy(wlb[:], ptf[:]), ["ptf"], ["wlb"])
        p.op("v", lambda e, h=h: e.tensor_copy(wlT[h*64:(h+1)*64, h*64:(h+1)*64], wlb[:]), ["wlb"], ["wlT"])
    gg = p.sb("gg", [128, 1])
    omg = p.sb("omg", [128, 1])
    p.dma("q", gg[0:64, :], gate[0:1].unsqueeze(0).to_broadcast([64, 1]), ["gate"], ["gg"])
    p.dma("q", gg[64:128, :], gate[1:2].unsqueeze(0).to_broadcast([64, 1]), ["gate"], ["gg"])
    p.op("s", lambda e: e.activation(gg[:], gg[:], AF.Sigmoid), ["gg"], ["gg"])
    p.op("v", lambda e: e.tensor_scalar(omg[:], gg[:], -1.0, 1.0, ALU.mult, ALU.add), ["gg"], ["omg"])

    q2 = p.sb("q2", [128, LSEQ], BF16)
    kn2 = p.sb("kn2", [128, LSEQ], BF16)
    v2 = p.sb("v2", [128, LSEQ], BF16)
    qTs = p.sb("qTs", [128, 2 * 32 * 64], BF16)
    Fb = [p.sb(f"Fb{i}", [128, 4096], BF16) for i in range(3)]
    Psi = p.sb("Psi", [128, 4096], BF16)
    P2 = p.sb("P2", [128, 4096], BF16)
    yfT = p.sb("yfT", [128, 128], BF16)
    yfb = p.sb("yfb", [128, 128])
    y2acc = p.sb("y2acc", [128, LSEQ])
    g2sc = p.sb("g2sc", [128, LSEQ], BF16)
    tmpb = p.sb("tmpb", [128, 512], BF16)
    ys2 = p.sb("ys2", [128, LSEQ], BF16)
    psA = p.ps("psA", [128, 2048])
    ptb = p.ps("ptb", [128, 128], BF16)
    for b in range(B):
        t0 = b * LSEQ
        p.dma("q", q2[:], qfm[:, t0:t0 + LSEQ], ["q_fmb"], ["q2"])
        p.dma("q", kn2[:], knfm[:, t0:t0 + LSEQ], ["kn_fmb"], ["kn2"])
        p.dma("q", v2[:], vfm[:, t0:t0 + LSEQ], ["v_fmb"], ["v2"])
        for hh in range(2):
            p.dma("q", qTs.rearrange("p (h m e) -> p h m e", h=2, m=32)[:, hh],
                  qT[hh, t0:t0 + LSEQ, :].rearrange("(m p) e -> p m e", p=128),
                  ["qT"], ["qTs"])
        for nc_ in range(8):
            p.op("t", lambda e, nc_=nc_: e.matmul(psA[:, (nc_%4)*512:(nc_%4+1)*512], wlT[:],
                                                  q2[:, nc_*512:(nc_+1)*512], start=True, stop=True),
                 ["wlT", "q2"], [f"psW{nc_%4}"])
            p.op("v", lambda e, nc_=nc_: e.tensor_copy(y2acc[:, nc_*512:(nc_+1)*512],
                                                       psA[:, (nc_%4)*512:(nc_%4+1)*512]),
                 [f"psW{nc_%4}"], ["y2acc"])
        for pre in range(2):
            p.dma("g", Fb[pre][:], FbH[pre % 2, b, 0:1, :].to_broadcast([128, 4096]),
                  ["FbH"], [f"Fb{pre}"])
        for m in range(1, NCH):
            for hh in range(2):
                it = 2 * (m - 1) + hh
                if it + 2 < 2 * (NCH - 1):
                    it2 = it + 2
                    m2, hh2 = 1 + it2 // 2, it2 % 2
                    p.dma("g", Fb[it2 % 3][:], FbH[hh2, b, m2-1:m2, :].to_broadcast([128, 4096]),
                          ["FbH"], [f"Fb{it2 % 3}"])
                p.op("v", lambda e, hh=hh, it=it: e.tensor_mul(Psi[:], Rp[:, hh*4096:(hh+1)*4096], Fb[it % 3][:]),
                     ["Rp", f"Fb{it % 3}"], ["Psi"])
                p.op("v", lambda e, hh=hh, m=m: e.tensor_mul(
                    P2.rearrange("p (d e) -> p d e", d=64), Psi.rearrange("p (d e) -> p d e", d=64),
                    qTs[:, (hh*32+m)*64:(hh*32+m+1)*64].unsqueeze(1).to_broadcast([128, 64, 64])),
                     ["Psi", "qTs"], ["P2"])
                for k in (32, 16, 8, 4, 2, 1):
                    p.op("v", lambda e, k=k: e.tensor_add(
                        P2.rearrange("p (d e) -> p d e", d=64)[:, :, 0:k],
                        P2.rearrange("p (d e) -> p d e", d=64)[:, :, 0:k],
                        P2.rearrange("p (d e) -> p d e", d=64)[:, :, k:2*k]),
                         ["P2"], ["P2"])
                p.op("v", lambda e, hh=hh: e.tensor_copy(yfT[:, hh*64:(hh+1)*64],
                                                         P2.rearrange("p (d e) -> p d e", d=64)[:, :, 0]),
                     ["P2"], ["yfT"])
            p.op("t", lambda e: e.transpose(ptb[:], yfT[:], sid[:]), ["yfT", "sid"], ["ptb"])
            p.op("v", lambda e: e.tensor_copy(yfb[:], ptb[:]), ["ptb"], ["yfb"])
            p.op("v", lambda e, m=m: e.tensor_add(y2acc[:, m*128:(m+1)*128],
                                                  y2acc[:, m*128:(m+1)*128], yfb[:]),
                 ["yfb", "y2acc"], ["y2acc"])
        for l in range(128):
            W = 128 - l
            qv = q2.rearrange("p (m t) -> p m t", m=32)[:, :, l:128]
            kv = kn2.rearrange("p (m t) -> p m t", m=32)[:, :, 0:W]
            p.op("v", lambda e, qv=qv, kv=kv, W=W: e.tensor_mul(
                g2sc[:, 0:32*W].rearrange("p (m t) -> p m t", t=W), qv, kv), ["q2", "kn2"], ["g2sc"])
            segs = max(1, 512 // W)
            m0 = 0
            while m0 < 32:
                ns = min(segs, 32 - m0)
                width = ns * W
                p.op("t", lambda e, l=l, m0=m0, width=width, W=W:
                     e.matmul(psA[:, 0:width], Pblk[:, l*128:(l+1)*128],
                              g2sc[:, m0*W:m0*W+width], start=True, stop=True),
                     ["Pblk", "g2sc"], ["psM", "psW0", "psW1", "psW2", "psW3"])
                p.op("v", lambda e, m0=m0, ns=ns, W=W, width=width:
                     e.tensor_mul(tmpb[:, 0:width].rearrange("p (m t) -> p m t", t=W),
                                  psA[:, 0:width].rearrange("p (m t) -> p m t", t=W),
                                  v2.rearrange("p (m t) -> p m t", m=32)[:, m0:m0+ns, 0:W]),
                     ["psM", "v2"], ["tmpb"])
                p.op("v", lambda e, m0=m0, ns=ns, W=W, l=l:
                     e.tensor_add(y2acc.rearrange("p (m t) -> p m t", m=32)[:, m0:m0+ns, l:128],
                                  y2acc.rearrange("p (m t) -> p m t", m=32)[:, m0:m0+ns, l:128],
                                  tmpb[:, 0:ns*W].rearrange("p (m t) -> p m t", t=W)),
                     ["tmpb", "y2acc"], ["y2acc"])
                m0 += ns
        p.dma("q", ys2[:], ysf[:, t0:t0 + LSEQ], ["ysf"], ["ys2"])
        p.op("v", lambda e: e.tensor_scalar(y2acc[:], y2acc[:], omg[:], None, ALU.mult),
             ["y2acc", "omg"], ["y2acc"])
        p.op("v", lambda e: e.scalar_tensor_tensor(y2acc[:], ys2[:], gg[:], y2acc[:], ALU.mult, ALU.add),
             ["ys2", "gg", "y2acc"], ["y2acc"])
        p.dma("g", y0[:, t0:t0 + LSEQ], y2acc[:], ["y2acc"], ["y0fm"])


def emit_LC(p):
    y0 = p.din("y0_fmb", [D, NTOK], BF16)
    wo = p.din("W_o", [D, D])
    idenf = p.din("idenf", [128, 128])
    s0 = p.dout("s0_tm", [TSH, D])
    sidf = p.sb("sidf", [128, 128])
    p.dma("q", sidf[:], idenf, ["idenf"], ["sidf"])
    wob = p.sb("wob", [128, 8 * D], BF16)
    p.dma("g", wob.rearrange("p (t c) -> p t c", t=8), wo.rearrange("(t p) c -> p t c", p=128),
          ["W_o"], ["wob"])
    y0s = p.sb("y0s", [128, 8 * TSH], BF16)
    pm = p.ps("pm", [128, 1024])
    ptf = p.ps("ptf", [128, 128])
    ev = p.sb("ev", [128, 512])
    s0c = p.sb("s0c", [128, 4 * D])
    y0own = p.din("y0own", [D, TSH], BF16)
    p.dma("q", y0s.rearrange("p (t n) -> p t n", t=8), y0own.rearrange("(t p) n -> p t n", p=128),
          ["y0own"], ["y0s"])
    for tc in range(4):
        for oc in range(8):
            for kt in range(8):
                p.op("t", lambda e, oc=oc, kt=kt, tc=tc:
                     e.matmul(pm[:, (oc % 2) * 512:(oc % 2) * 512 + 512],
                              wob[:, kt * D + oc * 128: kt * D + oc * 128 + 128],
                              y0s[:, kt * TSH + tc * 512: kt * TSH + tc * 512 + 512],
                              start=(kt == 0), stop=(kt == 7)),
                     ["wob", "y0s"], [f"pm{oc%2}"], "p" if kt else "c")
            p.op("s", lambda e, oc=oc: e.copy(ev[:], pm[:, (oc % 2) * 512:(oc % 2) * 512 + 512]),
                 [f"pm{oc%2}"], ["ev"])
            for cc in range(4):
                p.op("t", lambda e, cc=cc: e.transpose(ptf[:], ev[:, cc * 128:(cc + 1) * 128], sidf[:]),
                     ["ev", "sidf"], ["ptf"])
                p.op("v", lambda e, cc=cc, oc=oc: e.tensor_copy(s0c[:, cc * D + oc * 128: cc * D + (oc + 1) * 128], ptf[:]),
                     ["ptf"], ["s0c"])
        p.dma("q", s0.rearrange("(a p) d -> p a d", p=128)[:, tc * 4:(tc + 1) * 4, :],
              s0c.rearrange("p (a d) -> p a d", a=4), ["s0c"], ["s0t"])


def emit_LD(p):
    xno = p.din("xnown", [D, TSH], BF16)
    wr = p.din("Wr", [D, EXP])
    br = p.din("br", [EXP])
    e0o = p.dout("e0", [TSH]); e1o = p.dout("e1", [TSH])
    w0o = p.dout("w0", [TSH]); w1o = p.dout("w1", [TSH])
    xns = p.sb("xns", [128, 8 * TSH], BF16)
    p.dma("q", xns.rearrange("p (t n) -> p t n", t=8), xno.rearrange("(t p) n -> p t n", p=128),
          ["xnown"], ["xns"])
    wrb = p.sb("wrb", [128, 8 * EXP], BF16)
    p.dma("g", wrb.rearrange("p (t c) -> p t c", t=8), wr.rearrange("(t p) c -> p t c", p=128),
          ["Wr"], ["wrb"])
    brb = p.sb("brb", [128, EXP])
    p.dma("q", brb[:], br.unsqueeze(0).to_broadcast([128, EXP]), ["br"], ["brb"])
    iot = p.sb("iot", [128, EXP])
    p.op("g", lambda e: e.iota(iot[:], pattern=[[1, EXP]], base=0, channel_multiplier=0,
                               allow_small_or_imprecise_dtypes=True), [], ["iot"])
    big = p.sb("big", [128, EXP])
    p.op("v", lambda e: e.memset(big[:], 999.0), [], ["big"])
    pm = p.ps("pm", [128, EXP])
    lg = p.sb("lg", [128, EXP])
    m8 = p.sb("m8", [128, 8])
    l0 = p.sb("l0", [128, 1]); l1 = p.sb("l1", [128, 1])
    w0c = p.sb("w0c", [128, 1]); w1c = p.sb("w1c", [128, 1])
    oh = p.sb("oh", [128, EXP], I32)
    cand = p.sb("cand", [128, EXP])
    e0c = p.sb("e0c", [128, 1]); e1c = p.sb("e1c", [128, 1])
    neq = p.sb("neq", [128, 1])
    c3 = p.sb("c3", [128, EXP])
    for tt in range(TSH // 128):
        for kt in range(8):
            p.op("t", lambda e, tt=tt, kt=kt:
                 e.matmul(pm[:], xns[:, kt * TSH + tt * 128: kt * TSH + (tt + 1) * 128],
                          wrb[:, kt * EXP:(kt + 1) * EXP], start=(kt == 0), stop=(kt == 7)),
                 ["xns", "wrb"], ["pm"], "p" if kt else "c")
        p.op("v", lambda e: e.tensor_add(lg[:], pm[:], brb[:]), ["pm", "brb"], ["lg"])
        p.op("v", lambda e: e.max(out=m8[:], in_=lg[:]), ["lg"], ["m8"])
        p.op("v", lambda e: e.tensor_sub(w0c[:], m8[:, 0:1], m8[:, 1:2]), ["m8"], ["w0c"])
        p.op("s", lambda e: e.activation(w0c[:], w0c[:], AF.Sigmoid), ["w0c"], ["w0c"])
        p.op("v", lambda e: e.tensor_scalar(w1c[:], w0c[:], -1.0, 1.0, ALU.mult, ALU.add), ["w0c"], ["w1c"])
        p.op("v", lambda e: e.tensor_copy(l0[:], m8[:, 0:1]), ["m8"], ["l0"])
        p.op("v", lambda e: e.tensor_copy(l1[:], m8[:, 1:2]), ["m8"], ["l1"])
        p.op("v", lambda e: e.tensor_scalar(oh[:], lg[:], l0[:], None, ALU.is_equal), ["lg", "l0"], ["oh"])
        p.op("v", lambda e: e.select(cand[:], oh[:], iot[:], big[:]), ["oh", "iot", "big"], ["cand"])
        p.op("v", lambda e: e.tensor_reduce(e0c[:], cand[:], AX.X, ALU.min), ["cand"], ["e0c"])
        p.op("v", lambda e: e.tensor_tensor(neq[:], l1[:], l0[:], ALU.not_equal), ["l0", "l1"], ["neq"])
        p.op("v", lambda e: e.tensor_scalar(c3[:], iot[:], e0c[:], None, ALU.is_gt), ["iot", "e0c"], ["c3"])
        p.op("v", lambda e: e.tensor_scalar(c3[:], c3[:], neq[:], None, ALU.add), ["c3", "neq"], ["c3"])
        p.op("v", lambda e: e.tensor_scalar_min(c3[:], c3[:], 1.0), ["c3"], ["c3"])
        p.op("v", lambda e: e.tensor_scalar(cand[:], lg[:], l1[:], None, ALU.is_equal), ["lg", "l1"], ["cand"])
        p.op("v", lambda e: e.tensor_mul(c3[:], c3[:], cand[:]), ["c3", "cand"], ["c3"])
        p.op("v", lambda e: e.tensor_single_scalar(oh[:], c3[:], 0.5, ALU.is_gt), ["c3"], ["oh"])
        p.op("v", lambda e: e.select(cand[:], oh[:], iot[:], big[:]), ["oh", "iot", "big"], ["cand"])
        p.op("v", lambda e: e.tensor_reduce(e1c[:], cand[:], AX.X, ALU.min), ["cand"], ["e1c"])
        p.dma("q", e0o[tt * 128:(tt + 1) * 128].unsqueeze(1), e0c[:], ["e0c"], ["e0o"])
        p.dma("q", e1o[tt * 128:(tt + 1) * 128].unsqueeze(1), e1c[:], ["e1c"], ["e1o"])
        p.dma("q", w0o[tt * 128:(tt + 1) * 128].unsqueeze(1), w0c[:], ["w0c"], ["w0o"])
        p.dma("q", w1o[tt * 128:(tt + 1) * 128].unsqueeze(1), w1c[:], ["w1c"], ["w1o"])


def emit_LE(p):
    e0 = p.din("e0", [NTOK]); e1 = p.din("e1", [NTOK])
    w0 = p.din("w0", [NTOK]); w1 = p.din("w1", [NTOK])
    xnb = p.din("xn_tmb", [NTOK, D], BF16)
    utri = p.din("utri", [128, 128], BF16)
    NP = NTOK // 128
    er = p.sb("er", [128, 2 * NP])
    wr_ = p.sb("wr_", [128, 2 * NP])
    p.dma("q", er[:, 0:NP], e0.rearrange("(p f) -> p f", p=128), ["e0"], ["er"])
    p.dma("q", er[:, NP:2*NP], e1.rearrange("(p f) -> p f", p=128), ["e1"], ["er"])
    p.dma("q", wr_[:, 0:NP], w0.rearrange("(p f) -> p f", p=128), ["w0"], ["wr_"])
    p.dma("q", wr_[:, NP:2*NP], w1.rearrange("(p f) -> p f", p=128), ["w1"], ["wr_"])
    vals = p.sb("vals", [128, EXP * 256])
    neg1 = p.sb("neg1", [128, 256])
    mski = p.sb("mski", [128, 256], I32)
    p.op("v", lambda e: e.memset(neg1[:], -1.0), [], ["neg1"])
    for ex in range(EXP):
        for r in range(2):
            p.op("v", lambda e, ex=ex, r=r: e.tensor_single_scalar(mski.bitcast(F32)[:, r*NP:(r+1)*NP] if False else mski[:, r*NP:(r+1)*NP], er[:, r*NP:(r+1)*NP], float(ex), ALU.is_equal),
                 ["er"], ["mski"])
            p.op("v", lambda e, ex=ex, r=r: e.select(vals[:, ex*256+r*NP:ex*256+(r+1)*NP],
                                                     mski[:, r*NP:(r+1)*NP], wr_[:, r*NP:(r+1)*NP],
                                                     neg1[:, 0:NP]),
                 ["mski", "wr_", "neg1"], ["vals"])
    myslot = p.sb("myslot", [128, EXP * 256])
    p.op("g", lambda e: e.iota(myslot[:], pattern=[[0, EXP], [128, 2], [1, NP]], base=0,
                               channel_multiplier=256, allow_small_or_imprecise_dtypes=True),
         [], ["myslot"])
    ones1f = p.sb("ones1f", [1, 128])
    ones128f = p.sb("ones128f", [128, 1])
    p.op("v", lambda e: e.memset(ones1f[:], 1.0), [], ["ones1f"])
    p.op("v", lambda e: e.memset(ones128f[:], 1.0), [], ["ones128f"])
    lo = p.sb("lo", [1, EXP]); hi = p.sb("hi", [1, EXP]); mid = p.sb("mid", [1, EXP])
    cnt = p.sb("cnt", [1, EXP]); cmpc = p.sb("cmpc", [1, EXP], I32)
    p.op("v", lambda e: e.memset(lo[:], -1.0), [], ["lo"])
    p.op("v", lambda e: e.memset(hi[:], 1.000001), [], ["hi"])
    psB = p.ps("psB", [128, 8])
    psC = p.ps("psC", [1, 8])
    gemask = p.sb("gemask", [128, 2048])
    seg = p.sb("seg", [128, 8])
    thr8 = p.sb("thr8", [128, 8])
    thrb = thr8[:].unsqueeze(2).to_broadcast([128, EXP, 256])

    def count_pass(thr_row, dst_cnt, pred_fn, rkeys):
        p.op("t", lambda e, thr_row=thr_row: e.matmul(psB[:], ones1f[:],
             thr_row, start=True, stop=True), list(rkeys) + ["ones1f"], ["psB"])
        p.op("v", lambda e: e.tensor_copy(thr8[:], psB[:]), ["psB"], ["thrb"])
        pred_fn()
        p.op("v", lambda e: e.tensor_reduce(seg[:].unsqueeze(2), gemask.rearrange("p (x c) -> p x c", x=8), AX.X, ALU.add),
             ["gemask"], ["seg"])
        p.op("t", lambda e: e.matmul(psC[:], ones128f[:], seg[:], start=True, stop=True),
             ["ones128f", "seg"], ["psC"])
        p.op("v", lambda e, dst_cnt=dst_cnt: e.tensor_copy(dst_cnt, psC[:]), ["psC"], ["cnt"])

    for it in range(38):
        p.op("v", lambda e: e.tensor_add(mid[:], lo[:], hi[:]), ["lo", "hi"], ["mid"])
        p.op("v", lambda e: e.tensor_scalar_mul(mid[:], mid[:], 0.5), ["mid"], ["mid"])
        count_pass(mid[:], cnt[:],
                   lambda: p.op("v", lambda e: e.tensor_tensor(
                       gemask[:].rearrange("p (x c) -> p x c", x=8),
                       vals[:].rearrange("p (x c) -> p x c", x=8), thrb, ALU.is_gt),
                                ["vals", "thrb"], ["gemask"]), ["mid"])
        p.op("v", lambda e: e.tensor_single_scalar(cmpc[:], cnt[:], CAP - 0.5, ALU.is_ge), ["cnt"], ["cmpc"])
        p.op("v", lambda e: e.select(lo[:], cmpc[:], mid[:], lo[:]), ["cmpc", "mid", "lo"], ["lo"])
        p.op("v", lambda e: e.select(hi[:], cmpc[:], hi[:], mid[:]), ["cmpc", "mid", "hi"], ["hi"])
    gtm = p.sb("gtm", [128, 2048])
    eqm = p.sb("eqm", [128, 2048])
    count_pass(hi[:], cnt[:],
               lambda: p.op("v", lambda e: e.tensor_tensor(
                   gemask[:].rearrange("p (x c) -> p x c", x=8),
                   vals[:].rearrange("p (x c) -> p x c", x=8), thrb, ALU.is_gt),
                            ["vals", "thrb"], ["gemask"]), ["hi"])
    p.op("v", lambda e: e.tensor_copy(gtm[:], gemask[:]), ["gemask"], ["gtm"])
    mrow = p.sb("mrow", [1, EXP])
    p.op("v", lambda e: e.tensor_scalar(mrow[:], cnt[:], -1.0, float(CAP), ALU.mult, ALU.add), ["cnt"], ["mrow"])
    count_pass(lo[:], cnt[:],
               lambda: p.op("v", lambda e: e.tensor_tensor(
                   gemask[:].rearrange("p (x c) -> p x c", x=8),
                   vals[:].rearrange("p (x c) -> p x c", x=8), thrb, ALU.is_gt),
                            ["vals", "thrb"], ["gemask"]), ["lo"])
    p.op("v", lambda e: e.tensor_sub(eqm[:], gemask[:], gtm[:]), ["gemask", "gtm"], ["eqm"])
    lo2 = p.sb("lo2", [1, EXP])
    p.op("v", lambda e: e.memset(lo2[:], 0.0), [], ["lo2"])
    for it in range(17):
        step = float(2 ** (16 - it))
        p.op("v", lambda e, step=step: e.tensor_scalar(mid[:], lo2[:], 1.0, step, ALU.mult, ALU.add), ["lo2"], ["mid"])
        def predi():
            p.op("v", lambda e: e.tensor_tensor(
                gemask[:].rearrange("p (x c) -> p x c", x=8),
                myslot[:].rearrange("p (x c) -> p x c", x=8), thrb, ALU.is_lt),
                 ["myslot", "thrb"], ["gemask"])
            p.op("v", lambda e: e.tensor_mul(gemask[:], gemask[:], eqm[:]), ["gemask", "eqm"], ["gemask"])
        count_pass(mid[:], cnt[:], predi, ["mid"])
        p.op("v", lambda e: e.tensor_tensor(cmpc[:], cnt[:], mrow[:], ALU.is_le), ["cnt", "mrow"], ["cmpc"])
        p.op("v", lambda e: e.select(lo2[:], cmpc[:], mid[:], lo2[:]), ["cmpc", "mid", "lo2"], ["lo2"])
    count_pass(lo2[:], cnt[:],
               lambda: (p.op("v", lambda e: e.tensor_tensor(
                        gemask[:].rearrange("p (x c) -> p x c", x=8),
                        myslot[:].rearrange("p (x c) -> p x c", x=8), thrb, ALU.is_lt),
                             ["myslot", "thrb"], ["gemask"]),
                        p.op("v", lambda e: e.tensor_mul(gemask[:], gemask[:], eqm[:]), ["gemask", "eqm"], ["gemask"]))[0] and None,
               ["lo2"])
    sel = p.sb("sel", [128, 2048])
    p.op("v", lambda e: e.tensor_add(sel[:], gtm[:], gemask[:]), ["gtm", "gemask"], ["sel"])
    onesB = p.sb("onesB", [128, 256])
    p.op("v", lambda e: e.memset(onesB[:], 1.0), [], ["onesB"])
    pfx = p.sb("pfx", [128, 2048])
    for ex in range(EXP):
        p.op("v", lambda e, ex=ex: e.tensor_tensor_scan(pfx[:, ex*256:(ex+1)*256], onesB[:],
                                                        sel[:, ex*256:(ex+1)*256], 0.0, ALU.mult, ALU.add),
             ["onesB", "sel"], ["pfx"])
    rowr = p.sb("rowr", [128, 2048])
    p.op("v", lambda e: e.tensor_sub(rowr[:], pfx[:], sel[:]), ["pfx", "sel"], ["rowr"])
    rsum = p.sb("rsum", [128, EXP])
    p.op("v", lambda e: e.tensor_copy(rsum[:], pfx.rearrange("p (x c) -> p x c", x=8)[:, :, 255]), ["pfx"], ["rsum"])
    rsb = p.sb("rsb", [128, EXP], BF16)
    p.op("v", lambda e: e.tensor_copy(rsb[:], rsum[:]), ["rsum"], ["rsb"])
    sutri = p.sb("sutri", [128, 128], BF16)
    p.dma("q", sutri[:], utri, ["utri"], ["sutri"])
    psD = p.ps("psD", [128, EXP])
    p.op("t", lambda e: e.matmul(psD[:], sutri[:], rsb[:], start=True, stop=True),
         ["sutri", "rsb"], ["psD"])
    roff = p.sb("roff", [128, EXP])
    p.op("v", lambda e: e.tensor_copy(roff[:], psD[:]), ["psD"], ["roff"])
    ebase = p.sb("ebase", [128, 2048])
    p.op("g", lambda e: e.iota(ebase[:], pattern=[[CAP, EXP], [0, 2], [0, NP]], base=0,
                               channel_multiplier=0, allow_small_or_imprecise_dtypes=True), [], ["ebase"])
    dest = p.sb("dest", [128, 2048])
    p.op("v", lambda e: e.tensor_add(dest[:], rowr[:], ebase[:]), ["rowr", "ebase"], ["dest"])
    for ex in range(EXP):
        p.op("v", lambda e, ex=ex: e.tensor_scalar(dest[:, ex*256:(ex+1)*256], dest[:, ex*256:(ex+1)*256],
                                                   roff[:, ex:ex+1], None, ALU.add),
             ["dest", "roff"], ["dest"])
    bigs = p.sb("bigs", [128, 2048])
    p.op("v", lambda e: e.tensor_scalar(bigs[:], sel[:], -1.0e6, 1.0e6, ALU.mult, ALU.add), ["sel"], ["bigs"])
    p.op("v", lambda e: e.tensor_mul(dest[:], dest[:], sel[:]), ["dest", "sel"], ["dest"])
    p.op("v", lambda e: e.tensor_add(dest[:], dest[:], bigs[:]), ["dest", "bigs"], ["dest"])
    mdst = p.sb("mdst", [128, 256])
    p.op("v", lambda e: e.tensor_reduce(mdst[:].unsqueeze(2), dest.rearrange("p (x c) -> p c x", x=8), AX.X, ALU.min),
         ["dest"], ["mdst"])
    mdsti = p.sb("mdsti", [128, 256], I32)
    p.op("v", lambda e: e.tensor_copy(mdsti[:], mdst[:]), ["mdst"], ["mdsti"])
    mdstiH = p.dout("mdstiH", [128, 256], I32)
    cmprowH = p.dout("cmprowH", [32768])
    p.dma("q", mdstiH, mdsti[:], ["mdsti"], ["mdstiH"])
    p.dma("q", cmprowH.rearrange("(p c) -> p c", p=128), mdst[:], ["mdst"], ["cmprowH"])
    slotid = p.sb("slotid", [128, 256])
    p.op("g", lambda e: e.iota(slotid[:], pattern=[[16384, 2], [1, NP]], base=0,
                               channel_multiplier=128, allow_small_or_imprecise_dtypes=True), [], ["slotid"])
    meta = p.sb("meta", [128, 512])
    p.op("v", lambda e: e.tensor_copy(meta.rearrange("p (c two) -> p c two", two=2)[:, :, 0], slotid[:]),
         ["slotid"], ["meta"])
    p.op("v", lambda e: e.tensor_copy(meta.rearrange("p (c two) -> p c two", two=2)[:, :, 1], wr_[:]),
         ["wr_"], ["meta"])
    metaIH = p.dout("metaIH", [128, 512])
    p.dma("q", metaIH, meta[:], ["meta"], ["metaIH"])
    xemH = p.dout("xemH", [EXP * CAP, 1040], BF16)
    zb = p.sb("zb", [128, 1040], BF16)
    p.op("v", lambda e: e.memset(zb[:], 0.0), [], ["zb"])
    p.dma("q", xemH.rearrange("(a p) d -> p a d", p=128), zb[:].unsqueeze(1).to_broadcast([128, EXP*CAP//128, 1040]),
          ["zb"], ["xemH"])


NSF = 13
FOH = NCORE * NSF
NOH = 128 - FOH


def emit_YTZ(p):
    ytok = p.dram("ytokP", [NTOK, D], BF16)
    z = p.sb("z", [128, D], BF16)
    p.op("v", lambda e: e.memset(z[:], 0.0), [], ["ytz"])
    p.dma("q", ytok.rearrange("(a p) d -> p a d", p=128),
          z[:].unsqueeze(1).to_broadcast([128, NTOK // 128, D]), ["ytz"], ["ytokP"])


def emit_DISPATCH(p):
    xnF = p.internals["xn_tmb_full"]
    mdD = p.internals["mdstiH"]
    mtD = p.internals["metaIH"]
    xem = p.internals["xemH"]
    xview = xnF.rearrange("(pp f) d -> pp f d", f=128)
    xsel = p.sb("xsel", [128, NSF * 1024], BF16)
    mdsel = p.sb("mdsel", [128, 2 * NSF], I32)
    mtsel = p.sb("mtsel", [128, 4 * NSF])
    p.swdma(lambda e, c: e.dma_start(out=xsel[:].rearrange("p (i d) -> p i d", d=1024),
                                     in_=xview[:, NSF*c:NSF*(c+1), :]),
            ["xn_tmb_full"], ["xsel"])
    p.swdma(lambda e, c: e.dma_start(out=mdsel[:, 0:NSF], in_=mdD[:, NSF*c:NSF*(c+1)]),
            ["mdstiH"], ["mdsel"])
    p.swdma(lambda e, c: e.dma_start(out=mdsel[:, NSF:2*NSF], in_=mdD[:, 128+NSF*c:128+NSF*(c+1)]),
            ["mdstiH"], ["mdsel"])
    p.swdma(lambda e, c: e.dma_start(out=mtsel[:, 0:2*NSF], in_=mtD[:, 2*NSF*c:2*NSF*(c+1)]),
            ["metaIH"], ["mtsel"])
    p.swdma(lambda e, c: e.dma_start(out=mtsel[:, 2*NSF:4*NSF], in_=mtD[:, 256+2*NSF*c:256+2*NSF*(c+1)]),
            ["metaIH"], ["mtsel"])
    pay = p.sb("pay", [128, 2 * NSF * 1040], BF16)
    payv = pay[:].rearrange("p (j d) -> p j d", d=1040)
    p.op("v", lambda e: e.memset(pay[:], 0.0), [], ["pay"])
    th = p.sb("th", [128, 1])
    tl = p.sb("tl", [128, 1])
    for j in range(2 * NSF):
        r, i = divmod(j, NSF)
        p.op("v", lambda e, j=j, i=i: e.tensor_copy(
            payv[:, j, 0:1024], xsel[:].rearrange("p (i d) -> p i d", d=1024)[:, i, :]),
            ["xsel", "pay"], ["pay"])
        p.op("v", lambda e, j=j: e.tensor_scalar(th[:], mtsel[:, 2*j:2*j+1],
             1.0 / 128.0, -0.4999, ALU.mult, ALU.add), ["mtsel"], ["th"])
        p.op("v", lambda e: e.tensor_scalar(th[:], th[:], 1.0, RBIG, ALU.mult, ALU.add),
             ["th"], ["th"])
        p.op("v", lambda e: e.tensor_scalar(th[:], th[:], RBIG, None, ALU.subtract),
             ["th"], ["th"])
        p.op("v", lambda e, j=j: e.tensor_scalar(tl[:], th[:], -128.0, None, ALU.mult),
             ["th"], ["tl"])
        p.op("v", lambda e, j=j: e.tensor_tensor(tl[:], tl[:], mtsel[:, 2*j:2*j+1], ALU.add),
             ["tl", "mtsel"], ["tl"])
        p.op("v", lambda e, j=j: e.tensor_copy(payv[:, j, 1024:1025], th[:]),
             ["th", "pay"], ["pay"])
        p.op("v", lambda e, j=j: e.tensor_copy(payv[:, j, 1025:1026], tl[:]),
             ["tl", "pay"], ["pay"])
        p.op("v", lambda e, j=j: e.tensor_copy(
            payv[:, j, 1026:1028], mtsel[:, 2*j+1:2*j+2].bitcast(BF16)),
            ["mtsel", "pay"], ["pay"])
    for j in range(2 * NSF):
        p.op("g", lambda e, j=j: e.indirect_dma_start(
            out=xem, out_offset=IOoA(mdsel[:, j:j+1], 0),
            in_=payv[:, j, :], in_offset=None,
            bounds_check=EXP*CAP-1, oob_is_err=False),
            ["mdsel", "pay", "xemH"], ["xemH"], "d")


def emit_ONEHOT(p):
    cmF = p.internals["cmprowH"].rearrange("(p c) -> p c", p=128)
    mtD = p.internals["metaIH"]
    xnF = p.internals["xn_tmb_full"]
    xview = xnF.rearrange("(pp f) d -> pp f d", f=128)
    ohx = p.dram("ohxD", [CAP, 1024], BF16)
    ohm = p.dram("ohmD", [CAP, 4])
    md48 = p.sb("md48", [128, 2 * NOH])
    p.dma("q", md48[:, 0:NOH], cmF[:, FOH:128], ["cmprowH"], ["md48"])
    p.dma("q", md48[:, NOH:2*NOH], cmF[:, 128+FOH:256], ["cmprowH"], ["md48"])
    mt48 = p.sb("mt48", [128, 4 * NOH])
    p.dma("q", mt48[:, 0:2*NOH], mtD[:, 2*FOH:256], ["metaIH"], ["mt48"])
    p.dma("q", mt48[:, 2*NOH:4*NOH], mtD[:, 256+2*FOH:512], ["metaIH"], ["mt48"])
    NJ4 = 2 * NOH
    slv = mt48[:].rearrange("p (j two) -> p two j", two=2)
    th48 = p.sb("th48", [128, NJ4])
    tl48 = p.sb("tl48", [128, NJ4])
    rhs4 = p.sb("rhs4", [128, 4 * NJ4], BF16)
    r4v = rhs4[:].rearrange("p (j f) -> p f j", f=4)
    p.op("v", lambda e: e.tensor_scalar(th48[:], slv[:, 0, :], 1.0 / 128.0, -0.4999,
         ALU.mult, ALU.add), ["mt48"], ["th48"])
    p.op("v", lambda e: e.tensor_scalar(th48[:], th48[:], 1.0, RBIG, ALU.mult, ALU.add),
         ["th48"], ["th48"])
    p.op("v", lambda e: e.tensor_scalar(th48[:], th48[:], RBIG, None, ALU.subtract),
         ["th48"], ["th48"])
    p.op("v", lambda e: e.tensor_scalar(tl48[:], th48[:], -128.0, None, ALU.mult),
         ["th48"], ["tl48"])
    p.op("v", lambda e: e.tensor_tensor(tl48[:], tl48[:], slv[:, 0, :], ALU.add),
         ["tl48", "mt48"], ["tl48"])
    p.op("v", lambda e: e.memset(rhs4[:], 0.0), [], ["rhs4"])
    p.op("v", lambda e: e.tensor_copy(r4v[:, 0, :], th48[:]), ["th48", "rhs4"], ["rhs4"])
    p.op("v", lambda e: e.tensor_copy(r4v[:, 1, :], tl48[:]), ["tl48", "rhs4"], ["rhs4"])
    p.op("v", lambda e: e.tensor_copy(r4v[:, 2, :], slv[:, 1, :]), ["mt48", "rhs4"], ["rhs4"])
    xp = p.sb("xp", [128, NOH * 1024], BF16)
    p.dma("q", xp[:].rearrange("p (i d) -> p i d", d=1024), xview[:, FOH:128, :],
          ["xn_tmb_full"], ["xp"])
    cb = p.din("cbase", [1, 1])
    cbb = p.sb("cbb", [128, 1])
    p.dma("q", cbb[:], cb.to_broadcast([128, 1]), ["cbase"], ["cbb"])
    e48 = p.sb("e48", [128, 2 * NOH])
    p.op("v", lambda e: e.tensor_scalar(e48[:], md48[:], cbb[:], None, ALU.subtract),
         ["md48", "cbb"], ["e48"])
    iof = p.sb("iof", [128, 128])
    p.op("g", lambda e: e.iota(iof[:], pattern=[[1, 128]], base=0, channel_multiplier=0,
                               allow_small_or_imprecise_dtypes=True), [], ["iof"])
    ohb = [p.sb(f"ohb{i}", [128, 128], BF16) for i in range(2)]
    psx = p.ps("psx", [128, 1024])
    psm = p.ps("psm", [128, 4])
    oxt = p.sb("oxt", [128, 1024], BF16)
    omt = p.sb("omt", [128, 4])
    NJ = 2 * NOH
    for rt in range(CAP // 128):
        for j in range(NJ):
            i = j % 2
            p.op("v", lambda e, j=j, rt=rt, i=i: e.tensor_scalar(
                ohb[i][:], iof[:], e48[:, j:j+1], float(-(rt * 128)),
                ALU.subtract, ALU.is_equal), ["iof", "e48"], [f"ohb{i}"])
            for half in range(2):
                p.op("t", lambda e, j=j, i=i, half=half: e.matmul(
                    psx[:, half*512:(half+1)*512], ohb[i][:],
                    xp[:, (j % NOH)*1024 + half*512:(j % NOH)*1024 + half*512 + 512],
                    start=(j == 0), stop=(j == NJ - 1)),
                    [f"ohb{i}", "xp"], ["psx"], "p" if j else "c")
            p.op("t", lambda e, j=j, i=i: e.matmul(
                psm[:], ohb[i][:], rhs4[:, 4*j:4*j+4],
                start=(j == 0), stop=(j == NJ - 1)),
                [f"ohb{i}", "rhs4"], ["psm"], "p" if j else "c")
        p.op("v", lambda e: e.tensor_copy(oxt[:], psx[:]), ["psx"], ["oxt"])
        p.op("v", lambda e: e.tensor_copy(omt[:], psm[:]), ["psm"], ["omt"])
        p.dma("q", ohx[rt*128:(rt+1)*128, :], oxt[:], ["oxt"], ["ohxD"])
        p.dma("q", ohm[rt*128:(rt+1)*128, :], omt[:], ["omt"], ["ohmD"])


def emit_LF(p):
    xem = p.din("xem", [CAP, 1040], BF16)
    ohx = p.din("ohxD", [CAP, 1024], BF16)
    ohm = p.din("ohmD", [CAP, 2])
    ytokP = p.din("ytokP", [NTOK, D], BF16)
    w1 = p.din("w1e", [D, 4096])
    b1 = p.din("b1e", [4096])
    w2 = p.din("w2e", [4096, D])
    b2 = p.din("b2e", [D])
    iden = p.din("iden", [128, 128], BF16)
    idenf = p.din("idenf", [128, 128])
    sid = p.sb("sid", [128, 128], BF16)
    sidf2 = p.sb("sidf2", [128, 128])
    p.dma("q", sid[:], iden, ["iden"], ["sid"])
    p.dma("q", sidf2[:], idenf, ["idenf"], ["sidf2"])
    w1b = p.sb("w1b", [128, 8 * 4096], BF16)
    p.dma("g", w1b.rearrange("p (t c) -> p t c", t=8), w1.rearrange("(t p) c -> p t c", p=128),
          ["w1e"], ["w1b"])
    w2b = p.sb("w2b", [128, 32 * 128], BF16)
    xeT = p.sb("xeT", [128, 8 * CAP], BF16)
    xet = p.sb("xet", [128, D], BF16)
    oxt = p.sb("oxt", [128, D], BF16)
    ptb = p.ps("ptb", [128, 128], BF16)
    for rt in range(CAP // 128):
        p.dma("q", xet[:], xem[rt*128:(rt+1)*128, 0:1024], ["xem"], ["xet"])
        p.dma("q", oxt[:], ohx[rt*128:(rt+1)*128, :], ["ohxD"], ["oxt"])
        p.op("v", lambda e: e.tensor_add(xet[:], xet[:], oxt[:]), ["xet", "oxt"], ["xet"])
        for f in range(8):
            p.op("t", lambda e, f=f: e.transpose(ptb[:], xet[:, f*128:(f+1)*128], sid[:]),
                 ["xet", "sid"], ["ptb"])
            p.op("v", lambda e, f=f, rt=rt: e.tensor_copy(xeT[:, f*CAP+rt*128:f*CAP+(rt+1)*128], ptb[:]),
                 ["ptb"], ["xeT"])
    b1c = p.sb("b1c", [128, 32])
    p.dma("q", b1c[:], b1.rearrange("(m p) -> p m", p=128), ["b1e"], ["b1c"],
          allow_slow_non_contiguous=True)
    b2c = p.sb("b2c", [128, 8])
    p.dma("q", b2c[:], b2.rearrange("(m p) -> p m", p=128), ["b2e"], ["b2c"],
          allow_slow_non_contiguous=True)
    h1T = p.sb("h1T", [128, 32 * 512], BF16)
    pmm = p.ps("pmm", [128, 1024])
    yT = p.sb("yT", [128, 8 * 512])
    ytm = p.sb("ytm", [128, D])
    ytmb = p.sb("ytmb", [128, D], BF16)
    gcol = p.sb("gcol", [128, 2])
    gsl = p.sb("gsl", [128, 2], BF16)
    ghl = p.sb("ghl", [128, 2])
    ggc = p.sb("ggc", [128, 1])
    omtt = p.sb("omtt", [128, 4])
    msk1 = p.sb("msk1", [128, 1])
    dstf = p.sb("dstf", [128, 1])
    dsti = p.sb("dsti", [128, 1], I32)
    ptf = p.ps("ptf", [128, 128])
    for nch in range(CAP // 512):
        for mt in range(32):
            sl = mt % 2
            for kt in range(8):
                p.op("t", lambda e, mt=mt, kt=kt, nch=nch, sl=sl:
                     e.matmul(pmm[:, sl*512:(sl+1)*512], w1b[:, kt*4096+mt*128:kt*4096+(mt+1)*128],
                              xeT[:, kt*CAP+nch*512:kt*CAP+nch*512+512], start=(kt == 0), stop=(kt == 7)),
                     ["w1b", "xeT"], [f"pmm{sl}"], "p" if kt else "c")
            p.op("s", lambda e, mt=mt, sl=sl: e.activation(h1T[:, mt*512:(mt+1)*512], pmm[:, sl*512:(sl+1)*512],
                                                           AF.Gelu, bias=b1c[:, mt:mt+1]),
                 [f"pmm{sl}", "b1c"], ["h1T"])
        for oc in range(8):
            sl = oc % 2
            p.dma("g", w2b.rearrange("p (t c) -> p t c", t=32),
                  w2.rearrange("(t p) c -> p t c", p=128)[:, :, oc*128:(oc+1)*128],
                  ["w2e"], ["w2b"])
            for mt in range(32):
                p.op("t", lambda e, oc=oc, mt=mt, sl=sl:
                     e.matmul(pmm[:, sl*512:(sl+1)*512], w2b[:, mt*128:(mt+1)*128],
                              h1T[:, mt*512:(mt+1)*512], start=(mt == 0), stop=(mt == 31)),
                     ["w2b", "h1T"], [f"pmm{sl}"], "p" if mt else "c")
            p.op("v", lambda e, oc=oc, sl=sl: e.tensor_scalar(yT[:, oc*512:(oc+1)*512], pmm[:, sl*512:(sl+1)*512],
                                                              b2c[:, oc:oc+1], None, ALU.add),
                 [f"pmm{sl}", "b2c"], ["yT"])
        for cc in range(4):
            p.dma("q", gsl[:], xem[nch*512+cc*128:nch*512+(cc+1)*128, 1024:1026], ["xem"], ["gsl"])
            p.dma("q", ggc[:].bitcast(BF16), xem[nch*512+cc*128:nch*512+(cc+1)*128, 1026:1028], ["xem"], ["ggc"])
            p.dma("q", omtt[:], ohm[nch*512+cc*128:nch*512+(cc+1)*128, :], ["ohmD"], ["omtt"])
            p.op("v", lambda e: e.tensor_copy(ghl[:], gsl[:]), ["gsl"], ["ghl"])
            p.op("v", lambda e: e.tensor_tensor(ghl[:], ghl[:], omtt[:, 0:2], ALU.add),
                 ["ghl", "omtt"], ["ghl"])
            p.op("v", lambda e: e.tensor_scalar(gcol[:, 0:1], ghl[:, 0:1], 128.0, None, ALU.mult),
                 ["ghl"], ["gcol"])
            p.op("v", lambda e: e.tensor_tensor(gcol[:, 0:1], gcol[:, 0:1], ghl[:, 1:2], ALU.add),
                 ["gcol", "ghl"], ["gcol"])
            p.op("v", lambda e: e.tensor_tensor(gcol[:, 1:2], ggc[:], omtt[:, 2:3], ALU.add),
                 ["ggc", "omtt", "gcol"], ["gcol"])
            for oc in range(8):
                p.op("t", lambda e, cc=cc, oc=oc: e.transpose(ptf[:], yT[:, oc*512+cc*128:oc*512+(cc+1)*128], sidf2[:]),
                     ["yT", "sidf2"], ["ptf"])
                p.op("v", lambda e, oc=oc: e.tensor_copy(ytm[:, oc*128:(oc+1)*128], ptf[:]), ["ptf"], ["ytm"])
            p.op("v", lambda e: e.tensor_scalar(ytm[:], ytm[:], gcol[:, 1:2], None, ALU.mult),
                 ["ytm", "gcol"], ["ytm"])
            p.op("v", lambda e: e.tensor_copy(ytmb[:], ytm[:]), ["ytm"], ["ytmb"])
            p.op("v", lambda e: e.tensor_single_scalar(msk1[:], gcol[:, 0:1], 16383.5, ALU.is_ge),
                 ["gcol"], ["msk1"])
            p.op("v", lambda e: e.tensor_scalar(msk1[:], msk1[:], -16384.0, None, ALU.mult),
                 ["msk1"], ["msk1"])
            p.op("v", lambda e: e.tensor_add(dstf[:], gcol[:, 0:1], msk1[:]), ["gcol", "msk1"], ["dstf"])
            p.op("v", lambda e: e.tensor_single_scalar(msk1[:], gcol[:, 1:2], 0.0, ALU.is_le),
                 ["gcol"], ["msk1"])
            p.op("v", lambda e: e.tensor_scalar(msk1[:], msk1[:], 2.0e6, None, ALU.mult),
                 ["msk1"], ["msk1"])
            p.op("v", lambda e: e.tensor_add(dstf[:], dstf[:], msk1[:]), ["dstf", "msk1"], ["dstf"])
            p.op("v", lambda e: e.tensor_copy(dsti[:], dstf[:]), ["dstf"], ["dsti"])
            p.op("g", lambda e: e.indirect_dma_start(
                out=ytokP, out_offset=IOoA(dsti[:], 0), in_=ytmb[:], in_offset=None,
                bounds_check=NTOK-1, oob_is_err=False),
                ["dsti", "ytmb", "ytokP"], ["ytokP"], "d")


def emit_LG(p):
    yt = p.din("ytok_own", [TSH, D], BF16)
    xn = p.din("xnown", [TSH, D])
    x = p.din("xown", [TSH, D])
    s0 = p.din("s0own", [TSH, D])
    gam0 = p.din("gamma0", [D])
    bet0 = p.din("beta0", [D])
    iden = p.din("iden", [128, 128], BF16)
    x1o = p.dout("x1_tm", [TSH, D])
    xn1o = p.dout("xn1_tm", [TSH, D])
    xq1o = p.dout("xq1_fmb", [D, TSH], BF16)
    ma1o = p.dout("ma1", [TSH])
    xmaxo = p.dout("xmax1c", [1, 1])
    sid = p.sb("sid", [128, 128], BF16)
    p.dma("q", sid[:], iden, ["iden"], ["sid"])
    ytb = p.sb("ytb", [128, D], BF16)
    gb = p.sb("gb", [128, D])
    bb = p.sb("bb", [128, D])
    p.dma("q", gb[:], gam0.unsqueeze(0).to_broadcast([128, D]), ["gamma0"], ["gb"])
    p.dma("q", bb[:], bet0.unsqueeze(0).to_broadcast([128, D]), ["beta0"], ["bb"])
    ot = p.sb("ot", [128, D])
    xt = p.sb("xt", [128, D])
    st = p.sb("st", [128, D])
    xnt = p.sb("xnt", [128, D])
    sq = p.sb("sq", [128, D])
    mc = p.sb("mc", [128, 1]); vc = p.sb("vc", [128, 1])
    mx = p.sb("mx", [128, 1])
    gmax = p.sb("gmax", [128, 1])
    xqb = p.sb("xqb", [128, D], BF16)
    ptb = p.ps("ptb", [128, 128], BF16)
    trb = p.sb("trb", [128, D], BF16)
    pmx = p.ps("pmx", [128, 128])
    mxf = p.sb("mxf", [1, 128])
    mxs = p.sb("mxs", [1, 1])
    sidf = p.sb("sidf", [128, 128])
    idenf = p.din("idenf", [128, 128])
    p.dma("q", sidf[:], idenf, ["idenf"], ["sidf"])
    for tt in range(TSH // 128):
        p.dma("q", ytb[:], yt[tt*128:(tt+1)*128, :], ["ytok_own"], ["ytb"])
        p.dma("q", xnt[:], xn[tt*128:(tt+1)*128, :], ["xnown"], ["xnt"])
        p.op("v", lambda e: e.tensor_copy(ot[:], ytb[:]), ["ytb"], ["ot"])
        p.op("v", lambda e: e.tensor_add(ot[:], ot[:], xnt[:]), ["ot", "xnt"], ["ot"])
        p.op("v", lambda e: e.reduce_sum(mc[:], ot[:], AX.X), ["ot"], ["mc"])
        p.op("v", lambda e: e.tensor_scalar(mc[:], mc[:], 1.0/D, None, ALU.mult), ["mc"], ["mc"])
        p.op("v", lambda e: e.tensor_scalar(ot[:], ot[:], mc[:], None, ALU.subtract), ["ot", "mc"], ["ot"])
        p.op("s", lambda e: e.activation(sq[:], ot[:], AF.Square, accum_out=vc[:]), ["ot"], ["sq", "vc"])
        p.op("v", lambda e: e.tensor_scalar(vc[:], vc[:], 1.0/D, 1e-5, ALU.mult, ALU.add), ["vc"], ["vc"])
        p.op("s", lambda e: e.activation(vc[:], vc[:], AF.Sqrt), ["vc"], ["vc"])
        p.op("v", lambda e: e.reciprocal(vc[:], vc[:]), ["vc"], ["vc"])
        p.op("v", lambda e: e.tensor_scalar(ot[:], ot[:], vc[:], None, ALU.mult), ["ot", "vc"], ["ot"])
        p.op("v", lambda e: e.tensor_mul(ot[:], ot[:], gb[:]), ["ot", "gb"], ["ot"])
        p.op("v", lambda e: e.tensor_add(ot[:], ot[:], bb[:]), ["ot", "bb"], ["ot"])
        p.op("s", lambda e: e.activation(ot[:], ot[:], AF.Silu), ["ot"], ["ot"])
        p.dma("q", xt[:], x[tt*128:(tt+1)*128, :], ["xown"], ["xt"])
        p.dma("q", st[:], s0[tt*128:(tt+1)*128, :], ["s0own"], ["st"])
        p.op("v", lambda e: e.tensor_mul(ot[:], ot[:], st[:]), ["ot", "st"], ["ot"])
        p.op("v", lambda e: e.tensor_add(xt[:], xt[:], ot[:]), ["xt", "ot"], ["xt"])
        p.dma("q", x1o[tt*128:(tt+1)*128, :], xt[:], ["xt"], ["x1o"])
        p.op("v", lambda e: e.reduce_sum(mc[:], xt[:], AX.X), ["xt"], ["mc"])
        p.op("v", lambda e: e.tensor_scalar(mc[:], mc[:], 1.0/D, None, ALU.mult), ["mc"], ["mc"])
        p.op("v", lambda e: e.tensor_scalar(xnt[:], xt[:], mc[:], None, ALU.subtract), ["xt", "mc"], ["xnt2"])
        p.op("s", lambda e: e.activation(sq[:], xnt[:], AF.Square, accum_out=vc[:]), ["xnt2"], ["sq", "vc"])
        p.op("v", lambda e: e.tensor_scalar(vc[:], vc[:], 1.0/D, 1e-5, ALU.mult, ALU.add), ["vc"], ["vc"])
        p.op("s", lambda e: e.activation(vc[:], vc[:], AF.Sqrt), ["vc"], ["vc"])
        p.op("v", lambda e: e.reciprocal(vc[:], vc[:]), ["vc"], ["vc"])
        p.op("v", lambda e: e.tensor_scalar(xnt[:], xnt[:], vc[:], None, ALU.mult), ["xnt2", "vc"], ["xnt2"])
        p.dma("q", xn1o[tt*128:(tt+1)*128, :], xnt[:], ["xnt2"], ["xn1o"])
        p.op("v", lambda e: e.reduce_max(mx[:], xnt[:], AX.X, apply_absolute_value=True), ["xnt2"], ["mx"])
        p.op("v", lambda e: e.tensor_scalar_max(mx[:], mx[:], 1e-6), ["mx"], ["mx"])
        p.dma("q", ma1o[tt*128:(tt+1)*128].unsqueeze(1), mx[:], ["mx"], ["ma1o"])
        if tt == 0:
            p.op("v", lambda e: e.tensor_copy(gmax[:], mx[:]), ["mx"], ["gmax"])
        else:
            p.op("v", lambda e: e.tensor_max(gmax[:], gmax[:], mx[:]), ["gmax", "mx"], ["gmax"])
        p.op("v", lambda e: e.reciprocal(mc[:], mx[:]), ["mx"], ["mc"])
        p.op("v", lambda e: e.tensor_scalar(mc[:], mc[:], 7.0, None, ALU.mult), ["mc"], ["mc"])
        p.op("v", lambda e: e.tensor_scalar(ot[:], xnt[:], mc[:], RBIG, ALU.mult, ALU.add), ["xnt2", "mc"], ["ot"])
        p.op("v", lambda e: e.tensor_scalar(ot[:], ot[:], RBIG, None, ALU.subtract), ["ot"], ["ot"])
        p.op("v", lambda e: e.tensor_scalar_min(ot[:], ot[:], 7.0), ["ot"], ["ot"])
        p.op("v", lambda e: e.tensor_scalar_max(ot[:], ot[:], -8.0), ["ot"], ["ot"])
        p.op("v", lambda e: e.tensor_copy(xqb[:], ot[:]), ["ot"], ["xqb"])
        for f in range(8):
            p.op("t", lambda e, f=f: e.transpose(ptb[:], xqb[:, f*128:(f+1)*128], sid[:]), ["xqb", "sid"], ["ptb"])
            p.op("v", lambda e, f=f: e.tensor_copy(trb[:, f*128:(f+1)*128], ptb[:]), ["ptb"], ["trb"])
        p.dma("g", xq1o.rearrange("(f p) n -> p f n", p=128)[:, :, tt*128:(tt+1)*128],
              trb.rearrange("p (f n) -> p f n", f=8), ["trb"], ["xq1o"])
    p.op("t", lambda e: e.transpose(pmx[0:1, 0:128], gmax[:], sidf[:]), ["gmax", "sidf"], ["pmx"])
    p.op("v", lambda e: e.tensor_copy(mxf[:], pmx[0:1, 0:128]), ["pmx"], ["mxf"])
    p.op("v", lambda e: e.reduce_max(mxs[:], mxf[:], AX.X), ["mxf"], ["mxs"])
    p.dma("q", xmaxo, mxs[:], ["mxs"], ["xmaxo"])


def _emit_ternary(p, w_ap, key, sid, sidf, ones128f):
    wt = p.sb(key + "wt", [128, 8 * D])
    wa = p.sb(key + "wa", [128, 8 * D])
    acol = p.sb(key + "ac", [128, 8])
    asum = p.sb(key + "as", [128, 1])
    ps1 = p.ps(key + "p1", [1, 1])
    thr = p.sb(key + "th", [1, 1])
    thrc = p.sb(key + "tc", [128, 1])
    psc = p.ps(key + "pc", [128, 1])
    sgn = p.sb(key + "sg", [128, D])
    kpm = p.sb(key + "kp", [128, D])
    wtb = p.sb(key + "wb", [128, D], BF16)
    WtT = p.sb(key + "WT", [128, 8 * D], BF16)
    ptb = p.ps(key + "pt", [128, 128], BF16)
    p.dma("q", wt.rearrange("p (t c) -> p t c", t=8), w_ap.rearrange("(t p) c -> p t c", p=128),
          [key + "w"], [key + "wt"])
    p.op("s", lambda e: e.activation(wa[:], wt[:], AF.Abs), [key + "wt"], [key + "wa"])
    for t in range(8):
        p.op("v", lambda e, t=t: e.reduce_sum(acol[:, t:t+1], wa[:, t*D:(t+1)*D], AX.X),
             [key + "wa"], [key + "ac"])
    p.op("v", lambda e: e.tensor_scalar_mul(acol[:], acol[:], 1.0 / D), [key + "ac"], [key + "ac"])
    p.op("v", lambda e: e.reduce_sum(asum[:], acol[:], AX.X), [key + "ac"], [key + "as"])
    p.op("t", lambda e: e.matmul(ps1[:], ones128f[:], asum[:], start=True, stop=True),
         ["ones128f", key + "as"], [key + "p1"])
    p.op("v", lambda e: e.tensor_scalar(thr[:], ps1[:], 0.05 / 1024.0, None, ALU.mult),
         [key + "p1"], [key + "th"])
    p.op("t", lambda e: e.matmul(psc[:], ones128f[0:1, :].transpose([1, 0]) if False else thr[:], thr[:], start=True, stop=True),
         [key + "th"], [key + "pc"]) if False else None
    ones1f = p.sb(key + "o1", [1, 128])
    p.op("v", lambda e: e.memset(ones1f[:], 1.0), [], [key + "o1"])
    p.op("t", lambda e: e.matmul(psc[:], ones1f[:], thr[:], start=True, stop=True),
         [key + "o1", key + "th"], [key + "pc"])
    p.op("v", lambda e: e.tensor_copy(thrc[:], psc[:]), [key + "pc"], [key + "tc"])
    for t in range(8):
        p.op("s", lambda e, t=t: e.activation(sgn[:], wt[:, t*D:(t+1)*D], AF.Sign), [key + "wt"], [key + "sg"])
        p.op("v", lambda e, t=t: e.tensor_scalar(kpm[:], wa[:, t*D:(t+1)*D], thrc[:], None, ALU.is_ge),
             [key + "wa", key + "tc"], [key + "kp"])
        p.op("v", lambda e: e.tensor_mul(sgn[:], sgn[:], kpm[:]), [key + "sg", key + "kp"], [key + "sg"])
        p.op("v", lambda e: e.tensor_copy(wtb[:], sgn[:]), [key + "sg"], [key + "wb"])
        for f in range(8):
            p.op("t", lambda e, f=f: e.transpose(ptb[:], wtb[:, f*128:(f+1)*128], sid[:]),
                 [key + "wb", "sid"], [key + "pt"])
            p.op("v", lambda e, f=f, t=t: e.tensor_copy(WtT[:, f*D+t*128:f*D+(t+1)*128], ptb[:]),
                 [key + "pt"], [key + "WT"])
    return WtT, acol


def emit_LH(p):
    xq = p.din("xqkv", [D, TSH], BF16)
    ma = p.din("makv", [TSH])
    wq = p.din("LHw", [D, D])
    bq = p.din("LHb", [D])
    iden = p.din("iden", [128, 128], BF16)
    projo = p.dout("projkv", [D, TSH])
    sid = p.sb("sid", [128, 128], BF16)
    p.dma("q", sid[:], iden, ["iden"], ["sid"])
    ones128f = p.sb("ones128f", [128, 1])
    p.op("v", lambda e: e.memset(ones128f[:], 1.0), [], ["ones128f"])
    WtT, acol = _emit_ternary(p, wq, "LHw", sid, None, ones128f)
    p.op("v", lambda e: e.tensor_scalar_mul(acol[:], acol[:], 1.0 / 7.0), ["LHwac"], ["LHwac"])
    bcol = p.sb("bcol", [128, 8])
    p.dma("q", bcol[:], bq.rearrange("(m p) -> p m", p=128), ["LHb"], ["bcol"],
          allow_slow_non_contiguous=True)
    xqs = p.sb("xqs", [128, 8 * TSH], BF16)
    p.dma("q", xqs.rearrange("p (t n) -> p t n", t=8), xq.rearrange("(t p) n -> p t n", p=128),
          ["xqkv"], ["xqs"])
    mab = p.sb("mab", [128, 512])
    pmm = p.ps("pmm", [128, 1024])
    t1 = p.sb("t1", [128, 512])
    for ch in range(4):
        p.dma("q", mab[:], ma[ch*512:(ch+1)*512].unsqueeze(0).to_broadcast([128, 512]),
              ["makv"], ["mab"])
        for oc in range(8):
            sl = oc % 2
            for kt in range(8):
                p.op("t", lambda e, oc=oc, kt=kt, ch=ch, sl=sl:
                     e.matmul(pmm[:, sl*512:(sl+1)*512], WtT[:, kt*D+oc*128:kt*D+(oc+1)*128],
                              xqs[:, kt*TSH+ch*512:kt*TSH+ch*512+512], start=(kt == 0), stop=(kt == 7)),
                     ["LHwWT", "xqs"], [f"pmm{sl}"], "p" if kt else "c")
            p.op("v", lambda e, oc=oc, sl=sl: e.tensor_scalar(t1[:], pmm[:, sl*512:(sl+1)*512],
                                                              acol[:, oc:oc+1], None, ALU.mult),
                 [f"pmm{sl}", "LHwac"], ["t1"])
            p.op("v", lambda e: e.tensor_mul(t1[:], t1[:], mab[:]), ["t1", "mab"], ["t1"])
            p.op("v", lambda e, oc=oc: e.tensor_scalar(t1[:], t1[:], bcol[:, oc:oc+1], None, ALU.add),
                 ["t1", "bcol"], ["t1"])
            p.dma("q", projo[oc*128:(oc+1)*128, ch*512:(ch+1)*512], t1[:], ["t1"], ["projo"])


def emit_LI(p):
    qin = p.din("qin", [512, LSEQ])
    kvin = p.din("kvin", [D, 2048])
    U = p.din("U", [64, 32])
    cosq = p.din("cosq", [LSEQ, 32]); sinq = p.din("sinq", [LSEQ, 32])
    cosk = p.din("cosk", [LSEQ, 32]); sink = p.din("sink", [LSEQ, 32])
    iden = p.din("iden", [128, 128], BF16)
    idenf = p.din("idenf", [128, 128])
    y1c = p.dout("y1c", [512, LSEQ])
    sid = p.sb("sid", [128, 128], BF16)
    sidf = p.sb("sidf", [128, 128])
    p.dma("q", sid[:], iden, ["iden"], ["sid"])
    p.dma("q", sidf[:], idenf, ["idenf"], ["sidf"])
    Ub = p.sb("Ub", [64, 32], BF16)
    p.dma("g", Ub[:], U, ["U"], ["Ub"])
    tabs = {}
    for nm, ap in (("cosq", cosq), ("sinq", sinq), ("cosk", cosk), ("sink", sink)):
        t = p.sb("s" + nm, [128, 32 * 32])
        p.dma("q", t.rearrange("p (n c) -> p n c", n=32), ap.rearrange("(n p) c -> p n c", p=128),
              [nm], ["s" + nm])
        tabs[nm] = t
    hfmb = p.sb("hfmb", [64, LSEQ], BF16)
    hT = p.sb("hT", [128, 32 * 64], BF16)
    rT = p.sb("rT", [128, 32 * 64], BF16)
    ptb = p.ps("ptb", [128, 128], BF16)
    rfm = p.sb("rfm", [64, LSEQ], BF16)
    qlf = p.sb("qlf", [32, LSEQ], BF16)
    klf = p.sb("klf", [32, LSEQ], BF16)
    vte = p.sb("vte", [128, 32 * 65], BF16)
    pU = p.ps("pU", [32, 512])
    pS = [p.ps(f"pS{i}", [128, 512]) for i in range(2)]
    pt_ = [p.sb(f"pt_{i}", [128, 512], BF16) for i in range(2)]
    pY = p.ps("pY", [65, 512])
    pR = p.ps("pR", [64, 512])
    ones1 = p.sb("ones1", [1, 64])
    p.op("v", lambda e: e.memset(ones1[:], 1.0), [], ["ones1"])
    rs = p.sb("rs", [1, 512])
    rsb = p.sb("rsb", [64, 512])
    yo = p.sb("yo", [64, 512])

    def rot_path(src_key, loader, ctab, stab, dst, dstkey):
        loader()
        for nt in range(32):
            p.op("t", lambda e, nt=nt: e.transpose(ptb[0:128, 0:64], hfmb[:, nt*128:(nt+1)*128], sid[0:64, 0:64]),
                 [src_key, "sid"], ["ptb"])
            p.op("v", lambda e, nt=nt: e.tensor_copy(hT[:, nt*64:(nt+1)*64], ptb[0:128, 0:64]),
                 ["ptb"], ["hT"])
        for nt in range(32):
            ev = hT[:, nt*64:(nt+1)*64].rearrange("p (c two) -> p c two", two=2)[:, :, 0]
            od = hT[:, nt*64:(nt+1)*64].rearrange("p (c two) -> p c two", two=2)[:, :, 1]
            ct = ctab[:, nt*32:(nt+1)*32]
            st_ = stab[:, nt*32:(nt+1)*32]
            p.op("v", lambda e, ev=ev, ct=ct, nt=nt: e.tensor_mul(rT[:, nt*64:nt*64+32], ev, ct),
                 ["hT", "ctab"], ["rT"])
            p.op("v", lambda e, od=od, st_=st_, nt=nt: e.tensor_mul(scr32[:], od, st_),
                 ["hT", "stab"], ["scr32"])
            p.op("v", lambda e, nt=nt: e.tensor_sub(rT[:, nt*64:nt*64+32], rT[:, nt*64:nt*64+32], scr32[:]),
                 ["rT", "scr32"], ["rT"])
            p.op("v", lambda e, ev=ev, st_=st_, nt=nt: e.tensor_mul(rT[:, nt*64+32:nt*64+64], ev, st_),
                 ["hT", "stab"], ["rT"])
            p.op("v", lambda e, od=od, ct=ct, nt=nt: e.tensor_mul(scr32[:], od, ct),
                 ["hT", "ctab"], ["scr32"])
            p.op("v", lambda e, nt=nt: e.tensor_add(rT[:, nt*64+32:nt*64+64], rT[:, nt*64+32:nt*64+64], scr32[:]),
                 ["rT", "scr32"], ["rT"])
        for nt in range(32):
            p.op("t", lambda e, nt=nt: e.transpose(ptb[0:64, 0:128], rT[:, nt*64:(nt+1)*64], sid[:]),
                 ["rT", "sid"], ["ptb"])
            p.op("v", lambda e, nt=nt: e.tensor_copy(dst[:, nt*128:(nt+1)*128], ptb[0:64, 0:128]),
                 ["ptb"], [dstkey])

    scr32 = p.sb("scr32", [128, 32], BF16)
    for hp in range(8):
        p.dma("g", hfmb[:], qin[hp*64:(hp+1)*64, :], ["qin"], ["hfmb"])
        rot_path("hfmb", lambda: None, tabs["cosq"], tabs["sinq"], rfm, "rfm")
        for ch in range(8):
            p.op("t", lambda e, ch=ch: e.matmul(pU[:], Ub[:], rfm[:, ch*512:(ch+1)*512], start=True, stop=True),
                 ["Ub", "rfm"], ["pU"])
            p.op("v", lambda e, ch=ch: e.tensor_copy(qlf[:, ch*512:(ch+1)*512], pU[:]), ["pU"], ["qlf"])
        kview = kvin.rearrange("(jc dd) n -> dd jc n", dd=64)
        p.dma("g", hfmb.rearrange("p (jc nm) -> p jc nm", jc=16),
              kview[:, :, hp*256:(hp+1)*256], ["kvin"], ["hfmb"])
        rot_path("hfmb", lambda: None, tabs["cosk"], tabs["sink"], rfm, "rfm")
        for ch in range(8):
            p.op("t", lambda e, ch=ch: e.matmul(pU[:], Ub[:], rfm[:, ch*512:(ch+1)*512], start=True, stop=True),
                 ["Ub", "rfm"], ["pU"])
            p.op("v", lambda e, ch=ch: e.tensor_copy(klf[:, ch*512:(ch+1)*512], pU[:]), ["pU"], ["klf"])
        p.dma("g", hfmb.rearrange("p (jc nm) -> p jc nm", jc=16),
              kview[:, :, hp*256:(hp+1)*256], ["kvin"], ["hfmb"])
        for nt in range(32):
            p.op("t", lambda e, nt=nt: e.transpose(ptb[0:128, 0:64], hfmb[:, nt*128:(nt+1)*128], sid[0:64, 0:64]),
                 ["hfmb", "sid"], ["ptb"])
            p.op("v", lambda e, nt=nt: e.tensor_copy(vte[:, nt*65:nt*65+64], ptb[0:128, 0:64]),
                 ["ptb"], ["vte"])
        p.op("v", lambda e: e.memset(vte.rearrange("p (k c) -> p k c", k=32)[:, :, 64:65], 1.0),
             ["vte"], ["vte"])
        for qc in range(8):
            def emit_S(kt, qc=qc):
                i = kt % 2
                p.op("t", lambda e, kt=kt, qc=qc, i=i:
                     e.matmul(pS[i][:], klf[:, kt*128:(kt+1)*128], qlf[:, qc*512:(qc+1)*512],
                              start=True, stop=True), ["klf", "qlf"], [f"pS{i}"])
                p.op("s", lambda e, i=i: e.activation(pt_[i][:], pS[i][:], AF.Exp, scale=0.125),
                     [f"pS{i}"], [f"pt_{i}"])

            def emit_PV(kt):
                i = kt % 2
                p.op("t", lambda e, kt=kt, i=i: e.matmul(pY[:], vte[:, kt*65:(kt+1)*65], pt_[i][:],
                                                         start=(kt == 0), stop=(kt == 31)),
                     ["vte", f"pt_{i}"], ["pY"], "p" if kt else "c")

            emit_S(0)
            for kt in range(1, 32):
                emit_S(kt)
                emit_PV(kt - 1)
            emit_PV(31)
            p.op("v", lambda e: e.reciprocal(rs[:], pY[64:65, :]), ["pY"], ["rs"])
            p.op("t", lambda e: e.matmul(pR[:], ones1[:], rs[:], start=True, stop=True),
                 ["ones1", "rs"], ["pR"])
            p.op("s", lambda e: e.copy(rsb[:], pR[:]), ["pR"], ["rsb"])
            p.op("v", lambda e: e.tensor_mul(yo[:], pY[0:64, :], rsb[:]), ["pY", "rsb"], ["yo"])
            p.dma("q", y1c[hp*64:(hp+1)*64, qc*512:(qc+1)*512], yo[:], ["yo"], ["y1c"])


def emit_LJ(p):
    y1 = p.din("y1own", [D, TSH])
    wo = p.din("LJw", [D, D])
    iden = p.din("iden", [128, 128], BF16)
    idenf = p.din("idenf", [128, 128])
    s1o = p.dout("s1_fm", [D, TSH])
    ma2o = p.dout("ma2", [TSH])
    sid = p.sb("sid", [128, 128], BF16)
    sidf = p.sb("sidf", [128, 128])
    p.dma("q", sid[:], iden, ["iden"], ["sid"])
    p.dma("q", sidf[:], idenf, ["idenf"], ["sidf"])
    ones128f = p.sb("ones128f", [128, 1])
    p.op("v", lambda e: e.memset(ones128f[:], 1.0), [], ["ones128f"])
    WtT, acol = _emit_ternary(p, wo, "LJw", sid, None, ones128f)
    p.op("v", lambda e: e.tensor_scalar_mul(acol[:], acol[:], 1.0 / 7.0), ["LJwac"], ["LJwac"])
    ytm = p.sb("ytm", [128, D])
    ptf = p.ps("ptf2", [128, 128])
    ptb = p.ps("ptb2", [128, 128], BF16)
    mx = p.sb("mx2", [128, 1]); mc = p.sb("mc2", [128, 1])
    xqb = p.sb("xqb2", [128, D], BF16)
    xqf = p.sb("xqf2", [128, 8 * TSH], BF16)
    y1s = p.sb("y1s", [128, 8 * 128])
    for tt in range(TSH // 128):
        p.dma("q", y1s.rearrange("p (t n) -> p t n", t=8),
              y1.rearrange("(t p) n -> p t n", p=128)[:, :, tt*128:(tt+1)*128], ["y1own"], ["y1s"])
        for f in range(8):
            p.op("t", lambda e, f=f: e.transpose(ptf[:], y1s[:, f*128:(f+1)*128], sidf[:]),
                 ["y1s", "sidf"], ["ptfk"])
            p.op("v", lambda e, f=f: e.tensor_copy(ytm[:, f*128:(f+1)*128], ptf[:]), ["ptfk"], ["ytm"])
        p.op("v", lambda e: e.reduce_max(mx[:], ytm[:], AX.X, apply_absolute_value=True), ["ytm"], ["mx2"])
        p.op("v", lambda e: e.tensor_scalar_max(mx[:], mx[:], 1e-6), ["mx2"], ["mx2"])
        p.dma("q", ma2o[tt*128:(tt+1)*128].unsqueeze(1), mx[:], ["mx2"], ["ma2o"])
        p.op("v", lambda e: e.reciprocal(mc[:], mx[:]), ["mx2"], ["mc2"])
        p.op("v", lambda e: e.tensor_scalar(mc[:], mc[:], 7.0, None, ALU.mult), ["mc2"], ["mc2"])
        p.op("v", lambda e: e.tensor_scalar(ytm[:], ytm[:], mc[:], RBIG, ALU.mult, ALU.add), ["ytm", "mc2"], ["ytm"])
        p.op("v", lambda e: e.tensor_scalar(ytm[:], ytm[:], RBIG, None, ALU.subtract), ["ytm"], ["ytm"])
        p.op("v", lambda e: e.tensor_scalar_min(ytm[:], ytm[:], 7.0), ["ytm"], ["ytm"])
        p.op("v", lambda e: e.tensor_scalar_max(ytm[:], ytm[:], -8.0), ["ytm"], ["ytm"])
        p.op("v", lambda e: e.tensor_copy(xqb[:], ytm[:]), ["ytm"], ["xqb2"])
        for f in range(8):
            p.op("t", lambda e, f=f: e.transpose(ptb[:], xqb[:, f*128:(f+1)*128], sid[:]), ["xqb2", "sid"], ["ptbk"])
            p.op("v", lambda e, f=f, tt=tt: e.tensor_copy(xqf[:, f*TSH+tt*128:f*TSH+(tt+1)*128], ptb[:]),
                 ["ptbk"], ["xqf2"])
    mab = p.sb("mab2", [128, 512])
    pmm = p.ps("pmm2", [128, 1024])
    t1 = p.sb("t12", [128, 512])
    for ch in range(4):
        p.dma("q", mab[:], ma2o[ch*512:(ch+1)*512].unsqueeze(0).to_broadcast([128, 512]),
              ["ma2o"], ["mab2"])
        for oc in range(8):
            sl = oc % 2
            for kt in range(8):
                p.op("t", lambda e, oc=oc, kt=kt, ch=ch, sl=sl:
                     e.matmul(pmm[:, sl*512:(sl+1)*512], WtT[:, kt*D+oc*128:kt*D+(oc+1)*128],
                              xqf[:, kt*TSH+ch*512:kt*TSH+ch*512+512], start=(kt == 0), stop=(kt == 7)),
                     ["LJwWT", "xqf2"], [f"pm2{sl}"], "p" if kt else "c")
            p.op("v", lambda e, oc=oc, sl=sl: e.tensor_scalar(t1[:], pmm[:, sl*512:(sl+1)*512],
                                                              acol[:, oc:oc+1], None, ALU.mult),
                 [f"pm2{sl}", "LJwac"], ["t12"])
            p.op("v", lambda e: e.tensor_mul(t1[:], t1[:], mab[:]), ["t12", "mab2"], ["t12"])
            p.dma("q", s1o[oc*128:(oc+1)*128, ch*512:(ch+1)*512], t1[:], ["t12"], ["s1o"])


def _emit_globmax_rows(p, key, src_tiles_fn, ntiles, sidf, out_ap):
    pass

def emit_LK1(p):
    xn1 = p.din("xn1own", [TSH, D])
    xmax8 = p.din("xmax8", [8, 1])
    w1 = p.din("L1w1", [4096, D])
    b1 = p.din("L1b1", [4096])
    iden = p.din("iden", [128, 128], BF16)
    idenf = p.din("idenf", [128, 128])
    hfmo = p.dout("h_fmb", [4096, TSH], BF16)
    hmaxo = p.dout("hmaxc", [1, 1])
    sid = p.sb("sid", [128, 128], BF16)
    sidf = p.sb("sidf", [128, 128])
    p.dma("q", sid[:], iden, ["iden"], ["sid"])
    p.dma("q", sidf[:], idenf, ["idenf"], ["sidf"])
    ones1f = p.sb("ones1f", [1, 128])
    p.op("v", lambda e: e.memset(ones1f[:], 1.0), [], ["ones1f"])
    xm8 = p.sb("xm8", [1, 8])
    p.dma("q", xm8[:], xmax8.rearrange("a b -> b a"), ["xmax8"], ["xm8"])
    sxs = p.sb("sxs", [1, 1])
    p.op("v", lambda e: e.reduce_max(sxs[:], xm8[:], AX.X), ["xm8"], ["sxs"])
    p.op("v", lambda e: e.tensor_scalar_max(sxs[:], sxs[:], 1e-6), ["sxs"], ["sxs"])
    p.op("v", lambda e: e.tensor_scalar_mul(sxs[:], sxs[:], 1.0 / 127.0), ["sxs"], ["sxs"])
    sxi = p.sb("sxi", [1, 1])
    p.op("v", lambda e: e.reciprocal(sxi[:], sxs[:]), ["sxs"], ["sxi"])
    psk = p.ps("psk", [128, 1])
    sxc = p.sb("sxc", [128, 1])
    p.op("t", lambda e: e.matmul(psk[:], ones1f[:], sxi[:], start=True, stop=True),
         ["ones1f", "sxi"], ["psk"])
    p.op("v", lambda e: e.tensor_copy(sxc[:], psk[:]), ["psk"], ["sxc"])
    wt = p.sb("wt", [128, D])
    wmax = p.sb("wmax", [128, 1])
    mxt = p.sb("mxt", [128, 1])
    for i in range(32):
        p.dma("q", wt[:], w1[i*128:(i+1)*128, :], ["L1w1"], ["wt"])
        if i == 0:
            p.op("v", lambda e: e.reduce_max(wmax[:], wt[:], AX.X, apply_absolute_value=True),
                 ["wt"], ["wmax"])
        else:
            p.op("v", lambda e: e.reduce_max(mxt[:], wt[:], AX.X, apply_absolute_value=True),
                 ["wt"], ["mxt"])
            p.op("v", lambda e: e.tensor_max(wmax[:], wmax[:], mxt[:]), ["wmax", "mxt"], ["wmax"])
    pmx = p.ps("pmx", [1, 128])
    mxr = p.sb("mxr", [1, 128])
    sw1 = p.sb("sw1", [1, 1])
    p.op("t", lambda e: e.transpose(pmx[:], wmax[:], sidf[:]), ["wmax", "sidf"], ["pmx"])
    p.op("v", lambda e: e.tensor_copy(mxr[:], pmx[:]), ["pmx"], ["mxr"])
    p.op("v", lambda e: e.reduce_max(sw1[:], mxr[:], AX.X), ["mxr"], ["sw1"])
    p.op("v", lambda e: e.tensor_scalar_max(sw1[:], sw1[:], 1e-6), ["sw1"], ["sw1"])
    p.op("v", lambda e: e.tensor_scalar_mul(sw1[:], sw1[:], 1.0 / 127.0), ["sw1"], ["sw1"])
    sw1i = p.sb("sw1i", [1, 1])
    p.op("v", lambda e: e.reciprocal(sw1i[:], sw1[:]), ["sw1"], ["sw1i"])
    sw1c = p.sb("sw1c", [128, 1])
    p.op("t", lambda e: e.matmul(psk[:], ones1f[:], sw1i[:], start=True, stop=True),
         ["ones1f", "sw1i"], ["psk"])
    p.op("v", lambda e: e.tensor_copy(sw1c[:], psk[:]), ["psk"], ["sw1c"])
    ssx = p.sb("ssx", [1, 1])
    p.op("v", lambda e: e.tensor_mul(ssx[:], sw1[:], sxs[:]), ["sw1", "sxs"], ["ssx"])
    ssxc = p.sb("ssxc", [128, 1])
    p.op("t", lambda e: e.matmul(psk[:], ones1f[:], ssx[:], start=True, stop=True),
         ["ones1f", "ssx"], ["psk"])
    p.op("v", lambda e: e.tensor_copy(ssxc[:], psk[:]), ["psk"], ["ssxc"])
    w1qT = p.sb("w1qT", [128, 8 * 4096], BF16)
    wq = p.sb("wq", [128, D], BF16)
    ptb = p.ps("ptbk", [128, 128], BF16)
    for i in range(32):
        p.dma("q", wt[:], w1[i*128:(i+1)*128, :], ["L1w1"], ["wt"])
        p.op("v", lambda e: e.tensor_scalar(wt[:], wt[:], sw1c[:], RBIG, ALU.mult, ALU.add), ["wt", "sw1c"], ["wt"])
        p.op("v", lambda e: e.tensor_scalar(wt[:], wt[:], RBIG, None, ALU.subtract), ["wt"], ["wt"])
        p.op("v", lambda e: e.tensor_copy(wq[:], wt[:]), ["wt"], ["wq"])
        for f in range(8):
            p.op("t", lambda e, f=f: e.transpose(ptb[:], wq[:, f*128:(f+1)*128], sid[:]), ["wq", "sid"], ["ptk"])
            p.op("v", lambda e, f=f, i=i: e.tensor_copy(w1qT[:, f*4096+i*128:f*4096+(i+1)*128], ptb[:]),
                 ["ptk"], ["w1qT"])
    xqf = p.sb("xqfk", [128, 8 * TSH], BF16)
    xt = p.sb("xtk", [128, D])
    xqb = p.sb("xqbk", [128, D], BF16)
    for tt in range(TSH // 128):
        p.dma("q", xt[:], xn1[tt*128:(tt+1)*128, :], ["xn1own"], ["xtk"])
        p.op("v", lambda e: e.tensor_scalar(xt[:], xt[:], sxc[:], RBIG, ALU.mult, ALU.add), ["xtk", "sxc"], ["xtk"])
        p.op("v", lambda e: e.tensor_scalar(xt[:], xt[:], RBIG, None, ALU.subtract), ["xtk"], ["xtk"])
        p.op("v", lambda e: e.tensor_copy(xqb[:], xt[:]), ["xtk"], ["xqbk"])
        for f in range(8):
            p.op("t", lambda e, f=f: e.transpose(ptb[:], xqb[:, f*128:(f+1)*128], sid[:]), ["xqbk", "sid"], ["ptk"])
            p.op("v", lambda e, f=f, tt=tt: e.tensor_copy(xqf[:, f*TSH+tt*128:f*TSH+(tt+1)*128], ptb[:]),
                 ["ptk"], ["xqfk"])
    b1c = p.sb("b1ck", [128, 32])
    p.dma("q", b1c[:], b1.rearrange("(m p) -> p m", p=128), ["L1b1"], ["b1ck"],
          allow_slow_non_contiguous=True)
    pmm = p.ps("pmmk", [128, 1024])
    hb = p.sb("hbk", [128, 512], BF16)
    hmax = p.sb("hmaxk", [128, 1])
    hm1 = p.sb("hm1k", [128, 1])
    first = [True]
    for mt in range(32):
        for ch in range(4):
            sl = ch % 2
            for kt in range(8):
                p.op("t", lambda e, mt=mt, ch=ch, kt=kt, sl=sl:
                     e.matmul(pmm[:, sl*512:(sl+1)*512], w1qT[:, kt*4096+mt*128:kt*4096+(mt+1)*128],
                              xqf[:, kt*TSH+ch*512:kt*TSH+ch*512+512], start=(kt == 0), stop=(kt == 7)),
                     ["w1qT", "xqfk"], [f"pmk{sl}"], "p" if kt else "c")
            p.op("s", lambda e, mt=mt, sl=sl: e.activation(hb[:], pmm[:, sl*512:(sl+1)*512], AF.Gelu,
                                                           bias=b1c[:, mt:mt+1], scale=ssxc[:]),
                 [f"pmk{sl}", "b1ck", "ssxc"], ["hbk"])
            p.op("v", lambda e: e.reduce_max(hm1[:], hb[:], AX.X, apply_absolute_value=True), ["hbk"], ["hm1k"])
            if first[0]:
                p.op("v", lambda e: e.tensor_copy(hmax[:], hm1[:]), ["hm1k"], ["hmaxk"])
                first[0] = False
            else:
                p.op("v", lambda e: e.tensor_max(hmax[:], hmax[:], hm1[:]), ["hmaxk", "hm1k"], ["hmaxk"])
            p.dma("q", hfmo[mt*128:(mt+1)*128, ch*512:(ch+1)*512], hb[:], ["hbk"], ["hfmo"])
    p.op("t", lambda e: e.transpose(pmx[:], hmax[:], sidf[:]), ["hmaxk", "sidf"], ["pmx"])
    p.op("v", lambda e: e.tensor_copy(mxr[:], pmx[:]), ["pmx"], ["mxr"])
    p.op("v", lambda e: e.reduce_max(sw1[:], mxr[:], AX.X), ["mxr"], ["hms"])
    p.dma("q", hmaxo, sw1[:], ["hms"], ["hmaxo"])


def emit_LK2(p):
    hfm = p.din("h_own", [4096, TSH], BF16)
    hmax8 = p.din("hmax8", [8, 1])
    w2 = p.din("L1w2", [D, 4096])
    b2 = p.din("L1b2", [D])
    s1 = p.din("s1own", [D, TSH])
    x1 = p.din("x1own", [TSH, D])
    iden = p.din("iden", [128, 128], BF16)
    idenf = p.din("idenf", [128, 128])
    outo = p.dout("out_tm", [TSH, D])
    sid = p.sb("sid", [128, 128], BF16)
    sidf = p.sb("sidf", [128, 128])
    p.dma("q", sid[:], iden, ["iden"], ["sid"])
    p.dma("q", sidf[:], idenf, ["idenf"], ["sidf"])
    ones1f = p.sb("ones1f", [1, 128])
    p.op("v", lambda e: e.memset(ones1f[:], 1.0), [], ["ones1f"])
    psk = p.ps("psk", [128, 1])
    hm8 = p.sb("hm8", [1, 8])
    p.dma("q", hm8[:], hmax8.rearrange("a b -> b a"), ["hmax8"], ["hm8"])
    shs = p.sb("shs", [1, 1])
    p.op("v", lambda e: e.reduce_max(shs[:], hm8[:], AX.X), ["hm8"], ["shs"])
    p.op("v", lambda e: e.tensor_scalar_max(shs[:], shs[:], 1e-6), ["shs"], ["shs"])
    p.op("v", lambda e: e.tensor_scalar_mul(shs[:], shs[:], 1.0 / 127.0), ["shs"], ["shs"])
    shi = p.sb("shi", [1, 1])
    p.op("v", lambda e: e.reciprocal(shi[:], shs[:]), ["shs"], ["shi"])
    shc = p.sb("shc", [128, 1])
    p.op("t", lambda e: e.matmul(psk[:], ones1f[:], shi[:], start=True, stop=True),
         ["ones1f", "shi"], ["psk"])
    p.op("v", lambda e: e.tensor_copy(shc[:], psk[:]), ["psk"], ["shc"])
    wt = p.sb("wtq", [128, 4096])
    wmax = p.sb("wmaxq", [128, 1]); mxt = p.sb("mxtq", [128, 1])
    for i in range(8):
        p.dma("q", wt[:], w2[i*128:(i+1)*128, :], ["L1w2"], ["wtq"])
        if i == 0:
            p.op("v", lambda e: e.reduce_max(wmax[:], wt[:], AX.X, apply_absolute_value=True), ["wtq"], ["wmaxq"])
        else:
            p.op("v", lambda e: e.reduce_max(mxt[:], wt[:], AX.X, apply_absolute_value=True), ["wtq"], ["mxtq"])
            p.op("v", lambda e: e.tensor_max(wmax[:], wmax[:], mxt[:]), ["wmaxq", "mxtq"], ["wmaxq"])
    pmx = p.ps("pmxq", [1, 128])
    mxr = p.sb("mxrq", [1, 128])
    sw2 = p.sb("sw2", [1, 1])
    p.op("t", lambda e: e.transpose(pmx[:], wmax[:], sidf[:]), ["wmaxq", "sidf"], ["pmxq"])
    p.op("v", lambda e: e.tensor_copy(mxr[:], pmx[:]), ["pmxq"], ["mxrq"])
    p.op("v", lambda e: e.reduce_max(sw2[:], mxr[:], AX.X), ["mxrq"], ["sw2"])
    p.op("v", lambda e: e.tensor_scalar_max(sw2[:], sw2[:], 1e-6), ["sw2"], ["sw2"])
    p.op("v", lambda e: e.tensor_scalar_mul(sw2[:], sw2[:], 1.0 / 127.0), ["sw2"], ["sw2"])
    sw2i = p.sb("sw2i", [1, 1])
    p.op("v", lambda e: e.reciprocal(sw2i[:], sw2[:]), ["sw2"], ["sw2i"])
    sw2c = p.sb("sw2c", [128, 1])
    p.op("t", lambda e: e.matmul(psk[:], ones1f[:], sw2i[:], start=True, stop=True),
         ["ones1f", "sw2i"], ["psk"])
    p.op("v", lambda e: e.tensor_copy(sw2c[:], psk[:]), ["psk"], ["sw2c"])
    ssx = p.sb("ssxq", [1, 1])
    p.op("v", lambda e: e.tensor_mul(ssx[:], sw2[:], shs[:]), ["sw2", "shs"], ["ssxq"])
    ssxc = p.sb("ssxcq", [128, 1])
    p.op("t", lambda e: e.matmul(psk[:], ones1f[:], ssx[:], start=True, stop=True),
         ["ones1f", "ssxq"], ["psk"])
    p.op("v", lambda e: e.tensor_copy(ssxc[:], psk[:]), ["psk"], ["ssxcq"])
    b2c = p.sb("b2cq", [128, 8])
    p.dma("q", b2c[:], b2.rearrange("(m p) -> p m", p=128), ["L1b2"], ["b2cq"],
          allow_slow_non_contiguous=True)
    b2s = p.sb("b2sq", [128, 8])
    p.op("v", lambda e: e.tensor_scalar(b2s[:], b2c[:], ssxc[:], None, ALU.mult), ["b2cq", "ssxcq"], ["b2sq"])
    w2qT = p.sb("w2qT", [128, 32 * D], BF16)
    wqb = p.sb("wqbq", [128, 4096], BF16)
    ptb = p.ps("ptbq", [128, 128], BF16)
    for i in range(8):
        p.dma("q", wt[:], w2[i*128:(i+1)*128, :], ["L1w2"], ["wtq"])
        p.op("v", lambda e: e.tensor_scalar(wt[:], wt[:], sw2c[:], RBIG, ALU.mult, ALU.add), ["wtq", "sw2c"], ["wtq"])
        p.op("v", lambda e: e.tensor_scalar(wt[:], wt[:], RBIG, None, ALU.subtract), ["wtq"], ["wtq"])
        p.op("v", lambda e: e.tensor_copy(wqb[:], wt[:]), ["wtq"], ["wqbq"])
        for f in range(32):
            p.op("t", lambda e, f=f: e.transpose(ptb[:], wqb[:, f*128:(f+1)*128], sid[:]), ["wqbq", "sid"], ["ptq"])
            p.op("v", lambda e, f=f, i=i: e.tensor_copy(w2qT[:, f*D+i*128:f*D+(i+1)*128], ptb[:]),
                 ["ptq"], ["w2qT"])
    hqb = p.sb("hqb", [128, 32 * 512], BF16)
    hq1 = p.sb("hq1", [128, 512])
    pmm = p.ps("pmmq", [128, 1024])
    g1 = p.sb("g1q", [128, 512])
    sgf = p.sb("sgf", [128, 512])
    s1s = p.sb("s1sq", [128, 512])
    sgtm = p.sb("sgtm", [128, 4 * D])
    ptf = p.ps("ptfq", [128, 128])
    xt = p.sb("xtq", [128, D])
    for ch in range(4):
        for mt in range(32):
            p.dma("g", hq1[:], hfm[mt*128:(mt+1)*128, ch*512:(ch+1)*512], ["h_own"], ["hq1"])
            p.op("v", lambda e, mt=mt: e.tensor_scalar(hq1[:], hq1[:], shc[:], RBIG, ALU.mult, ALU.add),
                 ["hq1", "shc"], ["hq1"])
            p.op("v", lambda e: e.tensor_scalar(hq1[:], hq1[:], RBIG, None, ALU.subtract), ["hq1"], ["hq1"])
            p.op("v", lambda e, mt=mt: e.tensor_copy(hqb[:, mt*512:(mt+1)*512], hq1[:]), ["hq1"], ["hqb"])
        for oc in range(8):
            sl = oc % 2
            for mt in range(32):
                p.op("t", lambda e, oc=oc, mt=mt, sl=sl:
                     e.matmul(pmm[:, sl*512:(sl+1)*512], w2qT[:, mt*D+oc*128:mt*D+(oc+1)*128],
                              hqb[:, mt*512:(mt+1)*512], start=(mt == 0), stop=(mt == 31)),
                     ["w2qT", "hqb"], [f"pq{sl}"], "p" if mt else "c")
            p.op("s", lambda e, oc=oc, sl=sl: e.activation(g1[:], pmm[:, sl*512:(sl+1)*512], AF.Silu,
                                                           bias=b2s[:, oc:oc+1], scale=ssxc[:]),
                 [f"pq{sl}", "b2sq", "ssxcq"], ["g1q"])
            p.dma("q", s1s[:], s1[oc*128:(oc+1)*128, ch*512:(ch+1)*512], ["s1own"], ["s1sq"])
            p.op("v", lambda e: e.tensor_mul(sgf[:], g1[:], s1s[:]), ["g1q", "s1sq"], ["sgf"])
            for cc in range(4):
                p.op("t", lambda e, cc=cc: e.transpose(ptf[:], sgf[:, cc*128:(cc+1)*128], sidf[:]),
                     ["sgf", "sidf"], ["ptfq2"])
                p.op("v", lambda e, cc=cc, oc=oc: e.tensor_copy(sgtm[:, cc*D+oc*128:cc*D+(oc+1)*128], ptf[:]),
                     ["ptfq2"], ["sgtm"])
        for cc in range(4):
            tt = ch * 4 + cc
            p.dma("q", xt[:], x1[tt*128:(tt+1)*128, :], ["x1own"], ["xtq"])
            p.op("v", lambda e, cc=cc: e.tensor_add(xt[:], xt[:], sgtm[:, cc*D:(cc+1)*D]),
                 ["xtq", "sgtm"], ["xtq"])
            p.dma("q", outo[tt*128:(tt+1)*128, :], xt[:], ["xtq"], ["outo"])


def _consts():
    import ml_dtypes
    BFn = ml_dtypes.bfloat16
    iden = np.eye(128).astype(BFn)
    idenf = np.eye(128, dtype=np.float32)
    cmask = np.zeros((4, 128, 512), np.float32)
    for m in range(4):
        cmask[m] = (np.arange(512)[None, :] >= (np.arange(128)[:, None] + m * 128))
    utri = np.triu(np.ones((128, 128), np.float32), 1).astype(BFn)
    half = 32
    inv = 1.0 / (10000.0 ** (np.arange(half, dtype=np.float32) / half))
    ang = np.arange(LSEQ, dtype=np.float32)[:, None] * inv[None, :]
    cosq = np.cos(ang).astype(np.float32)
    sinq = np.sin(ang).astype(np.float32)
    cc = np.arange(LSEQ)
    nprime = (cc % 256) * 16 + cc // 256
    cosk = cosq[nprime]
    sink = sinq[nprime]
    return dict(iden=iden, idenf=idenf, cmask=cmask.astype(BFn), utri=utri,
                cosq=cosq, sinq=sinq, cosk=cosk, sink=sink)

def kernel(**inp):
    C = _consts()
    ar = np.ascontiguousarray
    inp.get("L0_ffn_token_mask")
    X2 = ar(np.asarray(inp["L0_attn_x"]).reshape(NTOK, D))
    Wqkv = np.asarray(inp["L0_attn_W_qkv"])
    WL = np.asarray(inp["L0_attn_W_LTM"])
    pars = {k: np.asarray(inp["L0_attn_" + k]) for k in
            ("V_gs", "V_T0", "beta_tau", "beta_gm", "C_ch", "gamma",
             "alpha_ppd", "IC_threshold", "gate")}
    Wo0 = np.asarray(inp["L0_attn_W_o"])
    Wr_ = np.asarray(inp["L0_ffn_Wr"]); br_ = np.asarray(inp["L0_ffn_br"])
    W1m = np.asarray(inp["L0_ffn_W1"]); W2m = np.asarray(inp["L0_ffn_W2"])
    b1m = np.asarray(inp["L0_ffn_b1"]); b2m = np.asarray(inp["L0_ffn_b2"])
    gam0 = np.asarray(inp["L0_ffn_gamma"]); bet0 = np.asarray(inp["L0_ffn_beta"])
    Wq1 = np.asarray(inp["L1_attn_Wqkv_w"]); bq1 = np.asarray(inp["L1_attn_Wqkv_b"])
    U1 = np.asarray(inp["L1_attn_U"]); Wo1 = np.asarray(inp["L1_attn_Wo_w"])
    w1f_ = np.asarray(inp["L1_ffn_w1"]); b1f_ = np.asarray(inp["L1_ffn_b1"])
    w2f_ = np.asarray(inp["L1_ffn_w2"]); b2f_ = np.asarray(inp["L1_ffn_b2"])

    p = P("MK", fused=True)
    p.final_outputs = {"out_tm"}
    RG8 = [list(range(NCORE))]
    RG2 = [[0, 1], [2, 3], [4, 5], [6, 7]]
    AG, RSC = "AllGather", "ReduceScatter"
    I = p.internals

    def stage(pref, fn, wire=None):
        p.wire = dict(wire or {})
        with p.scope(pref):
            fn(p)
        p.wire = {}

    stage("z_", emit_YTZ)
    stage("a_", emit_LA)
    with p.scope("c1_"):
        p.coll(AG, ALU.bypass, "xn_tmb", "xn_tmb_full", [NTOK, D], BF16, RG8)
        p.coll(AG, ALU.bypass, "xn_fmb", "xn_fmb8", [NCORE * D, TSH], BF16, RG8)
    stage("d_", emit_LD, {"xnown": I["xn_fmb"]})
    with p.scope("c2_"):
        for nm in ("e0", "e1", "w0", "w1"):
            p.coll(AG, ALU.bypass, nm, nm + "F", [NTOK], F32, RG8)
    stage("e_", emit_LE, {"e0": I["e0F"], "e1": I["e1F"], "w0": I["w0F"],
                          "w1": I["w1F"], "xn_tmb": I["xn_tmb_full"]})
    stage("ds_", emit_DISPATCH)
    with p.scope("c3_"):
        p.coll(RSC, ALU.add, "xemH", "xeRS", [CAP, 1040], BF16, RG8)
    stage("oh_", emit_ONEHOT)
    stage("f_", emit_LF, {"xem": I["xeRS"]})
    with p.scope("c4_"):
        p.coll(RSC, ALU.add, "ytokP", "ytokRS", [TSH, D], BF16, RG8)
    stage("b1_", emit_LB1)
    with p.scope("b23_"):
        p.wire = {}
        p.prefix = "b2_"
        emit_LB2(p)
        p.prefix = "b3_"
        emit_LB3a(p)
    stage("b4_", emit_LB3b)
    with p.scope("c5_"):
        y0F = p.coll(AG, ALU.bypass, "y0_fmb", "y0F", [D, NTOK], BF16, RG8)
        y0own = p.dram("y0ownD", [D, TSH], BF16)
        p.swdma(lambda e, c: e.dma_start(out=y0own, in_=y0F[:, c*TSH:(c+1)*TSH]),
                ["y0F"], ["y0ownD"])
    stage("lc_", emit_LC, {"y0_fmb": I["y0F"], "y0own": I["y0ownD"]})
    stage("g_", emit_LG, {"xnown": I["xn_tm"], "xown": p.inputs["x"],
                          "s0own": I["s0_tm"], "ytok_own": I["ytokRS"]})
    with p.scope("c6_"):
        p.coll(AG, ALU.bypass, "xmax1c", "xmax8F", [NCORE, 1], F32, RG8)
    stage("h_", emit_LH, {"xqkv": I["xq1_fmb"], "makv": I["ma1"]})
    stage("k1_", emit_LK1, {"xn1own": I["xn1_tm"], "xmax8": I["xmax8F"]})
    with p.scope("c7_"):
        projB = p.coll(AG, ALU.bypass, "projkv", "projB", [2 * D, TSH], F32, RG2)
        qinD = p.dram("qinD", [512, LSEQ], F32)
        p.swdma(lambda e, c: e.dma_start(out=qinD[:, 0:TSH],
                in_=projB[(c % 2)*512:(c % 2)*512+512, :]), ["projB"], ["qinD"])
        p.swdma(lambda e, c: e.dma_start(out=qinD[:, TSH:LSEQ],
                in_=projB[D+(c % 2)*512:D+(c % 2)*512+512, :]), ["projB"], ["qinD"])
    stage("i_", emit_LI, {"qin": I["qinD"], "kvin": I["projkv"]})
    with p.scope("c8_"):
        y1B = p.coll(AG, ALU.bypass, "y1c", "y1B", [NCORE * 512, LSEQ], F32, RG8)
        y1own = p.dram("y1ownD", [D, TSH], F32)
        p.swdma(lambda e, c: e.dma_start(out=y1own[0:512, :],
                in_=y1B[(2*(c//2))*512:(2*(c//2))*512+512, (c % 2)*TSH:(c % 2)*TSH+TSH]),
                ["y1B"], ["y1ownD"])
        p.swdma(lambda e, c: e.dma_start(out=y1own[512:D, :],
                in_=y1B[(2*(c//2)+1)*512:(2*(c//2)+1)*512+512, (c % 2)*TSH:(c % 2)*TSH+TSH]),
                ["y1B"], ["y1ownD"])
    stage("j_", emit_LJ, {"y1own": I["y1ownD"]})
    with p.scope("c9_"):
        p.coll(AG, ALU.bypass, "hmaxc", "hmax8F", [NCORE, 1], F32, RG8)
    stage("k2_", emit_LK2, {"h_own": I["h_fmb"], "hmax8": I["hmax8F"],
                            "s1own": I["s1_fm"], "x1own": I["x1_tm"]})

    ins = []
    for c in range(NCORE):
        ins.append(dict(
            x=X2[c*TSH:(c+1)*TSH],
            iden=C["iden"], idenf=C["idenf"], cmask=C["cmask"], utri=C["utri"],
            cosq=C["cosq"], sinq=C["sinq"], cosk=C["cosk"], sink=C["sink"],
            wq=ar(Wqkv[:, 2*c*64:2*c*64+128]),
            wk=ar(Wqkv[:, D+c*128:D+(c+1)*128]),
            wv=ar(Wqkv[:, 2*D+c*128:2*D+(c+1)*128]),
            wltm=ar(WL[2*c:2*c+2]),
            vgs=ar(pars["V_gs"][2*c:2*c+2]), vt0=ar(pars["V_T0"][2*c:2*c+2]),
            btau=ar(pars["beta_tau"][2*c:2*c+2]), bgm=ar(pars["beta_gm"][2*c:2*c+2]),
            cch=ar(pars["C_ch"][2*c:2*c+2]), gam=ar(pars["gamma"][2*c:2*c+2]),
            alphap=pars["alpha_ppd"], icthr=pars["IC_threshold"],
            gate=ar(pars["gate"][2*c:2*c+2]),
            W_o=Wo0, Wr=Wr_, br=br_,
            w1e=ar(W1m[c]), b1e=ar(b1m[c]), w2e=ar(W2m[c]), b2e=ar(b2m[c]),
            gamma0=gam0, beta0=bet0,
            LHw=Wq1, LHb=bq1, U=U1, LJw=Wo1,
            L1w1=w1f_, L1b1=b1f_, L1w2=w2f_, L1b2=b2f_,
            cbase=np.array([[float(CAP * c)]], np.float32),
        ))
    res = p.run(ins)
    out = np.concatenate([r["out_tm"] for r in res], 0)
    return out.reshape(B, LSEQ, D)


# version 24 (final)
# speedup vs baseline: 1.2121x; 1.0216x over previous
import numpy as np
from contextlib import ExitStack, contextmanager

import concourse.bass as bass
import concourse.mybir as mybir
from concourse.bass_utils import run_bass_kernel_spmd

F32 = mybir.dt.float32
BF16 = mybir.dt.bfloat16
I32 = mybir.dt.int32
AF = mybir.ActivationFunctionType
ALU = mybir.AluOpType
AX = mybir.AxisListType
IOoA = bass.IndirectOffsetOnAxis

NCORE = 8
NTOK = 16384
D = 1024
H, DK = 16, 64
TSH = NTOK // NCORE
RBIG = 12582912.0
EXP, CAP = 8, 2560
NSLOT = 2 * NTOK
TCH = 128
NCH = 4096 // TCH
LSEQ = 4096
B = 4


ENGINES = ("v", "s", "t", "g", "q")


class P:
    def __init__(self, name="prog", fused=False):
        self.nc = bass.Bass("TRN2", target_bir_lowering=False)
        self.ctx = ExitStack()
        self.ops = []
        self.inputs = {}
        self.outputs = {}
        self._name_i = 0
        self.fused = fused
        self.wire = {}
        self.internals = {}
        self.tensors = {}
        self.prefix = ""
        self._scopes = [self.ctx]
        self.final_outputs = set()

    def din(self, name, shape, dt=F32):
        if name in self.wire:
            return self.wire[name]
        if name in self.internals:
            return self.internals[name]
        if name in self.inputs:
            return self.inputs[name]
        ap = self.nc.dram_tensor(name, list(shape), dt, kind="ExternalInput").ap()
        self.inputs[name] = ap
        return ap

    def dout(self, name, shape, dt=F32):
        if name in self.wire:
            return self.wire[name]
        if self.fused and name not in self.final_outputs:
            return self.dram(name, shape, dt)
        ap = self.nc.dram_tensor(name, list(shape), dt, kind="ExternalOutput").ap()
        self.outputs[name] = ap
        return ap

    def dram(self, name, shape, dt=F32):
        if name in self.internals:
            return self.internals[name]
        t = self.nc.dram_tensor(name, list(shape), dt)
        self.tensors[name] = t
        self.internals[name] = t.ap()
        return self.internals[name]

    def sb(self, name, shape, dt=F32):
        return self._scopes[-1].enter_context(
            self.nc.sbuf_tensor(self.prefix + name, list(shape), dt))

    def ps(self, name, shape, dt=F32):
        return self._scopes[-1].enter_context(
            self.nc.psum_tensor(self.prefix + name, list(shape), dt))

    @contextmanager
    def scope(self, prefix):
        st = ExitStack()
        self._scopes.append(st)
        old = self.prefix
        self.prefix = prefix
        try:
            yield
        finally:
            self.prefix = old
            self._scopes.pop()
            self.barrier()
            st.close()

    def barrier(self):
        self.ops.append(("B", None, (), (), "b"))

    def swdma(self, mk, reads, writes):
        def fn(e, sem):
            for arm in e.switch_core_id(n=128):
                mk(e, arm.logical % NCORE).then_inc(sem, 16)
        pf = self.prefix
        self.ops.append(("g", fn, tuple(pf + b for b in reads),
                         tuple(pf + b for b in writes), "w"))

    def coll(self, kind, alu, in_name, out_name, out_shape, dt, groups=None):
        groups = groups if groups is not None else [list(range(NCORE))]
        tin = self.tensors[in_name]
        if out_name not in self.tensors:
            sp = ("Shared" if kind in ("AllGather", "AllReduce")
                  and len(groups[0]) > 4 else "Local")
            t = self.nc.dram_tensor(out_name, list(out_shape), dt, addr_space=sp)
            self.tensors[out_name] = t
            self.internals[out_name] = t.ap()
        tout = self.tensors[out_name]
        self.op("g", lambda e: e.collective_compute(
            kind, alu, replica_groups=groups,
            ins=[tin.ap().opt()], outs=[tout.ap().opt()]),
            [in_name], [out_name])
        return self.internals[out_name]

    def op(self, eng, fn, reads, writes, kind="c"):
        pf = self.prefix
        self.ops.append((eng, fn, tuple(pf + b for b in reads),
                         tuple(pf + b for b in writes), kind))

    def dma(self, eng, out, in_, reads, writes, **kw):
        self.op(eng, lambda e: e.dma_start(out=out, in_=in_, **kw), reads, writes, "d")

    def emit(self):
        nc = self.nc
        sems = {}
        for e in ENGINES:
            sems[e + "c"] = self.ctx.enter_context(nc.semaphore(e + "c"))
            sems[e + "d"] = self.ctx.enter_context(nc.semaphore(e + "d"))
        cnt = {k: 0 for k in sems}
        last_w = {}
        readers = {}
        waits_per_op = []
        final_out_waits = {}
        cur_bar = {}
        for (eng, fn, reads, writes, kind) in self.ops:
            if kind == "b":
                cur_bar = {k: v for k, v in cnt.items() if v > 0}
                waits_per_op.append(None)
                continue
            semkey = eng + ("d" if kind in ("d", "w") else "c")
            amt = 16 if kind in ("d", "w") else 1
            waits = {}

            def need(sk, c):
                if c > 0 and not (sk == semkey and kind == "p"):
                    waits[sk] = max(waits.get(sk, 0), c)
            for sk, c in cur_bar.items():
                need(sk, c)
            for b in reads:
                if b in last_w:
                    need(*last_w[b])
            for b in writes:
                if b in last_w:
                    need(*last_w[b])
                for sk, c in readers.get(b, {}).items():
                    need(sk, c)
            cnt[semkey] += amt
            waits_per_op.append(waits)
            for b in reads:
                readers.setdefault(b, {})[semkey] = cnt[semkey]
            for b in writes:
                last_w[b] = (semkey, cnt[semkey])
                readers[b] = {}
        totals = dict(cnt)

        block = self.ctx.enter_context(nc.Block())
        per_eng = {e: [] for e in ENGINES}
        for i, (eng, fn, reads, writes, kind) in enumerate(self.ops):
            if kind == "b":
                continue
            per_eng[eng].append((i, fn, kind, waits_per_op[i]))
        emitted_waits = {e: {} for e in ENGINES}

        def mk_section(eng):
            ops = per_eng[eng]

            def section(handle):
                ew = emitted_waits[eng]
                run = {k: 0 for k in sems}
                for (i, fn, kind, waits) in ops:
                    for sk, c in waits.items():
                        if sk == eng + "c" and kind != "d":
                            pass
                        if ew.get(sk, 0) >= c:
                            continue
                        handle.wait_ge(sems[sk], c)
                        ew[sk] = c
                    semkey = eng + ("d" if kind in ("d", "w") else "c")
                    if kind == "w":
                        fn(handle, sems[semkey])
                    else:
                        inst = fn(handle)
                        inst.then_inc(sems[semkey], 16 if kind == "d" else 1)
                    ew[semkey] = max(ew.get(semkey, 0), 0)
            return section

        for eng in ENGINES:
            if per_eng[eng]:
                getattr(block, {"v": "vector", "s": "scalar", "t": "tensor",
                                "g": "gpsimd", "q": "sync"}[eng])(mk_section(eng))
        def fin(handle):
            for sk, tot in totals.items():
                if tot > 0 and not (sk == "qc"):
                    handle.wait_ge(sems[sk], tot)
        block.sync(fin)
        return nc

    def run(self, in_maps, **kw):
        nc = self.emit()
        res = run_bass_kernel_spmd(nc, in_maps, core_ids=list(range(len(in_maps))), **kw)
        self.ctx.close()
        return res.results


def emit_LA(p):
    x = p.din("x", [TSH, D])
    iden = p.din("iden", [128, 128], BF16)
    xn_tm = p.dout("xn_tm", [TSH, D])
    xn_tmb = p.dout("xn_tmb", [TSH, D], BF16)
    xn_fmb = p.dout("xn_fmb", [D, TSH], BF16)
    NT = TSH // 128
    sid = p.sb("sid", [128, 128], BF16)
    p.dma("q", sid[:], iden, ["iden"], ["sid"])
    bufs = []
    for k in range(2):
        bufs.append(dict(
            xt=p.sb(f"xt{k}", [128, D]), xm=p.sb(f"xm{k}", [128, D]),
            sq=p.sb(f"sq{k}", [128, D]), xnb=p.sb(f"xnb{k}", [128, D], BF16),
            mc=p.sb(f"mc{k}", [128, 1]), vc=p.sb(f"vc{k}", [128, 1]),
            trb=p.sb(f"trb{k}", [128, D], BF16), ptr=p.ps(f"ptr{k}", [128, 128], BF16)))
    for i in range(NT):
        b = bufs[i % 2]
        ki = f"_{i%2}"
        xt, xm, sq, xnb, mc, vc, trb, ptr = (b[n] for n in ("xt", "xm", "sq", "xnb", "mc", "vc", "trb", "ptr"))
        p.dma("q", xt[:], x[i*128:(i+1)*128, :], ["x"], ["xt"+ki])
        p.op("v", lambda e, xt=xt, mc=mc: e.reduce_sum(mc[:], xt[:], AX.X), ["xt"+ki], ["mc"+ki])
        p.op("v", lambda e, mc=mc: e.tensor_scalar(mc[:], mc[:], 1.0/D, None, ALU.mult), ["mc"+ki], ["mc"+ki])
        p.op("v", lambda e, xt=xt, xm=xm, mc=mc: e.tensor_scalar(xm[:], xt[:], mc[:], None, ALU.subtract), ["xt"+ki, "mc"+ki], ["xm"+ki])
        p.op("s", lambda e, xm=xm, sq=sq, vc=vc: e.activation(sq[:], xm[:], AF.Square, accum_out=vc[:]), ["xm"+ki], ["sq"+ki, "vc"+ki])
        p.op("v", lambda e, vc=vc: e.tensor_scalar(vc[:], vc[:], 1.0/D, 1e-5, ALU.mult, ALU.add), ["vc"+ki], ["vc"+ki])
        p.op("s", lambda e, vc=vc: e.activation(vc[:], vc[:], AF.Sqrt), ["vc"+ki], ["vc"+ki])
        p.op("v", lambda e, vc=vc: e.reciprocal(vc[:], vc[:]), ["vc"+ki], ["vc"+ki])
        p.op("v", lambda e, xm=xm, vc=vc: e.tensor_scalar(xm[:], xm[:], vc[:], None, ALU.mult), ["xm"+ki, "vc"+ki], ["xm"+ki])
        p.op("v", lambda e, xm=xm, xnb=xnb: e.tensor_copy(xnb[:], xm[:]), ["xm"+ki], ["xnb"+ki])
        p.dma("q", xn_tm[i*128:(i+1)*128, :], xm[:], ["xm"+ki], ["xn_tm"])
        p.dma("q", xn_tmb[i*128:(i+1)*128, :], xnb[:], ["xnb"+ki], ["xn_tmb"])
        for f in range(8):
            p.op("t", lambda e, xnb=xnb, ptr=ptr, f=f: e.transpose(ptr[:], xnb[:, f*128:(f+1)*128], sid[:]),
                 ["xnb"+ki, "sid"], ["ptr"+ki])
            p.op("v", lambda e, ptr=ptr, trb=trb, f=f: e.tensor_copy(trb[:, f*128:(f+1)*128], ptr[:]),
                 ["ptr"+ki], ["trb"+ki])
        p.dma("g", xn_fmb.rearrange("(f p) n -> p f n", p=128)[:, :, i*128:(i+1)*128],
              trb.rearrange("p (f n) -> p f n", f=8), ["trb"+ki], ["xn_fmb"])


def emit_LB1(p):
    xn8 = p.din("xn_fmb8", [NCORE * D, TSH], BF16)
    xnv = xn8.rearrange("(blk t pp) n -> pp blk t n", blk=NCORE, t=8)
    wq = p.din("wq", [D, 128])
    wk = p.din("wk", [D, 128])
    wv = p.din("wv", [D, 128])
    iden = p.din("iden", [128, 128], BF16)
    q_fmb = p.dout("q_fmb", [128, NTOK], BF16)
    k_fmb = p.dout("k_fmb", [128, NTOK], BF16)
    kn_fmb = p.dout("kn_fmb", [128, NTOK], BF16)
    v_fmb = p.dout("v_fmb", [128, NTOK], BF16)
    qT = p.dout("qT", [2, NTOK, 64], BF16)
    knT = p.dout("knT", [2, NTOK, 64], BF16)
    vTf = p.dout("vTf", [2, NTOK, 64])
    vTb = p.dout("vTb", [2, NTOK, 64], BF16)

    idenf = p.din("idenf", [128, 128])
    sid = p.sb("sid", [128, 128], BF16)
    sidf = p.sb("sidf", [128, 128])
    p.dma("q", sid[:], iden, ["iden"], ["sid"])
    p.dma("q", sidf[:], idenf, ["idenf"], ["sidf"])
    wbf = {}
    for nm, ap in (("wq", wq), ("wk", wk), ("wv", wv)):
        wf = p.sb(nm + "f", [128, 8 * 128])
        wb = p.sb(nm + "b", [128, 8 * 128], BF16)
        p.dma("q", wf.rearrange("p (t c) -> p t c", t=8), ap.rearrange("(t p) c -> p t c", p=128),
              [nm], [nm + "f"])
        p.op("v", lambda e, wf=wf, wb=wb: e.tensor_copy(wb[:], wf[:]), [nm + "f"], [nm + "b"])
        wbf[nm] = wb

    xnb = p.sb("xnb", [128, 8 * LSEQ], BF16)
    pp = [p.ps(f"pp{i}", [128, 512]) for i in range(2)]
    pev = p.sb("pev", [128, 512])
    pevb = p.sb("pevb", [128, 512], BF16)
    ptT = p.ps("ptT", [128, 64])
    ptTb = p.ps("ptTb", [128, 64], BF16)
    tT = p.sb("tT", [128, 64])
    tTb = p.sb("tTb", [128, 64], BF16)
    mc = p.sb("mc", [128, 1]); vc = p.sb("vc", [128, 1]); sqs = p.sb("sqs", [128, 64])
    pback = p.ps("pback", [64, 128], BF16)
    knc = p.sb("knc", [128, 512], BF16)

    for b in range(B):
        for hb in range(2):
            p.dma("q", xnb.rearrange("p (t n) -> p t n", t=8)[:, :, hb * TSH:(hb + 1) * TSH],
                  xnv[:, 2 * b + hb, :, :], ["xn_fmb8"], ["xnb"])
        for nm, fmout, wantT in (("wq", q_fmb, "qT"), ("wk", k_fmb, "kT"),
                                 ("wv", v_fmb, "vT")):
            wb = wbf[nm]
            for nchunk in range(8):
                n0 = nchunk * 512
                sl = nchunk % 2
                for kt in range(8):
                    p.op("t", lambda e, wb=wb, kt=kt, n0=n0, sl=sl:
                         e.matmul(pp[sl][:], wb[:, kt * 128: (kt + 1) * 128],
                                  xnb[:, kt * LSEQ + n0: kt * LSEQ + n0 + 512],
                                  start=(kt == 0), stop=(kt == 7)),
                         [nm + "b", "xnb"], [f"pp{sl}"], "p" if kt else "c")
                p.op("s", lambda e, sl=sl: e.copy(pev[:], pp[sl][:]), [f"pp{sl}"], ["pev"])
                p.op("v", lambda e: e.tensor_copy(pevb[:], pev[:]), ["pev"], ["pevb"])
                p.dma("q", fmout[:, b * LSEQ + n0: b * LSEQ + n0 + 512],
                      pevb[:], ["pevb"], [nm + "fm"])
                for cc in range(4):
                    tok0 = b * LSEQ + n0 + cc * 128
                    for hh in range(2):
                        if nm == "wq":
                            p.op("t", lambda e, cc=cc, hh=hh: e.transpose(
                                ptTb[:], pevb[hh * 64:(hh + 1) * 64, cc * 128:(cc + 1) * 128],
                                sid[hh * 64:(hh + 1) * 64, hh * 64:(hh + 1) * 64]),
                                 ["pevb", "sid"], ["ptTb"])
                            p.op("v", lambda e: e.tensor_copy(tTb[:], ptTb[:]), ["ptTb"], ["tTb"])
                            p.dma("q", qT[hh, tok0:tok0 + 128, :], tTb[:], ["tTb"], ["qT"])
                        elif nm == "wv":
                            p.op("t", lambda e, cc=cc, hh=hh: e.transpose(
                                ptTb[:], pevb[hh * 64:(hh + 1) * 64, cc * 128:(cc + 1) * 128],
                                sid[hh * 64:(hh + 1) * 64, hh * 64:(hh + 1) * 64]),
                                 ["pevb", "sid"], ["ptTb"])
                            p.op("v", lambda e: e.tensor_copy(tTb[:], ptTb[:]), ["ptTb"], ["tTb"])
                            p.op("v", lambda e: e.tensor_copy(tT[:], ptTb[:]), ["ptTb"], ["tT"])
                            p.dma("q", vTb[hh, tok0:tok0 + 128, :], tTb[:], ["tTb"], ["vTb"])
                            p.dma("g", vTf[hh, tok0:tok0 + 128, :], tT[:], ["tT"], ["vTf"])
                        else:
                            p.op("t", lambda e, cc=cc, hh=hh: e.transpose(
                                ptT[:], pev[hh * 64:(hh + 1) * 64, cc * 128:(cc + 1) * 128],
                                sidf[hh * 64:(hh + 1) * 64, hh * 64:(hh + 1) * 64]),
                                 ["pev", "sidf"], ["ptT"])
                            p.op("v", lambda e: e.tensor_copy(tT[:], ptT[:]), ["ptT"], ["tT"])
                            p.op("v", lambda e: e.reduce_sum(mc[:], tT[:], AX.X), ["tT"], ["mc"])
                            p.op("v", lambda e: e.tensor_scalar(mc[:], mc[:], 1.0 / 64, None, ALU.mult), ["mc"], ["mc"])
                            p.op("v", lambda e: e.tensor_scalar(tT[:], tT[:], mc[:], None, ALU.subtract), ["tT", "mc"], ["tT"])
                            p.op("s", lambda e: e.activation(sqs[:], tT[:], AF.Square, accum_out=vc[:]), ["tT"], ["sqs", "vc"])
                            p.op("v", lambda e: e.tensor_scalar(vc[:], vc[:], 1.0 / 64, 1e-5, ALU.mult, ALU.add), ["vc"], ["vc"])
                            p.op("s", lambda e: e.activation(vc[:], vc[:], AF.Sqrt), ["vc"], ["vc"])
                            p.op("v", lambda e: e.reciprocal(vc[:], vc[:]), ["vc"], ["vc"])
                            p.op("v", lambda e: e.tensor_scalar(tT[:], tT[:], vc[:], None, ALU.mult), ["tT", "vc"], ["tT"])
                            p.op("v", lambda e: e.tensor_copy(tTb[:], tT[:]), ["tT"], ["tTb"])
                            p.dma("q", knT[hh, tok0:tok0 + 128, :], tTb[:], ["tTb"], ["knT"])
                            p.op("t", lambda e: e.transpose(pback[:], tTb[:], sid[:]), ["tTb", "sid"], ["pback"])
                            p.op("v", lambda e, cc=cc, hh=hh: e.tensor_copy(
                                knc[hh * 64:(hh + 1) * 64, cc * 128:(cc + 1) * 128], pback[:]),
                                 ["pback"], ["knc"])
                if nm == "wk":
                    p.dma("g", kn_fmb[:, b * LSEQ + n0: b * LSEQ + n0 + 512],
                          knc[:], ["knc"], ["kn_fmb"])


def emit_LB2(p):
    qf = p.din("q_fmb", [128, NTOK], BF16)
    kf = p.din("k_fmb", [128, NTOK], BF16)
    vtb = p.din("vTb", [2, NTOK, 64], BF16)
    cmask = p.din("cmask", [4, 128, 512], BF16)
    ysf = p.dout("ysoft_fm", [128, NTOK], BF16)
    qs = p.sb("qs", [64, LSEQ], BF16)
    ks = p.sb("ks", [64, LSEQ], BF16)
    vte = p.sb("vte", [128, 32 * 65], BF16)
    msk = p.sb("msk", [4, 128 * 512 // 128], BF16)
    cm = p.sb("cm", [128, 4 * 512], BF16)
    ones1 = p.sb("ones1", [1, 64])
    rs = p.sb("rs", [1, 512])
    rsb = p.sb("rsb", [64, 512])
    yo = p.sb("yo", [64, 512])
    pS = [p.ps(f"pS{i}", [128, 512]) for i in range(2)]
    pt = [p.sb(f"pt{i}", [128, 512], BF16) for i in range(2)]
    pY = p.ps("pY", [65, 512])
    pR = p.ps("pR", [64, 512])
    p.op("v", lambda e: e.memset(ones1[:], 1.0), [], ["ones1"])
    p.dma("q", cm.rearrange("p (m n) -> p m n", m=4), cmask.rearrange("m p n -> p m n"), ["cmask"], ["cm"])
    for hh in range(2):
        for b in range(B):
            t0 = b * LSEQ
            p.dma("q", qs[:], qf[hh * 64:(hh + 1) * 64, t0:t0 + LSEQ], ["q_fmb"], ["qs"])
            p.dma("q", ks[:], kf[hh * 64:(hh + 1) * 64, t0:t0 + LSEQ], ["k_fmb"], ["ks"])
            p.dma("g", vte.rearrange("p (k c) -> p k c", k=32)[:, :, 0:64],
                  vtb[hh, t0:t0 + LSEQ, :].rearrange("(k p) c -> p k c", p=128),
                  ["vTb"], ["vte"])
            p.op("v", lambda e: e.memset(vte.rearrange("p (k c) -> p k c", k=32)[:, :, 64:65], 1.0),
                 ["vte"], ["vte"])
            for qc in range(8):
                nkt = 4 * qc + 4

                def emit_S(kt, qc=qc):
                    i = kt % 2
                    p.op("t", lambda e, kt=kt, qc=qc, i=i:
                         e.matmul(pS[i][:], ks[:, kt * 128:(kt + 1) * 128],
                                  qs[:, qc * 512:(qc + 1) * 512], start=True, stop=True),
                         ["ks", "qs"], [f"pS{i}"])
                    p.op("s", lambda e, i=i: e.activation(pt[i][:], pS[i][:], AF.Exp, scale=0.125),
                         [f"pS{i}"], [f"pt{i}"])
                    if kt >= 4 * qc:
                        moff = kt - 4 * qc
                        p.op("v", lambda e, i=i, moff=moff:
                             e.tensor_mul(pt[i][:], pt[i][:], cm[:, moff * 512:(moff + 1) * 512]),
                             [f"pt{i}", "cm"], [f"pt{i}"])

                def emit_PV(kt, nkt=nkt):
                    i = kt % 2
                    p.op("t", lambda e, kt=kt, i=i, nkt=nkt:
                         e.matmul(pY[:], vte[:, kt * 65:(kt + 1) * 65], pt[i][:],
                                  start=(kt == 0), stop=(kt == nkt - 1)),
                         ["vte", f"pt{i}"], ["pY"], "p" if kt else "c")

                emit_S(0)
                for kt in range(1, nkt):
                    emit_S(kt)
                    emit_PV(kt - 1)
                emit_PV(nkt - 1)
                p.op("v", lambda e: e.reciprocal(rs[:], pY[64:65, :]), ["pY"], ["rs"])
                p.op("t", lambda e: e.matmul(pR[:], ones1[:], rs[:], start=True, stop=True),
                     ["ones1", "rs"], ["pR"])
                p.op("s", lambda e: e.copy(rsb[:], pR[:]), ["pR"], ["rsb"])
                p.op("v", lambda e: e.tensor_mul(yo[:], pY[0:64, :], rsb[:]), ["pY", "rsb"], ["yo"])
                p.dma("g", ysf[hh * 64:(hh + 1) * 64, t0 + qc * 512: t0 + (qc + 1) * 512],
                      yo[:], ["yo"], ["ysf"])


def _emit_tables(p, wltm, vgs, vt0, btau, bgm, cch, gam, alphap, icthr, idenf):
    out = {}
    sidf = p.sb("sidf_t", [64, 64])
    p.dma("q", sidf[:], idenf[0:64, 0:64], ["idenf"], ["sidf_t"])
    wl = p.sb("wl", [64, 2 * 64])
    p.dma("q", wl.rearrange("p (h e) -> p h e", h=2), wltm.rearrange("h d e -> d h e"),
          ["wltm"], ["wl"])
    cols = {}
    for nm, ap in (("vgs", vgs), ("vt0", vt0), ("btau", btau), ("bgm", bgm),
                   ("cch", cch), ("gam", gam)):
        t = p.sb("col_" + nm, [64, 2])
        p.dma("q", t[:, 0:1], ap[0:1].unsqueeze(0).to_broadcast([64, 1]), [nm], ["col_" + nm])
        p.dma("q", t[:, 1:2], ap[1:2].unsqueeze(0).to_broadcast([64, 1]), [nm], ["col_" + nm])
        cols[nm] = t
    alc = p.sb("col_al", [64, 1])
    icc = p.sb("col_ic", [64, 1])
    p.dma("q", alc[:], alphap[0:1].unsqueeze(0).to_broadcast([64, 1]), ["alphap"], ["col_al"])
    p.dma("q", icc[:], icthr[0:1].unsqueeze(0).to_broadcast([64, 1]), ["icthr"], ["col_ic"])
    r = p.sb("r_t", [64, 2 * 64])
    c = p.sb("c_t", [64, 2 * 64])
    t1 = p.sb("t1_t", [64, 64]); t2 = p.sb("t2_t", [64, 64]); t3 = p.sb("t3_t", [64, 64])
    t4 = p.sb("t4_t", [64, 64]); t5 = p.sb("t5_t", [64, 64])
    rcp = p.sb("rcp_t", [64, 1])
    for h in range(2):
        wlh = wl[:, h * 64:(h + 1) * 64]
        hh_ = h
        def V(fn, reads, writes):
            p.op("v", fn, reads, writes)
        def S(fn, reads, writes):
            p.op("s", fn, reads, writes)
        V(lambda e, wlh=wlh, h=hh_: e.tensor_scalar(t1[:], wlh, cols["vgs"][:, h:h+1], cols["vt0"][:, h:h+1], ALU.add, ALU.subtract), ["wl", "col_vgs", "col_vt0"], ["t1"])
        S(lambda e: e.activation(t2[:], t1[:], AF.Abs), ["t1"], ["t2"])
        S(lambda e: e.activation(t2[:], t2[:], AF.Exp, scale=-1.0), ["t2"], ["t2"])
        V(lambda e: e.tensor_scalar(t2[:], t2[:], 1.0, None, ALU.add), ["t2"], ["t2"])
        S(lambda e: e.activation(t2[:], t2[:], AF.Ln), ["t2"], ["t2"])
        S(lambda e: e.activation(t3[:], t1[:], AF.Relu), ["t1"], ["t3"])
        V(lambda e: e.tensor_add(t2[:], t2[:], t3[:]), ["t2", "t3"], ["t2"])
        V(lambda e, h=hh_: e.tensor_scalar(t3[:], t2[:], cols["btau"][:, h:h+1], None, ALU.mult), ["t2", "col_btau"], ["t3"])
        V(lambda e, h=hh_: e.reciprocal(rcp[:], cols["cch"][:, h:h+1]), ["col_cch"], ["rcp"])
        V(lambda e: e.tensor_scalar(t5[:], t3[:], rcp[:], None, ALU.mult), ["t3", "rcp"], ["t5"])
        S(lambda e: e.activation(t5[:], t5[:], AF.Exp, scale=-1.0), ["t5"], ["t5"])
        V(lambda e, h=hh_: e.tensor_copy(r[:, h*64:(h+1)*64], t5[:]), ["t5"], ["r_t"])
        S(lambda e: e.activation(t4[:], t1[:], AF.Sigmoid), ["t1"], ["t4"])
        V(lambda e: e.tensor_mul(t2[:], t2[:], t4[:]), ["t2", "t4"], ["t2"])
        V(lambda e, h=hh_: e.tensor_scalar(t2[:], t2[:], cols["bgm"][:, h:h+1], None, ALU.mult), ["t2", "col_bgm"], ["t2"])
        V(lambda e: e.tensor_scalar(t3[:], t3[:], icc[:], None, ALU.subtract), ["t3", "col_ic"], ["t3"])
        V(lambda e: e.tensor_scalar(t3[:], t3[:], alc[:], None, ALU.mult), ["t3", "col_al"], ["t3"])
        S(lambda e: e.activation(t3[:], t3[:], AF.Tanh), ["t3"], ["t3"])
        V(lambda e: e.tensor_mul(t2[:], t2[:], t3[:]), ["t2", "t3"], ["t2"])
        V(lambda e, h=hh_: e.tensor_scalar(c[:, h*64:(h+1)*64], t2[:], cols["gam"][:, h:h+1], None, ALU.mult), ["t2", "col_gam"], ["c_t"])
    out["r"] = r; out["c"] = c; out["sidf"] = sidf; out["cols"] = cols
    return out


def emit_LB3a(p):
    knT = p.din("knT", [2, NTOK, 64], BF16)
    vTf = p.din("vTf", [2, NTOK, 64])
    wltm = p.din("wltm", [2, 64, 64])
    vgs = p.din("vgs", [2]); vt0 = p.din("vt0", [2]); btau = p.din("btau", [2])
    bgm = p.din("bgm", [2]); cch = p.din("cch", [2]); gam = p.din("gam", [2])
    alphap = p.din("alphap", [1]); icthr = p.din("icthr", [1])
    idenf = p.din("idenf", [128, 128])
    iden = p.din("iden", [128, 128], BF16)
    FbH = p.dout("FbH", [2, B, NCH, 4096], BF16)
    rpb = p.dout("rpb", [2, 64, 128 * 64], BF16)
    ccb = p.dout("ccb", [2, 64, 64])
    sstH = p.dout("sstH", [2, B, NCH, 4096], BF16)

    tb = _emit_tables(p, wltm, vgs, vt0, btau, bgm, cch, gam, alphap, icthr, idenf)
    r, c = tb["r"], tb["c"]
    sid = p.sb("sid", [128, 128], BF16)
    p.dma("q", sid[:], iden, ["iden"], ["sid"])
    PS = p.sb("PS", [64, 128 * 64])
    PSb = p.sb("PSb", [64, 128 * 64], BF16)
    RcPb = p.sb("RcPb", [64, 128 * 64], BF16)
    r128T = p.sb("r128T", [64, 2 * 64])
    pr128 = p.ps("pr128", [64, 64])
    rcb_hbm = p.dout("rcb", [2, 64, 128 * 64], BF16)
    for h in range(2):
        rh = r[:, h * 64:(h + 1) * 64]
        ch = c[:, h * 64:(h + 1) * 64]
        p.op("v", lambda e, rh=rh: e.tensor_copy(PS[:, 0:64], rh), ["r_t"], ["PS"])
        for j in range(1, 128):
            p.op("v", lambda e, j=j, rh=rh: e.tensor_mul(PS[:, j*64:(j+1)*64], PS[:, (j-1)*64:j*64], rh),
                 ["PS", "r_t"], ["PS"])
        p.op("v", lambda e: e.tensor_copy(PSb[:], PS[:]), ["PS"], ["PSb"])
        p.dma("q", rpb[h], PSb[:], ["PSb"], ["rpb"])
        p.dma("q", ccb[h], ch, ["c_t"], ["ccb"])
        for j in range(128):
            if j == 127:
                p.op("v", lambda e, ch=ch: e.tensor_copy(RcPb[:, 127*64:128*64], ch), ["c_t"], ["RcPb"])
            else:
                p.op("v", lambda e, j=j, ch=ch: e.tensor_mul(RcPb[:, j*64:(j+1)*64], PS[:, (126-j)*64:(127-j)*64], ch),
                     ["PS", "c_t"], ["RcPb"])
        p.dma("q", rcb_hbm[h], RcPb[:], ["RcPb"], ["rcb"])
        p.op("t", lambda e: e.transpose(pr128[:], PS[:, 127*64:128*64], tb["sidf"][:]),
             ["PS", "sidf_t"], ["pr128"])
        p.op("v", lambda e, h=h: e.tensor_copy(r128T[:, h*64:(h+1)*64], pr128[:]), ["pr128"], ["r128T"])
    rrep = p.sb("rrep", [128, 2 * 32 * 32])
    for h in range(2):
        for dt in range(32):
            for half in range(2):
                d = 2 * dt + half
                p.op("v", lambda e, h=h, dt=dt, half=half, d=d:
                     e.tensor_copy(rrep[half*64:(half+1)*64, (h*32+dt)*32:(h*32+dt+1)*32],
                                   r128T[:, h*64+d:h*64+d+1].to_broadcast([64, 32])),
                     ["r128T"], ["rrep"])
    Rcs = p.sb("Rcs", [128, 2 * 64 * 64], BF16)
    p.dma("q", Rcs.rearrange("p (h d e) -> p h d e", h=2, d=64),
          rcb_hbm.rearrange("h d (j e) -> j h d e", e=64), ["rcb"], ["Rcs"])
    ones128 = p.sb("ones128", [128, 1], BF16)
    p.op("v", lambda e: e.memset(ones128[:], 1.0), [], ["ones128"])
    knTs = p.sb("knTs", [128, 32 * 64], BF16)
    vTs = p.sb("vTs", [128, 32 * 64])
    kv1 = p.sb("kv1", [128, 2048], BF16)
    kv2 = [p.sb(f"kv2{i}", [128, 2048], BF16) for i in range(2)]
    psA = p.ps("psA", [128, 1024])
    ptb = p.ps("ptb", [128, 128], BF16)
    srow = p.sb("srow", [1, 4096], BF16)
    sstS = p.sb("sstS", [32, 4096], BF16)
    Stile = p.sb("Stile", [128, 32])
    Fall = p.sb("Fall", [128, 32])
    FallB = p.sb("FallB", [128, 32], BF16)
    fT = p.sb("fT", [32, 128], BF16)
    for hh in range(2):
        for b in range(B):
            t0 = b * LSEQ
            p.dma("q", knTs.rearrange("p (m e) -> p m e", m=32),
                  knT[hh, t0:t0 + LSEQ, :].rearrange("(m p) e -> p m e", p=128),
                  ["knT"], ["knTs"])
            p.dma("g", vTs.rearrange("p (m e) -> p m e", m=32),
                  vTf[hh, t0:t0 + LSEQ, :].rearrange("(m p) e -> p m e", p=128),
                  ["vTf"], ["vTs"])
            for m in range(NCH):
                for half in range(2):
                    d0 = half * 32
                    p.op("v", lambda e, m=m, hh=hh, d0=d0: e.tensor_tensor(
                        kv1[:].rearrange("p (a b) -> p a b", a=32),
                        knTs[:, m*64:(m+1)*64].unsqueeze(1).to_broadcast([128, 32, 64]),
                        Rcs[:, (hh*64+d0)*64:(hh*64+d0+32)*64].rearrange("p (a b) -> p a b", a=32),
                        ALU.mult), ["knTs", "Rcs"], ["kv1"])
                    i2 = half
                    p.op("v", lambda e, m=m, d0=d0, i2=i2: e.tensor_tensor(
                        kv2[i2][:].rearrange("p (a b) -> p a b", a=32),
                        kv1[:].rearrange("p (a b) -> p a b", a=32),
                        vTs[:, m*64+d0:m*64+d0+32].unsqueeze(2).to_broadcast([128, 32, 64]),
                        ALU.mult), ["kv1", "vTs"], [f"kv2{i2}"])
                    for dgrp in range(4):
                        p.op("t", lambda e, dgrp=dgrp, i2=i2: e.matmul(
                            psA[0:1, (dgrp % 2)*512:(dgrp % 2 + 1)*512],
                            ones128[:], kv2[i2][:, dgrp*512:(dgrp+1)*512], start=True, stop=True),
                             ["ones128", f"kv2{i2}"], ["psW2"])
                        if dgrp % 2 == 1:
                            p.op("s", lambda e, half=half, dgrp=dgrp: e.copy(
                                srow[:, half*2048+(dgrp//2)*1024:half*2048+(dgrp//2)*1024+1024],
                                psA[0:1, :]), ["psW2"], ["srow"])
                p.dma("q", sstS[m:m+1, :], srow[:], ["srow"], ["sstS"])
            for dt in range(32):
                p.op("t", lambda e, dt=dt: e.transpose(ptb[0:128, 0:32], sstS[:, dt*128:(dt+1)*128], sid[0:32, 0:32]),
                     ["sstS", "sid"], ["ptb"])
                p.op("v", lambda e: e.tensor_copy(Stile[:], ptb[0:128, 0:32]), ["ptb"], ["Stile"])
                p.op("v", lambda e, hh=hh, dt=dt:
                     e.tensor_tensor_scan(Fall[:], rrep[:, (hh*32+dt)*32:(hh*32+dt+1)*32], Stile[:],
                                          0.0, ALU.mult, ALU.add),
                     ["rrep", "Stile"], ["Fall"])
                p.op("v", lambda e: e.tensor_copy(FallB[:], Fall[:]), ["Fall"], ["FallB"])
                p.op("t", lambda e: e.transpose(ptb[0:32, 0:128], FallB[:], sid[:]),
                     ["FallB", "sid"], ["ptb2"])
                p.op("v", lambda e: e.tensor_copy(fT[:], ptb[0:32, 0:128]), ["ptb2"], ["fT"])
                p.dma("q", FbH[hh, b][:, dt*128:(dt+1)*128], fT[:], ["fT"], ["FbH"])


def emit_LB3b(p):
    qfm = p.din("q_fmb", [128, NTOK], BF16)
    knfm = p.din("kn_fmb", [128, NTOK], BF16)
    vfm = p.din("v_fmb", [128, NTOK], BF16)
    qT = p.din("qT", [2, NTOK, 64], BF16)
    FbH = p.din("FbH", [2, B, NCH, 4096], BF16)
    rpb = p.din("rpb", [2, 64, 128 * 64], BF16)
    ccb = p.din("ccb", [2, 64, 64])
    wltm = p.din("wltm", [2, 64, 64])
    gate = p.din("gate", [2])
    ysf = p.din("ysoft_fm", [128, NTOK], BF16)
    idenf = p.din("idenf", [128, 128])
    iden = p.din("iden", [128, 128], BF16)
    y0 = p.dout("y0_fmb", [128, NTOK], BF16)

    sid = p.sb("sid", [128, 128], BF16)
    sidf = p.sb("sidf", [128, 128])
    p.dma("q", sid[:], iden, ["iden"], ["sid"])
    p.dma("q", sidf[:], idenf, ["idenf"], ["sidf"])
    Rp = p.sb("Rp", [128, 2 * 4096], BF16)
    p.dma("q", Rp.rearrange("p (h d e) -> p h d e", h=2, d=64),
          rpb.rearrange("h d (j e) -> j h d e", e=64), ["rpb"], ["Rp"])
    Pblk = p.sb("Pblk", [128, 128 * 128], BF16)
    p.op("v", lambda e: e.memset(Pblk[:], 0.0), [], ["Pblk"])
    rpn = p.sb("rpn", [64, 128 * 64], BF16)
    ccbf = p.sb("ccbf", [64, 2 * 64], BF16)
    pl = p.sb("pl", [64, 64], BF16)
    pt1 = p.ps("pt1", [64, 64], BF16)
    p.dma("g", ccbf.rearrange("p (h e) -> p h e", h=2), ccb.rearrange("h d e -> d h e"),
          ["ccb"], ["ccbf"])
    for h in range(2):
        p.dma("q", rpn[:], rpb[h], ["rpb"], ["rpn"])
        for l in range(128):
            if l == 0:
                p.op("v", lambda e, h=h: e.tensor_copy(pl[:], ccbf[:, h*64:(h+1)*64]), ["ccbf"], ["pl"])
            else:
                p.op("v", lambda e, l=l, h=h: e.tensor_mul(pl[:], rpn[:, (l-1)*64:l*64], ccbf[:, h*64:(h+1)*64]),
                     ["rpn", "ccbf"], ["pl"])
            p.op("t", lambda e: e.transpose(pt1[:], pl[:], sid[0:64, 0:64]), ["pl", "sid"], ["pt1"])
            p.op("v", lambda e, l=l, h=h: e.tensor_copy(Pblk[h*64:(h+1)*64, l*128+h*64:l*128+(h+1)*64], pt1[:]),
                 ["pt1"], ["Pblk"])
    wl = p.sb("wl2", [64, 2 * 64])
    wlb = p.sb("wlb", [64, 64], BF16)
    wlT = p.sb("wlT", [128, 128], BF16)
    ptf = p.ps("ptf", [64, 64])
    p.dma("q", wl.rearrange("p (h e) -> p h e", h=2), wltm.rearrange("h d e -> d h e"),
          ["wltm"], ["wl2"])
    p.op("v", lambda e: e.memset(wlT[:], 0.0), [], ["wlT"])
    for h in range(2):
        p.op("t", lambda e, h=h: e.transpose(ptf[:], wl[:, h*64:(h+1)*64], sidf[0:64, 0:64]),
             ["wl2", "sidf"], ["ptf"])
        p.op("v", lambda e: e.tensor_copy(wlb[:], ptf[:]), ["ptf"], ["wlb"])
        p.op("v", lambda e, h=h: e.tensor_copy(wlT[h*64:(h+1)*64, h*64:(h+1)*64], wlb[:]), ["wlb"], ["wlT"])
    gg = p.sb("gg", [128, 1])
    omg = p.sb("omg", [128, 1])
    p.dma("q", gg[0:64, :], gate[0:1].unsqueeze(0).to_broadcast([64, 1]), ["gate"], ["gg"])
    p.dma("q", gg[64:128, :], gate[1:2].unsqueeze(0).to_broadcast([64, 1]), ["gate"], ["gg"])
    p.op("s", lambda e: e.activation(gg[:], gg[:], AF.Sigmoid), ["gg"], ["gg"])
    p.op("v", lambda e: e.tensor_scalar(omg[:], gg[:], -1.0, 1.0, ALU.mult, ALU.add), ["gg"], ["omg"])

    q2 = p.sb("q2", [128, LSEQ], BF16)
    kn2 = p.sb("kn2", [128, LSEQ], BF16)
    v2 = p.sb("v2", [128, LSEQ], BF16)
    qTs = p.sb("qTs", [128, 2 * 32 * 64], BF16)
    Fb = [p.sb(f"Fb{i}", [128, 4096], BF16) for i in range(3)]
    Psi = p.sb("Psi", [128, 4096], BF16)
    P2 = p.sb("P2", [128, 4096], BF16)
    yfT = p.sb("yfT", [128, 128], BF16)
    yfb = p.sb("yfb", [128, 128])
    y2acc = p.sb("y2acc", [128, LSEQ])
    g2sc = p.sb("g2sc", [128, LSEQ], BF16)
    tmpb = p.sb("tmpb", [128, 512], BF16)
    ys2 = p.sb("ys2", [128, LSEQ], BF16)
    psA = p.ps("psA", [128, 2048])
    ptb = p.ps("ptb", [128, 128], BF16)
    for b in range(B):
        t0 = b * LSEQ
        p.dma("q", q2[:], qfm[:, t0:t0 + LSEQ], ["q_fmb"], ["q2"])
        p.dma("q", kn2[:], knfm[:, t0:t0 + LSEQ], ["kn_fmb"], ["kn2"])
        p.dma("q", v2[:], vfm[:, t0:t0 + LSEQ], ["v_fmb"], ["v2"])
        for hh in range(2):
            p.dma("q", qTs.rearrange("p (h m e) -> p h m e", h=2, m=32)[:, hh],
                  qT[hh, t0:t0 + LSEQ, :].rearrange("(m p) e -> p m e", p=128),
                  ["qT"], ["qTs"])
        for nc_ in range(8):
            p.op("t", lambda e, nc_=nc_: e.matmul(psA[:, (nc_%4)*512:(nc_%4+1)*512], wlT[:],
                                                  q2[:, nc_*512:(nc_+1)*512], start=True, stop=True),
                 ["wlT", "q2"], [f"psW{nc_%4}"])
            p.op("v", lambda e, nc_=nc_: e.tensor_copy(y2acc[:, nc_*512:(nc_+1)*512],
                                                       psA[:, (nc_%4)*512:(nc_%4+1)*512]),
                 [f"psW{nc_%4}"], ["y2acc"])
        for pre in range(2):
            p.dma("g", Fb[pre][:], FbH[pre % 2, b, 0:1, :].to_broadcast([128, 4096]),
                  ["FbH"], [f"Fb{pre}"])
        for m in range(1, NCH):
            for hh in range(2):
                it = 2 * (m - 1) + hh
                if it + 2 < 2 * (NCH - 1):
                    it2 = it + 2
                    m2, hh2 = 1 + it2 // 2, it2 % 2
                    p.dma("g", Fb[it2 % 3][:], FbH[hh2, b, m2-1:m2, :].to_broadcast([128, 4096]),
                          ["FbH"], [f"Fb{it2 % 3}"])
                p.op("v", lambda e, hh=hh, it=it: e.tensor_mul(Psi[:], Rp[:, hh*4096:(hh+1)*4096], Fb[it % 3][:]),
                     ["Rp", f"Fb{it % 3}"], ["Psi"])
                p.op("v", lambda e, hh=hh, m=m: e.tensor_mul(
                    P2.rearrange("p (d e) -> p d e", d=64), Psi.rearrange("p (d e) -> p d e", d=64),
                    qTs[:, (hh*32+m)*64:(hh*32+m+1)*64].unsqueeze(1).to_broadcast([128, 64, 64])),
                     ["Psi", "qTs"], ["P2"])
                for k in (32, 16, 8, 4, 2, 1):
                    p.op("v", lambda e, k=k: e.tensor_add(
                        P2.rearrange("p (d e) -> p d e", d=64)[:, :, 0:k],
                        P2.rearrange("p (d e) -> p d e", d=64)[:, :, 0:k],
                        P2.rearrange("p (d e) -> p d e", d=64)[:, :, k:2*k]),
                         ["P2"], ["P2"])
                p.op("v", lambda e, hh=hh: e.tensor_copy(yfT[:, hh*64:(hh+1)*64],
                                                         P2.rearrange("p (d e) -> p d e", d=64)[:, :, 0]),
                     ["P2"], ["yfT"])
            p.op("t", lambda e: e.transpose(ptb[:], yfT[:], sid[:]), ["yfT", "sid"], ["ptb"])
            p.op("v", lambda e: e.tensor_copy(yfb[:], ptb[:]), ["ptb"], ["yfb"])
            p.op("v", lambda e, m=m: e.tensor_add(y2acc[:, m*128:(m+1)*128],
                                                  y2acc[:, m*128:(m+1)*128], yfb[:]),
                 ["yfb", "y2acc"], ["y2acc"])
        for l in range(128):
            W = 128 - l
            qv = q2.rearrange("p (m t) -> p m t", m=32)[:, :, l:128]
            kv = kn2.rearrange("p (m t) -> p m t", m=32)[:, :, 0:W]
            p.op("v", lambda e, qv=qv, kv=kv, W=W: e.tensor_mul(
                g2sc[:, 0:32*W].rearrange("p (m t) -> p m t", t=W), qv, kv), ["q2", "kn2"], ["g2sc"])
            segs = max(1, 512 // W)
            m0 = 0
            while m0 < 32:
                ns = min(segs, 32 - m0)
                width = ns * W
                p.op("t", lambda e, l=l, m0=m0, width=width, W=W:
                     e.matmul(psA[:, 0:width], Pblk[:, l*128:(l+1)*128],
                              g2sc[:, m0*W:m0*W+width], start=True, stop=True),
                     ["Pblk", "g2sc"], ["psM", "psW0", "psW1", "psW2", "psW3"])
                p.op("v", lambda e, m0=m0, ns=ns, W=W, width=width:
                     e.tensor_mul(tmpb[:, 0:width].rearrange("p (m t) -> p m t", t=W),
                                  psA[:, 0:width].rearrange("p (m t) -> p m t", t=W),
                                  v2.rearrange("p (m t) -> p m t", m=32)[:, m0:m0+ns, 0:W]),
                     ["psM", "v2"], ["tmpb"])
                p.op("v", lambda e, m0=m0, ns=ns, W=W, l=l:
                     e.tensor_add(y2acc.rearrange("p (m t) -> p m t", m=32)[:, m0:m0+ns, l:128],
                                  y2acc.rearrange("p (m t) -> p m t", m=32)[:, m0:m0+ns, l:128],
                                  tmpb[:, 0:ns*W].rearrange("p (m t) -> p m t", t=W)),
                     ["tmpb", "y2acc"], ["y2acc"])
                m0 += ns
        p.dma("q", ys2[:], ysf[:, t0:t0 + LSEQ], ["ysf"], ["ys2"])
        p.op("v", lambda e: e.tensor_scalar(y2acc[:], y2acc[:], omg[:], None, ALU.mult),
             ["y2acc", "omg"], ["y2acc"])
        p.op("v", lambda e: e.scalar_tensor_tensor(y2acc[:], ys2[:], gg[:], y2acc[:], ALU.mult, ALU.add),
             ["ys2", "gg", "y2acc"], ["y2acc"])
        p.dma("g", y0[:, t0:t0 + LSEQ], y2acc[:], ["y2acc"], ["y0fm"])


def emit_LC(p):
    y0 = p.din("y0_fmb", [D, NTOK], BF16)
    wo = p.din("W_o", [D, D])
    idenf = p.din("idenf", [128, 128])
    s0 = p.dout("s0_tm", [TSH, D])
    sidf = p.sb("sidf", [128, 128])
    p.dma("q", sidf[:], idenf, ["idenf"], ["sidf"])
    wob = p.sb("wob", [128, 8 * D], BF16)
    p.dma("g", wob.rearrange("p (t c) -> p t c", t=8), wo.rearrange("(t p) c -> p t c", p=128),
          ["W_o"], ["wob"])
    y0s = p.sb("y0s", [128, 8 * TSH], BF16)
    pm = p.ps("pm", [128, 1024])
    ptf = p.ps("ptf", [128, 128])
    ev = p.sb("ev", [128, 512])
    s0c = p.sb("s0c", [128, 4 * D])
    y0own = p.din("y0own", [D, TSH], BF16)
    p.dma("q", y0s.rearrange("p (t n) -> p t n", t=8), y0own.rearrange("(t p) n -> p t n", p=128),
          ["y0own"], ["y0s"])
    for tc in range(4):
        for oc in range(8):
            for kt in range(8):
                p.op("t", lambda e, oc=oc, kt=kt, tc=tc:
                     e.matmul(pm[:, (oc % 2) * 512:(oc % 2) * 512 + 512],
                              wob[:, kt * D + oc * 128: kt * D + oc * 128 + 128],
                              y0s[:, kt * TSH + tc * 512: kt * TSH + tc * 512 + 512],
                              start=(kt == 0), stop=(kt == 7)),
                     ["wob", "y0s"], [f"pm{oc%2}"], "p" if kt else "c")
            p.op("s", lambda e, oc=oc: e.copy(ev[:], pm[:, (oc % 2) * 512:(oc % 2) * 512 + 512]),
                 [f"pm{oc%2}"], ["ev"])
            for cc in range(4):
                p.op("t", lambda e, cc=cc: e.transpose(ptf[:], ev[:, cc * 128:(cc + 1) * 128], sidf[:]),
                     ["ev", "sidf"], ["ptf"])
                p.op("v", lambda e, cc=cc, oc=oc: e.tensor_copy(s0c[:, cc * D + oc * 128: cc * D + (oc + 1) * 128], ptf[:]),
                     ["ptf"], ["s0c"])
        p.dma("q", s0.rearrange("(a p) d -> p a d", p=128)[:, tc * 4:(tc + 1) * 4, :],
              s0c.rearrange("p (a d) -> p a d", a=4), ["s0c"], ["s0t"])


def emit_LD(p):
    xno = p.din("xnown", [D, TSH], BF16)
    wr = p.din("Wr", [D, EXP])
    br = p.din("br", [EXP])
    e0o = p.dout("e0", [TSH]); e1o = p.dout("e1", [TSH])
    w0o = p.dout("w0", [TSH]); w1o = p.dout("w1", [TSH])
    xns = p.sb("xns", [128, 8 * TSH], BF16)
    p.dma("q", xns.rearrange("p (t n) -> p t n", t=8), xno.rearrange("(t p) n -> p t n", p=128),
          ["xnown"], ["xns"])
    wrb = p.sb("wrb", [128, 8 * EXP], BF16)
    p.dma("g", wrb.rearrange("p (t c) -> p t c", t=8), wr.rearrange("(t p) c -> p t c", p=128),
          ["Wr"], ["wrb"])
    brb = p.sb("brb", [128, EXP])
    p.dma("q", brb[:], br.unsqueeze(0).to_broadcast([128, EXP]), ["br"], ["brb"])
    iot = p.sb("iot", [128, EXP])
    p.op("g", lambda e: e.iota(iot[:], pattern=[[1, EXP]], base=0, channel_multiplier=0,
                               allow_small_or_imprecise_dtypes=True), [], ["iot"])
    big = p.sb("big", [128, EXP])
    p.op("v", lambda e: e.memset(big[:], 999.0), [], ["big"])
    pm = p.ps("pm", [128, EXP])
    lg = p.sb("lg", [128, EXP])
    m8 = p.sb("m8", [128, 8])
    l0 = p.sb("l0", [128, 1]); l1 = p.sb("l1", [128, 1])
    w0c = p.sb("w0c", [128, 1]); w1c = p.sb("w1c", [128, 1])
    oh = p.sb("oh", [128, EXP], I32)
    cand = p.sb("cand", [128, EXP])
    e0c = p.sb("e0c", [128, 1]); e1c = p.sb("e1c", [128, 1])
    neq = p.sb("neq", [128, 1])
    c3 = p.sb("c3", [128, EXP])
    for tt in range(TSH // 128):
        for kt in range(8):
            p.op("t", lambda e, tt=tt, kt=kt:
                 e.matmul(pm[:], xns[:, kt * TSH + tt * 128: kt * TSH + (tt + 1) * 128],
                          wrb[:, kt * EXP:(kt + 1) * EXP], start=(kt == 0), stop=(kt == 7)),
                 ["xns", "wrb"], ["pm"], "p" if kt else "c")
        p.op("v", lambda e: e.tensor_add(lg[:], pm[:], brb[:]), ["pm", "brb"], ["lg"])
        p.op("v", lambda e: e.max(out=m8[:], in_=lg[:]), ["lg"], ["m8"])
        p.op("v", lambda e: e.tensor_sub(w0c[:], m8[:, 0:1], m8[:, 1:2]), ["m8"], ["w0c"])
        p.op("s", lambda e: e.activation(w0c[:], w0c[:], AF.Sigmoid), ["w0c"], ["w0c"])
        p.op("v", lambda e: e.tensor_scalar(w1c[:], w0c[:], -1.0, 1.0, ALU.mult, ALU.add), ["w0c"], ["w1c"])
        p.op("v", lambda e: e.tensor_copy(l0[:], m8[:, 0:1]), ["m8"], ["l0"])
        p.op("v", lambda e: e.tensor_copy(l1[:], m8[:, 1:2]), ["m8"], ["l1"])
        p.op("v", lambda e: e.tensor_scalar(oh[:], lg[:], l0[:], None, ALU.is_equal), ["lg", "l0"], ["oh"])
        p.op("v", lambda e: e.select(cand[:], oh[:], iot[:], big[:]), ["oh", "iot", "big"], ["cand"])
        p.op("v", lambda e: e.tensor_reduce(e0c[:], cand[:], AX.X, ALU.min), ["cand"], ["e0c"])
        p.op("v", lambda e: e.tensor_tensor(neq[:], l1[:], l0[:], ALU.not_equal), ["l0", "l1"], ["neq"])
        p.op("v", lambda e: e.tensor_scalar(c3[:], iot[:], e0c[:], None, ALU.is_gt), ["iot", "e0c"], ["c3"])
        p.op("v", lambda e: e.tensor_scalar(c3[:], c3[:], neq[:], None, ALU.add), ["c3", "neq"], ["c3"])
        p.op("v", lambda e: e.tensor_scalar_min(c3[:], c3[:], 1.0), ["c3"], ["c3"])
        p.op("v", lambda e: e.tensor_scalar(cand[:], lg[:], l1[:], None, ALU.is_equal), ["lg", "l1"], ["cand"])
        p.op("v", lambda e: e.tensor_mul(c3[:], c3[:], cand[:]), ["c3", "cand"], ["c3"])
        p.op("v", lambda e: e.tensor_single_scalar(oh[:], c3[:], 0.5, ALU.is_gt), ["c3"], ["oh"])
        p.op("v", lambda e: e.select(cand[:], oh[:], iot[:], big[:]), ["oh", "iot", "big"], ["cand"])
        p.op("v", lambda e: e.tensor_reduce(e1c[:], cand[:], AX.X, ALU.min), ["cand"], ["e1c"])
        p.dma("q", e0o[tt * 128:(tt + 1) * 128].unsqueeze(1), e0c[:], ["e0c"], ["e0o"])
        p.dma("q", e1o[tt * 128:(tt + 1) * 128].unsqueeze(1), e1c[:], ["e1c"], ["e1o"])
        p.dma("q", w0o[tt * 128:(tt + 1) * 128].unsqueeze(1), w0c[:], ["w0c"], ["w0o"])
        p.dma("q", w1o[tt * 128:(tt + 1) * 128].unsqueeze(1), w1c[:], ["w1c"], ["w1o"])


def emit_LE(p):
    e0 = p.din("e0", [NTOK]); e1 = p.din("e1", [NTOK])
    w0 = p.din("w0", [NTOK]); w1 = p.din("w1", [NTOK])
    xnb = p.din("xn_tmb", [NTOK, D], BF16)
    utri = p.din("utri", [128, 128], BF16)
    NP = NTOK // 128
    er = p.sb("er", [128, 2 * NP])
    wr_ = p.sb("wr_", [128, 2 * NP])
    p.dma("q", er[:, 0:NP], e0.rearrange("(p f) -> p f", p=128), ["e0"], ["er"])
    p.dma("q", er[:, NP:2*NP], e1.rearrange("(p f) -> p f", p=128), ["e1"], ["er"])
    p.dma("q", wr_[:, 0:NP], w0.rearrange("(p f) -> p f", p=128), ["w0"], ["wr_"])
    p.dma("q", wr_[:, NP:2*NP], w1.rearrange("(p f) -> p f", p=128), ["w1"], ["wr_"])
    vals = p.sb("vals", [128, EXP * 256])
    neg1 = p.sb("neg1", [128, 256])
    mski = p.sb("mski", [128, 256], I32)
    p.op("v", lambda e: e.memset(neg1[:], -1.0), [], ["neg1"])
    for ex in range(EXP):
        for r in range(2):
            p.op("v", lambda e, ex=ex, r=r: e.tensor_single_scalar(mski.bitcast(F32)[:, r*NP:(r+1)*NP] if False else mski[:, r*NP:(r+1)*NP], er[:, r*NP:(r+1)*NP], float(ex), ALU.is_equal),
                 ["er"], ["mski"])
            p.op("v", lambda e, ex=ex, r=r: e.select(vals[:, ex*256+r*NP:ex*256+(r+1)*NP],
                                                     mski[:, r*NP:(r+1)*NP], wr_[:, r*NP:(r+1)*NP],
                                                     neg1[:, 0:NP]),
                 ["mski", "wr_", "neg1"], ["vals"])
    myslot = p.sb("myslot", [128, EXP * 256])
    p.op("g", lambda e: e.iota(myslot[:], pattern=[[0, EXP], [128, 2], [1, NP]], base=0,
                               channel_multiplier=256, allow_small_or_imprecise_dtypes=True),
         [], ["myslot"])
    ones1f = p.sb("ones1f", [1, 128])
    ones128f = p.sb("ones128f", [128, 1])
    p.op("v", lambda e: e.memset(ones1f[:], 1.0), [], ["ones1f"])
    p.op("v", lambda e: e.memset(ones128f[:], 1.0), [], ["ones128f"])
    lo = p.sb("lo", [1, EXP]); hi = p.sb("hi", [1, EXP]); mid = p.sb("mid", [1, EXP])
    cnt = p.sb("cnt", [1, EXP]); cmpc = p.sb("cmpc", [1, EXP], I32)
    p.op("v", lambda e: e.memset(lo[:], -1.0), [], ["lo"])
    p.op("v", lambda e: e.memset(hi[:], 1.000001), [], ["hi"])
    psB = p.ps("psB", [128, 8])
    psC = p.ps("psC", [1, 8])
    gemask = p.sb("gemask", [128, 2048])
    seg = p.sb("seg", [128, 8])
    thr8 = p.sb("thr8", [128, 8])
    thrb = thr8[:].unsqueeze(2).to_broadcast([128, EXP, 256])

    def count_pass(thr_row, dst_cnt, pred_fn, rkeys):
        p.op("t", lambda e, thr_row=thr_row: e.matmul(psB[:], ones1f[:],
             thr_row, start=True, stop=True), list(rkeys) + ["ones1f"], ["psB"])
        p.op("v", lambda e: e.tensor_copy(thr8[:], psB[:]), ["psB"], ["thrb"])
        pred_fn()
        p.op("v", lambda e: e.tensor_reduce(seg[:].unsqueeze(2), gemask.rearrange("p (x c) -> p x c", x=8), AX.X, ALU.add),
             ["gemask"], ["seg"])
        p.op("t", lambda e: e.matmul(psC[:], ones128f[:], seg[:], start=True, stop=True),
             ["ones128f", "seg"], ["psC"])
        p.op("v", lambda e, dst_cnt=dst_cnt: e.tensor_copy(dst_cnt, psC[:]), ["psC"], ["cnt"])

    for it in range(38):
        p.op("v", lambda e: e.tensor_add(mid[:], lo[:], hi[:]), ["lo", "hi"], ["mid"])
        p.op("v", lambda e: e.tensor_scalar_mul(mid[:], mid[:], 0.5), ["mid"], ["mid"])
        count_pass(mid[:], cnt[:],
                   lambda: p.op("v", lambda e: e.tensor_tensor(
                       gemask[:].rearrange("p (x c) -> p x c", x=8),
                       vals[:].rearrange("p (x c) -> p x c", x=8), thrb, ALU.is_gt),
                                ["vals", "thrb"], ["gemask"]), ["mid"])
        p.op("v", lambda e: e.tensor_single_scalar(cmpc[:], cnt[:], CAP - 0.5, ALU.is_ge), ["cnt"], ["cmpc"])
        p.op("v", lambda e: e.select(lo[:], cmpc[:], mid[:], lo[:]), ["cmpc", "mid", "lo"], ["lo"])
        p.op("v", lambda e: e.select(hi[:], cmpc[:], hi[:], mid[:]), ["cmpc", "mid", "hi"], ["hi"])
    gtm = p.sb("gtm", [128, 2048])
    eqm = p.sb("eqm", [128, 2048])
    count_pass(hi[:], cnt[:],
               lambda: p.op("v", lambda e: e.tensor_tensor(
                   gemask[:].rearrange("p (x c) -> p x c", x=8),
                   vals[:].rearrange("p (x c) -> p x c", x=8), thrb, ALU.is_gt),
                            ["vals", "thrb"], ["gemask"]), ["hi"])
    p.op("v", lambda e: e.tensor_copy(gtm[:], gemask[:]), ["gemask"], ["gtm"])
    mrow = p.sb("mrow", [1, EXP])
    p.op("v", lambda e: e.tensor_scalar(mrow[:], cnt[:], -1.0, float(CAP), ALU.mult, ALU.add), ["cnt"], ["mrow"])
    count_pass(lo[:], cnt[:],
               lambda: p.op("v", lambda e: e.tensor_tensor(
                   gemask[:].rearrange("p (x c) -> p x c", x=8),
                   vals[:].rearrange("p (x c) -> p x c", x=8), thrb, ALU.is_gt),
                            ["vals", "thrb"], ["gemask"]), ["lo"])
    p.op("v", lambda e: e.tensor_sub(eqm[:], gemask[:], gtm[:]), ["gemask", "gtm"], ["eqm"])
    lo2 = p.sb("lo2", [1, EXP])
    p.op("v", lambda e: e.memset(lo2[:], 0.0), [], ["lo2"])
    for it in range(17):
        step = float(2 ** (16 - it))
        p.op("v", lambda e, step=step: e.tensor_scalar(mid[:], lo2[:], 1.0, step, ALU.mult, ALU.add), ["lo2"], ["mid"])
        def predi():
            p.op("v", lambda e: e.tensor_tensor(
                gemask[:].rearrange("p (x c) -> p x c", x=8),
                myslot[:].rearrange("p (x c) -> p x c", x=8), thrb, ALU.is_lt),
                 ["myslot", "thrb"], ["gemask"])
            p.op("v", lambda e: e.tensor_mul(gemask[:], gemask[:], eqm[:]), ["gemask", "eqm"], ["gemask"])
        count_pass(mid[:], cnt[:], predi, ["mid"])
        p.op("v", lambda e: e.tensor_tensor(cmpc[:], cnt[:], mrow[:], ALU.is_le), ["cnt", "mrow"], ["cmpc"])
        p.op("v", lambda e: e.select(lo2[:], cmpc[:], mid[:], lo2[:]), ["cmpc", "mid", "lo2"], ["lo2"])
    count_pass(lo2[:], cnt[:],
               lambda: (p.op("v", lambda e: e.tensor_tensor(
                        gemask[:].rearrange("p (x c) -> p x c", x=8),
                        myslot[:].rearrange("p (x c) -> p x c", x=8), thrb, ALU.is_lt),
                             ["myslot", "thrb"], ["gemask"]),
                        p.op("v", lambda e: e.tensor_mul(gemask[:], gemask[:], eqm[:]), ["gemask", "eqm"], ["gemask"]))[0] and None,
               ["lo2"])
    sel = p.sb("sel", [128, 2048])
    p.op("v", lambda e: e.tensor_add(sel[:], gtm[:], gemask[:]), ["gtm", "gemask"], ["sel"])
    onesB = p.sb("onesB", [128, 256])
    p.op("v", lambda e: e.memset(onesB[:], 1.0), [], ["onesB"])
    pfx = p.sb("pfx", [128, 2048])
    for ex in range(EXP):
        p.op("v", lambda e, ex=ex: e.tensor_tensor_scan(pfx[:, ex*256:(ex+1)*256], onesB[:],
                                                        sel[:, ex*256:(ex+1)*256], 0.0, ALU.mult, ALU.add),
             ["onesB", "sel"], ["pfx"])
    rowr = p.sb("rowr", [128, 2048])
    p.op("v", lambda e: e.tensor_sub(rowr[:], pfx[:], sel[:]), ["pfx", "sel"], ["rowr"])
    rsum = p.sb("rsum", [128, EXP])
    p.op("v", lambda e: e.tensor_copy(rsum[:], pfx.rearrange("p (x c) -> p x c", x=8)[:, :, 255]), ["pfx"], ["rsum"])
    rsb = p.sb("rsb", [128, EXP], BF16)
    p.op("v", lambda e: e.tensor_copy(rsb[:], rsum[:]), ["rsum"], ["rsb"])
    sutri = p.sb("sutri", [128, 128], BF16)
    p.dma("q", sutri[:], utri, ["utri"], ["sutri"])
    psD = p.ps("psD", [128, EXP])
    p.op("t", lambda e: e.matmul(psD[:], sutri[:], rsb[:], start=True, stop=True),
         ["sutri", "rsb"], ["psD"])
    roff = p.sb("roff", [128, EXP])
    p.op("v", lambda e: e.tensor_copy(roff[:], psD[:]), ["psD"], ["roff"])
    ebase = p.sb("ebase", [128, 2048])
    p.op("g", lambda e: e.iota(ebase[:], pattern=[[CAP, EXP], [0, 2], [0, NP]], base=0,
                               channel_multiplier=0, allow_small_or_imprecise_dtypes=True), [], ["ebase"])
    dest = p.sb("dest", [128, 2048])
    p.op("v", lambda e: e.tensor_add(dest[:], rowr[:], ebase[:]), ["rowr", "ebase"], ["dest"])
    for ex in range(EXP):
        p.op("v", lambda e, ex=ex: e.tensor_scalar(dest[:, ex*256:(ex+1)*256], dest[:, ex*256:(ex+1)*256],
                                                   roff[:, ex:ex+1], None, ALU.add),
             ["dest", "roff"], ["dest"])
    bigs = p.sb("bigs", [128, 2048])
    p.op("v", lambda e: e.tensor_scalar(bigs[:], sel[:], -1.0e6, 1.0e6, ALU.mult, ALU.add), ["sel"], ["bigs"])
    p.op("v", lambda e: e.tensor_mul(dest[:], dest[:], sel[:]), ["dest", "sel"], ["dest"])
    p.op("v", lambda e: e.tensor_add(dest[:], dest[:], bigs[:]), ["dest", "bigs"], ["dest"])
    mdst = p.sb("mdst", [128, 256])
    p.op("v", lambda e: e.tensor_reduce(mdst[:].unsqueeze(2), dest.rearrange("p (x c) -> p c x", x=8), AX.X, ALU.min),
         ["dest"], ["mdst"])
    mdsti = p.sb("mdsti", [128, 256], I32)
    p.op("v", lambda e: e.tensor_copy(mdsti[:], mdst[:]), ["mdst"], ["mdsti"])
    mdstiH = p.dout("mdstiH", [128, 256], I32)
    cmprowH = p.dout("cmprowH", [32768])
    p.dma("q", mdstiH, mdsti[:], ["mdsti"], ["mdstiH"])
    p.dma("q", cmprowH.rearrange("(p c) -> p c", p=128), mdst[:], ["mdst"], ["cmprowH"])
    slotid = p.sb("slotid", [128, 256])
    p.op("g", lambda e: e.iota(slotid[:], pattern=[[16384, 2], [1, NP]], base=0,
                               channel_multiplier=128, allow_small_or_imprecise_dtypes=True), [], ["slotid"])
    meta = p.sb("meta", [128, 512])
    p.op("v", lambda e: e.tensor_copy(meta.rearrange("p (c two) -> p c two", two=2)[:, :, 0], slotid[:]),
         ["slotid"], ["meta"])
    p.op("v", lambda e: e.tensor_copy(meta.rearrange("p (c two) -> p c two", two=2)[:, :, 1], wr_[:]),
         ["wr_"], ["meta"])
    metaIH = p.dout("metaIH", [128, 512])
    p.dma("q", metaIH, meta[:], ["meta"], ["metaIH"])
    xemH = p.dout("xemH", [EXP * CAP, 1040], BF16)
    zb = p.sb("zb", [128, 1040], BF16)
    p.op("v", lambda e: e.memset(zb[:], 0.0), [], ["zb"])
    p.dma("q", xemH.rearrange("(a p) d -> p a d", p=128), zb[:].unsqueeze(1).to_broadcast([128, EXP*CAP//128, 1040]),
          ["zb"], ["xemH"])


NSF = 13
FOH = NCORE * NSF
NOH = 128 - FOH


def emit_YTZ(p):
    ytok = p.dram("ytokP", [NTOK, D], BF16)
    z = p.sb("z", [128, D], BF16)
    p.op("v", lambda e: e.memset(z[:], 0.0), [], ["ytz"])
    p.dma("q", ytok.rearrange("(a p) d -> p a d", p=128),
          z[:].unsqueeze(1).to_broadcast([128, NTOK // 128, D]), ["ytz"], ["ytokP"])


def emit_DISPATCH(p):
    xnF = p.internals["xn_tmb_full"]
    mdD = p.internals["mdstiH"]
    mtD = p.internals["metaIH"]
    xem = p.internals["xemH"]
    xview = xnF.rearrange("(pp f) d -> pp f d", f=128)
    xsel = p.sb("xsel", [128, NSF * 1024], BF16)
    mdsel = p.sb("mdsel", [128, 2 * NSF], I32)
    mtsel = p.sb("mtsel", [128, 4 * NSF])
    p.swdma(lambda e, c: e.dma_start(out=xsel[:].rearrange("p (i d) -> p i d", d=1024),
                                     in_=xview[:, NSF*c:NSF*(c+1), :]),
            ["xn_tmb_full"], ["xsel"])
    p.swdma(lambda e, c: e.dma_start(out=mdsel[:, 0:NSF], in_=mdD[:, NSF*c:NSF*(c+1)]),
            ["mdstiH"], ["mdsel"])
    p.swdma(lambda e, c: e.dma_start(out=mdsel[:, NSF:2*NSF], in_=mdD[:, 128+NSF*c:128+NSF*(c+1)]),
            ["mdstiH"], ["mdsel"])
    p.swdma(lambda e, c: e.dma_start(out=mtsel[:, 0:2*NSF], in_=mtD[:, 2*NSF*c:2*NSF*(c+1)]),
            ["metaIH"], ["mtsel"])
    p.swdma(lambda e, c: e.dma_start(out=mtsel[:, 2*NSF:4*NSF], in_=mtD[:, 256+2*NSF*c:256+2*NSF*(c+1)]),
            ["metaIH"], ["mtsel"])
    pay = p.sb("pay", [128, 2 * NSF * 1040], BF16)
    payv = pay[:].rearrange("p (j d) -> p j d", d=1040)
    p.op("v", lambda e: e.memset(pay[:], 0.0), [], ["pay"])
    th = p.sb("th", [128, 1])
    tl = p.sb("tl", [128, 1])
    for j in range(2 * NSF):
        r, i = divmod(j, NSF)
        p.op("v", lambda e, j=j, i=i: e.tensor_copy(
            payv[:, j, 0:1024], xsel[:].rearrange("p (i d) -> p i d", d=1024)[:, i, :]),
            ["xsel", "pay"], ["pay"])
        p.op("v", lambda e, j=j: e.tensor_scalar(th[:], mtsel[:, 2*j:2*j+1],
             1.0 / 128.0, -0.4999, ALU.mult, ALU.add), ["mtsel"], ["th"])
        p.op("v", lambda e: e.tensor_scalar(th[:], th[:], 1.0, RBIG, ALU.mult, ALU.add),
             ["th"], ["th"])
        p.op("v", lambda e: e.tensor_scalar(th[:], th[:], RBIG, None, ALU.subtract),
             ["th"], ["th"])
        p.op("v", lambda e, j=j: e.tensor_scalar(tl[:], th[:], -128.0, None, ALU.mult),
             ["th"], ["tl"])
        p.op("v", lambda e, j=j: e.tensor_tensor(tl[:], tl[:], mtsel[:, 2*j:2*j+1], ALU.add),
             ["tl", "mtsel"], ["tl"])
        p.op("v", lambda e, j=j: e.tensor_copy(payv[:, j, 1024:1025], th[:]),
             ["th", "pay"], ["pay"])
        p.op("v", lambda e, j=j: e.tensor_copy(payv[:, j, 1025:1026], tl[:]),
             ["tl", "pay"], ["pay"])
        p.op("v", lambda e, j=j: e.tensor_copy(
            payv[:, j, 1026:1028], mtsel[:, 2*j+1:2*j+2].bitcast(BF16)),
            ["mtsel", "pay"], ["pay"])
    for j in range(2 * NSF):
        p.op("g", lambda e, j=j: e.indirect_dma_start(
            out=xem, out_offset=IOoA(mdsel[:, j:j+1], 0),
            in_=payv[:, j, :], in_offset=None,
            bounds_check=EXP*CAP-1, oob_is_err=False),
            ["mdsel", "pay", "xemH"], ["xemH"], "d")


def emit_ONEHOT(p):
    cmF = p.internals["cmprowH"].rearrange("(p c) -> p c", p=128)
    mtD = p.internals["metaIH"]
    xnF = p.internals["xn_tmb_full"]
    xview = xnF.rearrange("(pp f) d -> pp f d", f=128)
    ohx = p.dram("ohxD", [CAP, 1024], BF16)
    ohm = p.dram("ohmD", [CAP, 4])
    md48 = p.sb("md48", [128, 2 * NOH])
    p.dma("q", md48[:, 0:NOH], cmF[:, FOH:128], ["cmprowH"], ["md48"])
    p.dma("q", md48[:, NOH:2*NOH], cmF[:, 128+FOH:256], ["cmprowH"], ["md48"])
    mt48 = p.sb("mt48", [128, 4 * NOH])
    p.dma("q", mt48[:, 0:2*NOH], mtD[:, 2*FOH:256], ["metaIH"], ["mt48"])
    p.dma("q", mt48[:, 2*NOH:4*NOH], mtD[:, 256+2*FOH:512], ["metaIH"], ["mt48"])
    NJ4 = 2 * NOH
    slv = mt48[:].rearrange("p (j two) -> p two j", two=2)
    th48 = p.sb("th48", [128, NJ4])
    tl48 = p.sb("tl48", [128, NJ4])
    rhs4 = p.sb("rhs4", [128, 4 * NJ4], BF16)
    r4v = rhs4[:].rearrange("p (j f) -> p f j", f=4)
    p.op("v", lambda e: e.tensor_scalar(th48[:], slv[:, 0, :], 1.0 / 128.0, -0.4999,
         ALU.mult, ALU.add), ["mt48"], ["th48"])
    p.op("v", lambda e: e.tensor_scalar(th48[:], th48[:], 1.0, RBIG, ALU.mult, ALU.add),
         ["th48"], ["th48"])
    p.op("v", lambda e: e.tensor_scalar(th48[:], th48[:], RBIG, None, ALU.subtract),
         ["th48"], ["th48"])
    p.op("v", lambda e: e.tensor_scalar(tl48[:], th48[:], -128.0, None, ALU.mult),
         ["th48"], ["tl48"])
    p.op("v", lambda e: e.tensor_tensor(tl48[:], tl48[:], slv[:, 0, :], ALU.add),
         ["tl48", "mt48"], ["tl48"])
    p.op("v", lambda e: e.memset(rhs4[:], 0.0), [], ["rhs4"])
    p.op("v", lambda e: e.tensor_copy(r4v[:, 0, :], th48[:]), ["th48", "rhs4"], ["rhs4"])
    p.op("v", lambda e: e.tensor_copy(r4v[:, 1, :], tl48[:]), ["tl48", "rhs4"], ["rhs4"])
    p.op("v", lambda e: e.tensor_copy(r4v[:, 2, :], slv[:, 1, :]), ["mt48", "rhs4"], ["rhs4"])
    xp = p.sb("xp", [128, NOH * 1024], BF16)
    p.dma("q", xp[:].rearrange("p (i d) -> p i d", d=1024), xview[:, FOH:128, :],
          ["xn_tmb_full"], ["xp"])
    cb = p.din("cbase", [1, 1])
    cbb = p.sb("cbb", [128, 1])
    p.dma("q", cbb[:], cb.to_broadcast([128, 1]), ["cbase"], ["cbb"])
    e48 = p.sb("e48", [128, 2 * NOH])
    p.op("v", lambda e: e.tensor_scalar(e48[:], md48[:], cbb[:], None, ALU.subtract),
         ["md48", "cbb"], ["e48"])
    iof = p.sb("iof", [128, 128])
    p.op("g", lambda e: e.iota(iof[:], pattern=[[1, 128]], base=0, channel_multiplier=0,
                               allow_small_or_imprecise_dtypes=True), [], ["iof"])
    ohb = [p.sb(f"ohb{i}", [128, 128], BF16) for i in range(2)]
    psx = p.ps("psx", [128, 1024])
    psm = p.ps("psm", [128, 4])
    oxt = p.sb("oxt", [128, 1024], BF16)
    omt = p.sb("omt", [128, 4])
    NJ = 2 * NOH
    for rt in range(CAP // 128):
        for j in range(NJ):
            i = j % 2
            p.op("v", lambda e, j=j, rt=rt, i=i: e.tensor_scalar(
                ohb[i][:], iof[:], e48[:, j:j+1], float(-(rt * 128)),
                ALU.subtract, ALU.is_equal), ["iof", "e48"], [f"ohb{i}"])
            for half in range(2):
                p.op("t", lambda e, j=j, i=i, half=half: e.matmul(
                    psx[:, half*512:(half+1)*512], ohb[i][:],
                    xp[:, (j % NOH)*1024 + half*512:(j % NOH)*1024 + half*512 + 512],
                    start=(j == 0), stop=(j == NJ - 1)),
                    [f"ohb{i}", "xp"], ["psx"], "p" if j else "c")
            p.op("t", lambda e, j=j, i=i: e.matmul(
                psm[:], ohb[i][:], rhs4[:, 4*j:4*j+4],
                start=(j == 0), stop=(j == NJ - 1)),
                [f"ohb{i}", "rhs4"], ["psm"], "p" if j else "c")
        p.op("v", lambda e: e.tensor_copy(oxt[:], psx[:]), ["psx"], ["oxt"])
        p.op("v", lambda e: e.tensor_copy(omt[:], psm[:]), ["psm"], ["omt"])
        p.dma("q", ohx[rt*128:(rt+1)*128, :], oxt[:], ["oxt"], ["ohxD"])
        p.dma("q", ohm[rt*128:(rt+1)*128, :], omt[:], ["omt"], ["ohmD"])


def emit_LF(p):
    xem = p.din("xem", [CAP, 1040], BF16)
    ohx = p.din("ohxD", [CAP, 1024], BF16)
    ohm = p.din("ohmD", [CAP, 2])
    ytokP = p.din("ytokP", [NTOK, D], BF16)
    w1 = p.din("w1e", [D, 4096])
    b1 = p.din("b1e", [4096])
    w2 = p.din("w2e", [4096, D])
    b2 = p.din("b2e", [D])
    iden = p.din("iden", [128, 128], BF16)
    idenf = p.din("idenf", [128, 128])
    sid = p.sb("sid", [128, 128], BF16)
    sidf2 = p.sb("sidf2", [128, 128])
    p.dma("q", sid[:], iden, ["iden"], ["sid"])
    p.dma("q", sidf2[:], idenf, ["idenf"], ["sidf2"])
    w1b = p.sb("w1b", [128, 8 * 4096], BF16)
    p.dma("g", w1b.rearrange("p (t c) -> p t c", t=8), w1.rearrange("(t p) c -> p t c", p=128),
          ["w1e"], ["w1b"])
    w2b = p.sb("w2b", [128, 32 * 128], BF16)
    xeT = p.sb("xeT", [128, 8 * CAP], BF16)
    xet = p.sb("xet", [128, D], BF16)
    oxt = p.sb("oxt", [128, D], BF16)
    ptb = p.ps("ptb", [128, 128], BF16)
    for rt in range(CAP // 128):
        p.dma("q", xet[:], xem[rt*128:(rt+1)*128, 0:1024], ["xem"], ["xet"])
        p.dma("q", oxt[:], ohx[rt*128:(rt+1)*128, :], ["ohxD"], ["oxt"])
        p.op("v", lambda e: e.tensor_add(xet[:], xet[:], oxt[:]), ["xet", "oxt"], ["xet"])
        for f in range(8):
            p.op("t", lambda e, f=f: e.transpose(ptb[:], xet[:, f*128:(f+1)*128], sid[:]),
                 ["xet", "sid"], ["ptb"])
            p.op("v", lambda e, f=f, rt=rt: e.tensor_copy(xeT[:, f*CAP+rt*128:f*CAP+(rt+1)*128], ptb[:]),
                 ["ptb"], ["xeT"])
    b1c = p.sb("b1c", [128, 32])
    p.dma("q", b1c[:], b1.rearrange("(m p) -> p m", p=128), ["b1e"], ["b1c"],
          allow_slow_non_contiguous=True)
    b2c = p.sb("b2c", [128, 8])
    p.dma("q", b2c[:], b2.rearrange("(m p) -> p m", p=128), ["b2e"], ["b2c"],
          allow_slow_non_contiguous=True)
    h1T = p.sb("h1T", [128, 32 * 512], BF16)
    pmm = p.ps("pmm", [128, 1024])
    yT = p.sb("yT", [128, 8 * 512])
    ytm = p.sb("ytm", [128, D])
    ytmb = p.sb("ytmb", [128, D], BF16)
    gcol = p.sb("gcol", [128, 2])
    gsl = p.sb("gsl", [128, 2], BF16)
    ghl = p.sb("ghl", [128, 2])
    ggc = p.sb("ggc", [128, 1])
    omtt = p.sb("omtt", [128, 4])
    msk1 = p.sb("msk1", [128, 1])
    dstf = p.sb("dstf", [128, 1])
    dsti = p.sb("dsti", [128, 1], I32)
    ptf = p.ps("ptf", [128, 128])
    for nch in range(CAP // 512):
        for mt in range(32):
            sl = mt % 2
            for kt in range(8):
                p.op("t", lambda e, mt=mt, kt=kt, nch=nch, sl=sl:
                     e.matmul(pmm[:, sl*512:(sl+1)*512], w1b[:, kt*4096+mt*128:kt*4096+(mt+1)*128],
                              xeT[:, kt*CAP+nch*512:kt*CAP+nch*512+512], start=(kt == 0), stop=(kt == 7)),
                     ["w1b", "xeT"], [f"pmm{sl}"], "p" if kt else "c")
            p.op("s", lambda e, mt=mt, sl=sl: e.activation(h1T[:, mt*512:(mt+1)*512], pmm[:, sl*512:(sl+1)*512],
                                                           AF.Gelu, bias=b1c[:, mt:mt+1]),
                 [f"pmm{sl}", "b1c"], ["h1T"])
        for oc in range(8):
            sl = oc % 2
            p.dma("g", w2b.rearrange("p (t c) -> p t c", t=32),
                  w2.rearrange("(t p) c -> p t c", p=128)[:, :, oc*128:(oc+1)*128],
                  ["w2e"], ["w2b"])
            for mt in range(32):
                p.op("t", lambda e, oc=oc, mt=mt, sl=sl:
                     e.matmul(pmm[:, sl*512:(sl+1)*512], w2b[:, mt*128:(mt+1)*128],
                              h1T[:, mt*512:(mt+1)*512], start=(mt == 0), stop=(mt == 31)),
                     ["w2b", "h1T"], [f"pmm{sl}"], "p" if mt else "c")
            p.op("v", lambda e, oc=oc, sl=sl: e.tensor_scalar(yT[:, oc*512:(oc+1)*512], pmm[:, sl*512:(sl+1)*512],
                                                              b2c[:, oc:oc+1], None, ALU.add),
                 [f"pmm{sl}", "b2c"], ["yT"])
        for cc in range(4):
            p.dma("q", gsl[:], xem[nch*512+cc*128:nch*512+(cc+1)*128, 1024:1026], ["xem"], ["gsl"])
            p.dma("q", ggc[:].bitcast(BF16), xem[nch*512+cc*128:nch*512+(cc+1)*128, 1026:1028], ["xem"], ["ggc"])
            p.dma("q", omtt[:], ohm[nch*512+cc*128:nch*512+(cc+1)*128, :], ["ohmD"], ["omtt"])
            p.op("v", lambda e: e.tensor_copy(ghl[:], gsl[:]), ["gsl"], ["ghl"])
            p.op("v", lambda e: e.tensor_tensor(ghl[:], ghl[:], omtt[:, 0:2], ALU.add),
                 ["ghl", "omtt"], ["ghl"])
            p.op("v", lambda e: e.tensor_scalar(gcol[:, 0:1], ghl[:, 0:1], 128.0, None, ALU.mult),
                 ["ghl"], ["gcol"])
            p.op("v", lambda e: e.tensor_tensor(gcol[:, 0:1], gcol[:, 0:1], ghl[:, 1:2], ALU.add),
                 ["gcol", "ghl"], ["gcol"])
            p.op("v", lambda e: e.tensor_tensor(gcol[:, 1:2], ggc[:], omtt[:, 2:3], ALU.add),
                 ["ggc", "omtt", "gcol"], ["gcol"])
            for oc in range(8):
                p.op("t", lambda e, cc=cc, oc=oc: e.transpose(ptf[:], yT[:, oc*512+cc*128:oc*512+(cc+1)*128], sidf2[:]),
                     ["yT", "sidf2"], ["ptf"])
                p.op("v", lambda e, oc=oc: e.tensor_copy(ytm[:, oc*128:(oc+1)*128], ptf[:]), ["ptf"], ["ytm"])
            p.op("v", lambda e: e.tensor_scalar(ytm[:], ytm[:], gcol[:, 1:2], None, ALU.mult),
                 ["ytm", "gcol"], ["ytm"])
            p.op("v", lambda e: e.tensor_copy(ytmb[:], ytm[:]), ["ytm"], ["ytmb"])
            p.op("v", lambda e: e.tensor_single_scalar(msk1[:], gcol[:, 0:1], 16383.5, ALU.is_ge),
                 ["gcol"], ["msk1"])
            p.op("v", lambda e: e.tensor_scalar(msk1[:], msk1[:], -16384.0, None, ALU.mult),
                 ["msk1"], ["msk1"])
            p.op("v", lambda e: e.tensor_add(dstf[:], gcol[:, 0:1], msk1[:]), ["gcol", "msk1"], ["dstf"])
            p.op("v", lambda e: e.tensor_single_scalar(msk1[:], gcol[:, 1:2], 0.0, ALU.is_le),
                 ["gcol"], ["msk1"])
            p.op("v", lambda e: e.tensor_scalar(msk1[:], msk1[:], 2.0e6, None, ALU.mult),
                 ["msk1"], ["msk1"])
            p.op("v", lambda e: e.tensor_add(dstf[:], dstf[:], msk1[:]), ["dstf", "msk1"], ["dstf"])
            p.op("v", lambda e: e.tensor_copy(dsti[:], dstf[:]), ["dstf"], ["dsti"])
            p.op("g", lambda e: e.indirect_dma_start(
                out=ytokP, out_offset=IOoA(dsti[:], 0), in_=ytmb[:], in_offset=None,
                bounds_check=NTOK-1, oob_is_err=False),
                ["dsti", "ytmb", "ytokP"], ["ytokP"], "d")


def emit_LG(p):
    yt = p.din("ytok_own", [TSH, D], BF16)
    xn = p.din("xnown", [TSH, D])
    x = p.din("xown", [TSH, D])
    s0 = p.din("s0own", [TSH, D])
    gam0 = p.din("gamma0", [D])
    bet0 = p.din("beta0", [D])
    iden = p.din("iden", [128, 128], BF16)
    x1o = p.dout("x1_tm", [TSH, D])
    xn1o = p.dout("xn1_tm", [TSH, D])
    xq1o = p.dout("xq1_fmb", [D, TSH], BF16)
    ma1o = p.dout("ma1", [TSH])
    xmaxo = p.dout("xmax1c", [1, 1])
    sid = p.sb("sid", [128, 128], BF16)
    p.dma("q", sid[:], iden, ["iden"], ["sid"])
    ytb = p.sb("ytb", [128, D], BF16)
    gb = p.sb("gb", [128, D])
    bb = p.sb("bb", [128, D])
    p.dma("q", gb[:], gam0.unsqueeze(0).to_broadcast([128, D]), ["gamma0"], ["gb"])
    p.dma("q", bb[:], bet0.unsqueeze(0).to_broadcast([128, D]), ["beta0"], ["bb"])
    ot = p.sb("ot", [128, D])
    xt = p.sb("xt", [128, D])
    st = p.sb("st", [128, D])
    xnt = p.sb("xnt", [128, D])
    sq = p.sb("sq", [128, D])
    mc = p.sb("mc", [128, 1]); vc = p.sb("vc", [128, 1])
    mx = p.sb("mx", [128, 1])
    gmax = p.sb("gmax", [128, 1])
    xqb = p.sb("xqb", [128, D], BF16)
    ptb = p.ps("ptb", [128, 128], BF16)
    trb = p.sb("trb", [128, D], BF16)
    pmx = p.ps("pmx", [128, 128])
    mxf = p.sb("mxf", [1, 128])
    mxs = p.sb("mxs", [1, 1])
    sidf = p.sb("sidf", [128, 128])
    idenf = p.din("idenf", [128, 128])
    p.dma("q", sidf[:], idenf, ["idenf"], ["sidf"])
    for tt in range(TSH // 128):
        p.dma("q", ytb[:], yt[tt*128:(tt+1)*128, :], ["ytok_own"], ["ytb"])
        p.dma("q", xnt[:], xn[tt*128:(tt+1)*128, :], ["xnown"], ["xnt"])
        p.op("v", lambda e: e.tensor_copy(ot[:], ytb[:]), ["ytb"], ["ot"])
        p.op("v", lambda e: e.tensor_add(ot[:], ot[:], xnt[:]), ["ot", "xnt"], ["ot"])
        p.op("v", lambda e: e.reduce_sum(mc[:], ot[:], AX.X), ["ot"], ["mc"])
        p.op("v", lambda e: e.tensor_scalar(mc[:], mc[:], 1.0/D, None, ALU.mult), ["mc"], ["mc"])
        p.op("v", lambda e: e.tensor_scalar(ot[:], ot[:], mc[:], None, ALU.subtract), ["ot", "mc"], ["ot"])
        p.op("s", lambda e: e.activation(sq[:], ot[:], AF.Square, accum_out=vc[:]), ["ot"], ["sq", "vc"])
        p.op("v", lambda e: e.tensor_scalar(vc[:], vc[:], 1.0/D, 1e-5, ALU.mult, ALU.add), ["vc"], ["vc"])
        p.op("s", lambda e: e.activation(vc[:], vc[:], AF.Sqrt), ["vc"], ["vc"])
        p.op("v", lambda e: e.reciprocal(vc[:], vc[:]), ["vc"], ["vc"])
        p.op("v", lambda e: e.tensor_scalar(ot[:], ot[:], vc[:], None, ALU.mult), ["ot", "vc"], ["ot"])
        p.op("v", lambda e: e.tensor_mul(ot[:], ot[:], gb[:]), ["ot", "gb"], ["ot"])
        p.op("v", lambda e: e.tensor_add(ot[:], ot[:], bb[:]), ["ot", "bb"], ["ot"])
        p.op("s", lambda e: e.activation(ot[:], ot[:], AF.Silu), ["ot"], ["ot"])
        p.dma("q", xt[:], x[tt*128:(tt+1)*128, :], ["xown"], ["xt"])
        p.dma("q", st[:], s0[tt*128:(tt+1)*128, :], ["s0own"], ["st"])
        p.op("v", lambda e: e.tensor_mul(ot[:], ot[:], st[:]), ["ot", "st"], ["ot"])
        p.op("v", lambda e: e.tensor_add(xt[:], xt[:], ot[:]), ["xt", "ot"], ["xt"])
        p.dma("q", x1o[tt*128:(tt+1)*128, :], xt[:], ["xt"], ["x1o"])
        p.op("v", lambda e: e.reduce_sum(mc[:], xt[:], AX.X), ["xt"], ["mc"])
        p.op("v", lambda e: e.tensor_scalar(mc[:], mc[:], 1.0/D, None, ALU.mult), ["mc"], ["mc"])
        p.op("v", lambda e: e.tensor_scalar(xnt[:], xt[:], mc[:], None, ALU.subtract), ["xt", "mc"], ["xnt2"])
        p.op("s", lambda e: e.activation(sq[:], xnt[:], AF.Square, accum_out=vc[:]), ["xnt2"], ["sq", "vc"])
        p.op("v", lambda e: e.tensor_scalar(vc[:], vc[:], 1.0/D, 1e-5, ALU.mult, ALU.add), ["vc"], ["vc"])
        p.op("s", lambda e: e.activation(vc[:], vc[:], AF.Sqrt), ["vc"], ["vc"])
        p.op("v", lambda e: e.reciprocal(vc[:], vc[:]), ["vc"], ["vc"])
        p.op("v", lambda e: e.tensor_scalar(xnt[:], xnt[:], vc[:], None, ALU.mult), ["xnt2", "vc"], ["xnt2"])
        p.dma("q", xn1o[tt*128:(tt+1)*128, :], xnt[:], ["xnt2"], ["xn1o"])
        p.op("v", lambda e: e.reduce_max(mx[:], xnt[:], AX.X, apply_absolute_value=True), ["xnt2"], ["mx"])
        p.op("v", lambda e: e.tensor_scalar_max(mx[:], mx[:], 1e-6), ["mx"], ["mx"])
        p.dma("q", ma1o[tt*128:(tt+1)*128].unsqueeze(1), mx[:], ["mx"], ["ma1o"])
        if tt == 0:
            p.op("v", lambda e: e.tensor_copy(gmax[:], mx[:]), ["mx"], ["gmax"])
        else:
            p.op("v", lambda e: e.tensor_max(gmax[:], gmax[:], mx[:]), ["gmax", "mx"], ["gmax"])
        p.op("v", lambda e: e.reciprocal(mc[:], mx[:]), ["mx"], ["mc"])
        p.op("v", lambda e: e.tensor_scalar(mc[:], mc[:], 7.0, None, ALU.mult), ["mc"], ["mc"])
        p.op("v", lambda e: e.tensor_scalar(ot[:], xnt[:], mc[:], RBIG, ALU.mult, ALU.add), ["xnt2", "mc"], ["ot"])
        p.op("v", lambda e: e.tensor_scalar(ot[:], ot[:], RBIG, None, ALU.subtract), ["ot"], ["ot"])
        p.op("v", lambda e: e.tensor_scalar_min(ot[:], ot[:], 7.0), ["ot"], ["ot"])
        p.op("v", lambda e: e.tensor_scalar_max(ot[:], ot[:], -8.0), ["ot"], ["ot"])
        p.op("v", lambda e: e.tensor_copy(xqb[:], ot[:]), ["ot"], ["xqb"])
        for f in range(8):
            p.op("t", lambda e, f=f: e.transpose(ptb[:], xqb[:, f*128:(f+1)*128], sid[:]), ["xqb", "sid"], ["ptb"])
            p.op("v", lambda e, f=f: e.tensor_copy(trb[:, f*128:(f+1)*128], ptb[:]), ["ptb"], ["trb"])
        p.dma("g", xq1o.rearrange("(f p) n -> p f n", p=128)[:, :, tt*128:(tt+1)*128],
              trb.rearrange("p (f n) -> p f n", f=8), ["trb"], ["xq1o"])
    p.op("t", lambda e: e.transpose(pmx[0:1, 0:128], gmax[:], sidf[:]), ["gmax", "sidf"], ["pmx"])
    p.op("v", lambda e: e.tensor_copy(mxf[:], pmx[0:1, 0:128]), ["pmx"], ["mxf"])
    p.op("v", lambda e: e.reduce_max(mxs[:], mxf[:], AX.X), ["mxf"], ["mxs"])
    p.dma("q", xmaxo, mxs[:], ["mxs"], ["xmaxo"])


def _emit_ternary(p, w_ap, key, sid, sidf, ones128f):
    wt = p.sb(key + "wt", [128, 8 * D])
    wa = p.sb(key + "wa", [128, 8 * D])
    acol = p.sb(key + "ac", [128, 8])
    asum = p.sb(key + "as", [128, 1])
    ps1 = p.ps(key + "p1", [1, 1])
    thr = p.sb(key + "th", [1, 1])
    thrc = p.sb(key + "tc", [128, 1])
    psc = p.ps(key + "pc", [128, 1])
    sgn = p.sb(key + "sg", [128, D])
    kpm = p.sb(key + "kp", [128, D])
    wtb = p.sb(key + "wb", [128, D], BF16)
    WtT = p.sb(key + "WT", [128, 8 * D], BF16)
    ptb = p.ps(key + "pt", [128, 128], BF16)
    p.dma("q", wt.rearrange("p (t c) -> p t c", t=8), w_ap.rearrange("(t p) c -> p t c", p=128),
          [key + "w"], [key + "wt"])
    p.op("s", lambda e: e.activation(wa[:], wt[:], AF.Abs), [key + "wt"], [key + "wa"])
    for t in range(8):
        p.op("v", lambda e, t=t: e.reduce_sum(acol[:, t:t+1], wa[:, t*D:(t+1)*D], AX.X),
             [key + "wa"], [key + "ac"])
    p.op("v", lambda e: e.tensor_scalar_mul(acol[:], acol[:], 1.0 / D), [key + "ac"], [key + "ac"])
    p.op("v", lambda e: e.reduce_sum(asum[:], acol[:], AX.X), [key + "ac"], [key + "as"])
    p.op("t", lambda e: e.matmul(ps1[:], ones128f[:], asum[:], start=True, stop=True),
         ["ones128f", key + "as"], [key + "p1"])
    p.op("v", lambda e: e.tensor_scalar(thr[:], ps1[:], 0.05 / 1024.0, None, ALU.mult),
         [key + "p1"], [key + "th"])
    p.op("t", lambda e: e.matmul(psc[:], ones128f[0:1, :].transpose([1, 0]) if False else thr[:], thr[:], start=True, stop=True),
         [key + "th"], [key + "pc"]) if False else None
    ones1f = p.sb(key + "o1", [1, 128])
    p.op("v", lambda e: e.memset(ones1f[:], 1.0), [], [key + "o1"])
    p.op("t", lambda e: e.matmul(psc[:], ones1f[:], thr[:], start=True, stop=True),
         [key + "o1", key + "th"], [key + "pc"])
    p.op("v", lambda e: e.tensor_copy(thrc[:], psc[:]), [key + "pc"], [key + "tc"])
    for t in range(8):
        p.op("s", lambda e, t=t: e.activation(sgn[:], wt[:, t*D:(t+1)*D], AF.Sign), [key + "wt"], [key + "sg"])
        p.op("v", lambda e, t=t: e.tensor_scalar(kpm[:], wa[:, t*D:(t+1)*D], thrc[:], None, ALU.is_ge),
             [key + "wa", key + "tc"], [key + "kp"])
        p.op("v", lambda e: e.tensor_mul(sgn[:], sgn[:], kpm[:]), [key + "sg", key + "kp"], [key + "sg"])
        p.op("v", lambda e: e.tensor_copy(wtb[:], sgn[:]), [key + "sg"], [key + "wb"])
        for f in range(8):
            p.op("t", lambda e, f=f: e.transpose(ptb[:], wtb[:, f*128:(f+1)*128], sid[:]),
                 [key + "wb", "sid"], [key + "pt"])
            p.op("v", lambda e, f=f, t=t: e.tensor_copy(WtT[:, f*D+t*128:f*D+(t+1)*128], ptb[:]),
                 [key + "pt"], [key + "WT"])
    return WtT, acol


def emit_LH(p):
    xq = p.din("xqkv", [D, TSH], BF16)
    ma = p.din("makv", [TSH])
    wq = p.din("LHw", [D, D])
    bq = p.din("LHb", [D])
    iden = p.din("iden", [128, 128], BF16)
    projo = p.dout("projkv", [D, TSH])
    sid = p.sb("sid", [128, 128], BF16)
    p.dma("q", sid[:], iden, ["iden"], ["sid"])
    ones128f = p.sb("ones128f", [128, 1])
    p.op("v", lambda e: e.memset(ones128f[:], 1.0), [], ["ones128f"])
    WtT, acol = _emit_ternary(p, wq, "LHw", sid, None, ones128f)
    p.op("v", lambda e: e.tensor_scalar_mul(acol[:], acol[:], 1.0 / 7.0), ["LHwac"], ["LHwac"])
    bcol = p.sb("bcol", [128, 8])
    p.dma("q", bcol[:], bq.rearrange("(m p) -> p m", p=128), ["LHb"], ["bcol"],
          allow_slow_non_contiguous=True)
    xqs = p.sb("xqs", [128, 8 * TSH], BF16)
    p.dma("q", xqs.rearrange("p (t n) -> p t n", t=8), xq.rearrange("(t p) n -> p t n", p=128),
          ["xqkv"], ["xqs"])
    mab = p.sb("mab", [128, 512])
    pmm = p.ps("pmm", [128, 1024])
    t1 = p.sb("t1", [128, 512])
    for ch in range(4):
        p.dma("q", mab[:], ma[ch*512:(ch+1)*512].unsqueeze(0).to_broadcast([128, 512]),
              ["makv"], ["mab"])
        for oc in range(8):
            sl = oc % 2
            for kt in range(8):
                p.op("t", lambda e, oc=oc, kt=kt, ch=ch, sl=sl:
                     e.matmul(pmm[:, sl*512:(sl+1)*512], WtT[:, kt*D+oc*128:kt*D+(oc+1)*128],
                              xqs[:, kt*TSH+ch*512:kt*TSH+ch*512+512], start=(kt == 0), stop=(kt == 7)),
                     ["LHwWT", "xqs"], [f"pmm{sl}"], "p" if kt else "c")
            p.op("v", lambda e, oc=oc, sl=sl: e.tensor_scalar(t1[:], pmm[:, sl*512:(sl+1)*512],
                                                              acol[:, oc:oc+1], None, ALU.mult),
                 [f"pmm{sl}", "LHwac"], ["t1"])
            p.op("v", lambda e: e.tensor_mul(t1[:], t1[:], mab[:]), ["t1", "mab"], ["t1"])
            p.op("v", lambda e, oc=oc: e.tensor_scalar(t1[:], t1[:], bcol[:, oc:oc+1], None, ALU.add),
                 ["t1", "bcol"], ["t1"])
            p.dma("q", projo[oc*128:(oc+1)*128, ch*512:(ch+1)*512], t1[:], ["t1"], ["projo"])


def emit_LI(p):
    qin = p.din("qin", [512, LSEQ])
    kvin = p.din("kvin", [D, 2048])
    U = p.din("U", [64, 32])
    cosq = p.din("cosq", [LSEQ, 32]); sinq = p.din("sinq", [LSEQ, 32])
    cosk = p.din("cosk", [LSEQ, 32]); sink = p.din("sink", [LSEQ, 32])
    iden = p.din("iden", [128, 128], BF16)
    idenf = p.din("idenf", [128, 128])
    y1c = p.dout("y1c", [512, LSEQ])
    sid = p.sb("sid", [128, 128], BF16)
    sidf = p.sb("sidf", [128, 128])
    p.dma("q", sid[:], iden, ["iden"], ["sid"])
    p.dma("q", sidf[:], idenf, ["idenf"], ["sidf"])
    Ub = p.sb("Ub", [64, 32], BF16)
    p.dma("g", Ub[:], U, ["U"], ["Ub"])
    tabs = {}
    for nm, ap in (("cosq", cosq), ("sinq", sinq), ("cosk", cosk), ("sink", sink)):
        t = p.sb("s" + nm, [128, 32 * 32])
        p.dma("q", t.rearrange("p (n c) -> p n c", n=32), ap.rearrange("(n p) c -> p n c", p=128),
              [nm], ["s" + nm])
        tabs[nm] = t
    hfmb = p.sb("hfmb", [64, LSEQ], BF16)
    hT = p.sb("hT", [128, 32 * 64], BF16)
    rT = p.sb("rT", [128, 32 * 64], BF16)
    ptb = p.ps("ptb", [128, 128], BF16)
    rfm = p.sb("rfm", [64, LSEQ], BF16)
    qlf = p.sb("qlf", [32, LSEQ], BF16)
    klf = p.sb("klf", [32, LSEQ], BF16)
    vte = p.sb("vte", [128, 32 * 65], BF16)
    pU = p.ps("pU", [32, 512])
    pS = [p.ps(f"pS{i}", [128, 512]) for i in range(2)]
    pt_ = [p.sb(f"pt_{i}", [128, 512], BF16) for i in range(2)]
    pY = p.ps("pY", [65, 512])
    pR = p.ps("pR", [64, 512])
    ones1 = p.sb("ones1", [1, 64])
    p.op("v", lambda e: e.memset(ones1[:], 1.0), [], ["ones1"])
    rs = p.sb("rs", [1, 512])
    rsb = p.sb("rsb", [64, 512])
    yo = p.sb("yo", [64, 512])

    def rot_path(src_key, loader, ctab, stab, dst, dstkey):
        loader()
        for nt in range(32):
            p.op("t", lambda e, nt=nt: e.transpose(ptb[0:128, 0:64], hfmb[:, nt*128:(nt+1)*128], sid[0:64, 0:64]),
                 [src_key, "sid"], ["ptb"])
            p.op("v", lambda e, nt=nt: e.tensor_copy(hT[:, nt*64:(nt+1)*64], ptb[0:128, 0:64]),
                 ["ptb"], ["hT"])
        for nt in range(32):
            ev = hT[:, nt*64:(nt+1)*64].rearrange("p (c two) -> p c two", two=2)[:, :, 0]
            od = hT[:, nt*64:(nt+1)*64].rearrange("p (c two) -> p c two", two=2)[:, :, 1]
            ct = ctab[:, nt*32:(nt+1)*32]
            st_ = stab[:, nt*32:(nt+1)*32]
            p.op("v", lambda e, ev=ev, ct=ct, nt=nt: e.tensor_mul(rT[:, nt*64:nt*64+32], ev, ct),
                 ["hT", "ctab"], ["rT"])
            p.op("v", lambda e, od=od, st_=st_, nt=nt: e.tensor_mul(scr32[:], od, st_),
                 ["hT", "stab"], ["scr32"])
            p.op("v", lambda e, nt=nt: e.tensor_sub(rT[:, nt*64:nt*64+32], rT[:, nt*64:nt*64+32], scr32[:]),
                 ["rT", "scr32"], ["rT"])
            p.op("v", lambda e, ev=ev, st_=st_, nt=nt: e.tensor_mul(rT[:, nt*64+32:nt*64+64], ev, st_),
                 ["hT", "stab"], ["rT"])
            p.op("v", lambda e, od=od, ct=ct, nt=nt: e.tensor_mul(scr32[:], od, ct),
                 ["hT", "ctab"], ["scr32"])
            p.op("v", lambda e, nt=nt: e.tensor_add(rT[:, nt*64+32:nt*64+64], rT[:, nt*64+32:nt*64+64], scr32[:]),
                 ["rT", "scr32"], ["rT"])
        for nt in range(32):
            p.op("t", lambda e, nt=nt: e.transpose(ptb[0:64, 0:128], rT[:, nt*64:(nt+1)*64], sid[:]),
                 ["rT", "sid"], ["ptb"])
            p.op("v", lambda e, nt=nt: e.tensor_copy(dst[:, nt*128:(nt+1)*128], ptb[0:64, 0:128]),
                 ["ptb"], [dstkey])

    scr32 = p.sb("scr32", [128, 32], BF16)
    for hp in range(8):
        p.dma("g", hfmb[:], qin[hp*64:(hp+1)*64, :], ["qin"], ["hfmb"])
        rot_path("hfmb", lambda: None, tabs["cosq"], tabs["sinq"], rfm, "rfm")
        for ch in range(8):
            p.op("t", lambda e, ch=ch: e.matmul(pU[:], Ub[:], rfm[:, ch*512:(ch+1)*512], start=True, stop=True),
                 ["Ub", "rfm"], ["pU"])
            p.op("v", lambda e, ch=ch: e.tensor_copy(qlf[:, ch*512:(ch+1)*512], pU[:]), ["pU"], ["qlf"])
        kview = kvin.rearrange("(jc dd) n -> dd jc n", dd=64)
        p.dma("g", hfmb.rearrange("p (jc nm) -> p jc nm", jc=16),
              kview[:, :, hp*256:(hp+1)*256], ["kvin"], ["hfmb"])
        rot_path("hfmb", lambda: None, tabs["cosk"], tabs["sink"], rfm, "rfm")
        for ch in range(8):
            p.op("t", lambda e, ch=ch: e.matmul(pU[:], Ub[:], rfm[:, ch*512:(ch+1)*512], start=True, stop=True),
                 ["Ub", "rfm"], ["pU"])
            p.op("v", lambda e, ch=ch: e.tensor_copy(klf[:, ch*512:(ch+1)*512], pU[:]), ["pU"], ["klf"])
        p.dma("g", hfmb.rearrange("p (jc nm) -> p jc nm", jc=16),
              kview[:, :, hp*256:(hp+1)*256], ["kvin"], ["hfmb"])
        for nt in range(32):
            p.op("t", lambda e, nt=nt: e.transpose(ptb[0:128, 0:64], hfmb[:, nt*128:(nt+1)*128], sid[0:64, 0:64]),
                 ["hfmb", "sid"], ["ptb"])
            p.op("v", lambda e, nt=nt: e.tensor_copy(vte[:, nt*65:nt*65+64], ptb[0:128, 0:64]),
                 ["ptb"], ["vte"])
        p.op("v", lambda e: e.memset(vte.rearrange("p (k c) -> p k c", k=32)[:, :, 64:65], 1.0),
             ["vte"], ["vte"])
        for qc in range(8):
            def emit_S(kt, qc=qc):
                i = kt % 2
                p.op("t", lambda e, kt=kt, qc=qc, i=i:
                     e.matmul(pS[i][:], klf[:, kt*128:(kt+1)*128], qlf[:, qc*512:(qc+1)*512],
                              start=True, stop=True), ["klf", "qlf"], [f"pS{i}"])
                p.op("s", lambda e, i=i: e.activation(pt_[i][:], pS[i][:], AF.Exp, scale=0.125),
                     [f"pS{i}"], [f"pt_{i}"])

            def emit_PV(kt):
                i = kt % 2
                p.op("t", lambda e, kt=kt, i=i: e.matmul(pY[:], vte[:, kt*65:(kt+1)*65], pt_[i][:],
                                                         start=(kt == 0), stop=(kt == 31)),
                     ["vte", f"pt_{i}"], ["pY"], "p" if kt else "c")

            emit_S(0)
            for kt in range(1, 32):
                emit_S(kt)
                emit_PV(kt - 1)
            emit_PV(31)
            p.op("v", lambda e: e.reciprocal(rs[:], pY[64:65, :]), ["pY"], ["rs"])
            p.op("t", lambda e: e.matmul(pR[:], ones1[:], rs[:], start=True, stop=True),
                 ["ones1", "rs"], ["pR"])
            p.op("s", lambda e: e.copy(rsb[:], pR[:]), ["pR"], ["rsb"])
            p.op("v", lambda e: e.tensor_mul(yo[:], pY[0:64, :], rsb[:]), ["pY", "rsb"], ["yo"])
            p.dma("q", y1c[hp*64:(hp+1)*64, qc*512:(qc+1)*512], yo[:], ["yo"], ["y1c"])


def emit_LJ(p):
    y1 = p.din("y1own", [D, TSH])
    wo = p.din("LJw", [D, D])
    iden = p.din("iden", [128, 128], BF16)
    idenf = p.din("idenf", [128, 128])
    s1o = p.dout("s1_fm", [D, TSH])
    ma2o = p.dout("ma2", [TSH])
    sid = p.sb("sid", [128, 128], BF16)
    sidf = p.sb("sidf", [128, 128])
    p.dma("q", sid[:], iden, ["iden"], ["sid"])
    p.dma("q", sidf[:], idenf, ["idenf"], ["sidf"])
    ones128f = p.sb("ones128f", [128, 1])
    p.op("v", lambda e: e.memset(ones128f[:], 1.0), [], ["ones128f"])
    WtT, acol = _emit_ternary(p, wo, "LJw", sid, None, ones128f)
    p.op("v", lambda e: e.tensor_scalar_mul(acol[:], acol[:], 1.0 / 7.0), ["LJwac"], ["LJwac"])
    ytm = p.sb("ytm", [128, D])
    ptf = p.ps("ptf2", [128, 128])
    ptb = p.ps("ptb2", [128, 128], BF16)
    mx = p.sb("mx2", [128, 1]); mc = p.sb("mc2", [128, 1])
    xqb = p.sb("xqb2", [128, D], BF16)
    xqf = p.sb("xqf2", [128, 8 * TSH], BF16)
    y1s = p.sb("y1s", [128, 8 * 128])
    for tt in range(TSH // 128):
        p.dma("q", y1s.rearrange("p (t n) -> p t n", t=8),
              y1.rearrange("(t p) n -> p t n", p=128)[:, :, tt*128:(tt+1)*128], ["y1own"], ["y1s"])
        for f in range(8):
            p.op("t", lambda e, f=f: e.transpose(ptf[:], y1s[:, f*128:(f+1)*128], sidf[:]),
                 ["y1s", "sidf"], ["ptfk"])
            p.op("v", lambda e, f=f: e.tensor_copy(ytm[:, f*128:(f+1)*128], ptf[:]), ["ptfk"], ["ytm"])
        p.op("v", lambda e: e.reduce_max(mx[:], ytm[:], AX.X, apply_absolute_value=True), ["ytm"], ["mx2"])
        p.op("v", lambda e: e.tensor_scalar_max(mx[:], mx[:], 1e-6), ["mx2"], ["mx2"])
        p.dma("q", ma2o[tt*128:(tt+1)*128].unsqueeze(1), mx[:], ["mx2"], ["ma2o"])
        p.op("v", lambda e: e.reciprocal(mc[:], mx[:]), ["mx2"], ["mc2"])
        p.op("v", lambda e: e.tensor_scalar(mc[:], mc[:], 7.0, None, ALU.mult), ["mc2"], ["mc2"])
        p.op("v", lambda e: e.tensor_scalar(ytm[:], ytm[:], mc[:], RBIG, ALU.mult, ALU.add), ["ytm", "mc2"], ["ytm"])
        p.op("v", lambda e: e.tensor_scalar(ytm[:], ytm[:], RBIG, None, ALU.subtract), ["ytm"], ["ytm"])
        p.op("v", lambda e: e.tensor_scalar_min(ytm[:], ytm[:], 7.0), ["ytm"], ["ytm"])
        p.op("v", lambda e: e.tensor_scalar_max(ytm[:], ytm[:], -8.0), ["ytm"], ["ytm"])
        p.op("v", lambda e: e.tensor_copy(xqb[:], ytm[:]), ["ytm"], ["xqb2"])
        for f in range(8):
            p.op("t", lambda e, f=f: e.transpose(ptb[:], xqb[:, f*128:(f+1)*128], sid[:]), ["xqb2", "sid"], ["ptbk"])
            p.op("v", lambda e, f=f, tt=tt: e.tensor_copy(xqf[:, f*TSH+tt*128:f*TSH+(tt+1)*128], ptb[:]),
                 ["ptbk"], ["xqf2"])
    mab = p.sb("mab2", [128, 512])
    pmm = p.ps("pmm2", [128, 1024])
    t1 = p.sb("t12", [128, 512])
    for ch in range(4):
        p.dma("q", mab[:], ma2o[ch*512:(ch+1)*512].unsqueeze(0).to_broadcast([128, 512]),
              ["ma2o"], ["mab2"])
        for oc in range(8):
            sl = oc % 2
            for kt in range(8):
                p.op("t", lambda e, oc=oc, kt=kt, ch=ch, sl=sl:
                     e.matmul(pmm[:, sl*512:(sl+1)*512], WtT[:, kt*D+oc*128:kt*D+(oc+1)*128],
                              xqf[:, kt*TSH+ch*512:kt*TSH+ch*512+512], start=(kt == 0), stop=(kt == 7)),
                     ["LJwWT", "xqf2"], [f"pm2{sl}"], "p" if kt else "c")
            p.op("v", lambda e, oc=oc, sl=sl: e.tensor_scalar(t1[:], pmm[:, sl*512:(sl+1)*512],
                                                              acol[:, oc:oc+1], None, ALU.mult),
                 [f"pm2{sl}", "LJwac"], ["t12"])
            p.op("v", lambda e: e.tensor_mul(t1[:], t1[:], mab[:]), ["t12", "mab2"], ["t12"])
            p.dma("q", s1o[oc*128:(oc+1)*128, ch*512:(ch+1)*512], t1[:], ["t12"], ["s1o"])


def _emit_globmax_rows(p, key, src_tiles_fn, ntiles, sidf, out_ap):
    pass

def emit_LK1(p):
    xn1 = p.din("xn1own", [TSH, D])
    xmax8 = p.din("xmax8", [8, 1])
    w1 = p.din("L1w1", [4096, D])
    b1 = p.din("L1b1", [4096])
    iden = p.din("iden", [128, 128], BF16)
    idenf = p.din("idenf", [128, 128])
    hfmo = p.dout("h_fmb", [4096, TSH], BF16)
    hmaxo = p.dout("hmaxc", [1, 1])
    sid = p.sb("sid", [128, 128], BF16)
    sidf = p.sb("sidf", [128, 128])
    p.dma("q", sid[:], iden, ["iden"], ["sid"])
    p.dma("q", sidf[:], idenf, ["idenf"], ["sidf"])
    ones1f = p.sb("ones1f", [1, 128])
    p.op("v", lambda e: e.memset(ones1f[:], 1.0), [], ["ones1f"])
    xm8 = p.sb("xm8", [1, 8])
    p.dma("q", xm8[:], xmax8.rearrange("a b -> b a"), ["xmax8"], ["xm8"])
    sxs = p.sb("sxs", [1, 1])
    p.op("v", lambda e: e.reduce_max(sxs[:], xm8[:], AX.X), ["xm8"], ["sxs"])
    p.op("v", lambda e: e.tensor_scalar_max(sxs[:], sxs[:], 1e-6), ["sxs"], ["sxs"])
    p.op("v", lambda e: e.tensor_scalar_mul(sxs[:], sxs[:], 1.0 / 127.0), ["sxs"], ["sxs"])
    sxi = p.sb("sxi", [1, 1])
    p.op("v", lambda e: e.reciprocal(sxi[:], sxs[:]), ["sxs"], ["sxi"])
    psk = p.ps("psk", [128, 1])
    sxc = p.sb("sxc", [128, 1])
    p.op("t", lambda e: e.matmul(psk[:], ones1f[:], sxi[:], start=True, stop=True),
         ["ones1f", "sxi"], ["psk"])
    p.op("v", lambda e: e.tensor_copy(sxc[:], psk[:]), ["psk"], ["sxc"])
    wt = p.sb("wt", [128, D])
    wmax = p.sb("wmax", [128, 1])
    mxt = p.sb("mxt", [128, 1])
    for i in range(32):
        p.dma("q", wt[:], w1[i*128:(i+1)*128, :], ["L1w1"], ["wt"])
        if i == 0:
            p.op("v", lambda e: e.reduce_max(wmax[:], wt[:], AX.X, apply_absolute_value=True),
                 ["wt"], ["wmax"])
        else:
            p.op("v", lambda e: e.reduce_max(mxt[:], wt[:], AX.X, apply_absolute_value=True),
                 ["wt"], ["mxt"])
            p.op("v", lambda e: e.tensor_max(wmax[:], wmax[:], mxt[:]), ["wmax", "mxt"], ["wmax"])
    pmx = p.ps("pmx", [1, 128])
    mxr = p.sb("mxr", [1, 128])
    sw1 = p.sb("sw1", [1, 1])
    p.op("t", lambda e: e.transpose(pmx[:], wmax[:], sidf[:]), ["wmax", "sidf"], ["pmx"])
    p.op("v", lambda e: e.tensor_copy(mxr[:], pmx[:]), ["pmx"], ["mxr"])
    p.op("v", lambda e: e.reduce_max(sw1[:], mxr[:], AX.X), ["mxr"], ["sw1"])
    p.op("v", lambda e: e.tensor_scalar_max(sw1[:], sw1[:], 1e-6), ["sw1"], ["sw1"])
    p.op("v", lambda e: e.tensor_scalar_mul(sw1[:], sw1[:], 1.0 / 127.0), ["sw1"], ["sw1"])
    sw1i = p.sb("sw1i", [1, 1])
    p.op("v", lambda e: e.reciprocal(sw1i[:], sw1[:]), ["sw1"], ["sw1i"])
    sw1c = p.sb("sw1c", [128, 1])
    p.op("t", lambda e: e.matmul(psk[:], ones1f[:], sw1i[:], start=True, stop=True),
         ["ones1f", "sw1i"], ["psk"])
    p.op("v", lambda e: e.tensor_copy(sw1c[:], psk[:]), ["psk"], ["sw1c"])
    ssx = p.sb("ssx", [1, 1])
    p.op("v", lambda e: e.tensor_mul(ssx[:], sw1[:], sxs[:]), ["sw1", "sxs"], ["ssx"])
    ssxc = p.sb("ssxc", [128, 1])
    p.op("t", lambda e: e.matmul(psk[:], ones1f[:], ssx[:], start=True, stop=True),
         ["ones1f", "ssx"], ["psk"])
    p.op("v", lambda e: e.tensor_copy(ssxc[:], psk[:]), ["psk"], ["ssxc"])
    w1qT = p.sb("w1qT", [128, 8 * 4096], BF16)
    wq = p.sb("wq", [128, D], BF16)
    ptb = p.ps("ptbk", [128, 128], BF16)
    for i in range(32):
        p.dma("q", wt[:], w1[i*128:(i+1)*128, :], ["L1w1"], ["wt"])
        p.op("v", lambda e: e.tensor_scalar(wt[:], wt[:], sw1c[:], RBIG, ALU.mult, ALU.add), ["wt", "sw1c"], ["wt"])
        p.op("v", lambda e: e.tensor_scalar(wt[:], wt[:], RBIG, None, ALU.subtract), ["wt"], ["wt"])
        p.op("v", lambda e: e.tensor_copy(wq[:], wt[:]), ["wt"], ["wq"])
        for f in range(8):
            p.op("t", lambda e, f=f: e.transpose(ptb[:], wq[:, f*128:(f+1)*128], sid[:]), ["wq", "sid"], ["ptk"])
            p.op("v", lambda e, f=f, i=i: e.tensor_copy(w1qT[:, f*4096+i*128:f*4096+(i+1)*128], ptb[:]),
                 ["ptk"], ["w1qT"])
    xqf = p.sb("xqfk", [128, 8 * TSH], BF16)
    xt = p.sb("xtk", [128, D])
    xqb = p.sb("xqbk", [128, D], BF16)
    for tt in range(TSH // 128):
        p.dma("q", xt[:], xn1[tt*128:(tt+1)*128, :], ["xn1own"], ["xtk"])
        p.op("v", lambda e: e.tensor_scalar(xt[:], xt[:], sxc[:], RBIG, ALU.mult, ALU.add), ["xtk", "sxc"], ["xtk"])
        p.op("v", lambda e: e.tensor_scalar(xt[:], xt[:], RBIG, None, ALU.subtract), ["xtk"], ["xtk"])
        p.op("v", lambda e: e.tensor_copy(xqb[:], xt[:]), ["xtk"], ["xqbk"])
        for f in range(8):
            p.op("t", lambda e, f=f: e.transpose(ptb[:], xqb[:, f*128:(f+1)*128], sid[:]), ["xqbk", "sid"], ["ptk"])
            p.op("v", lambda e, f=f, tt=tt: e.tensor_copy(xqf[:, f*TSH+tt*128:f*TSH+(tt+1)*128], ptb[:]),
                 ["ptk"], ["xqfk"])
    b1c = p.sb("b1ck", [128, 32])
    p.dma("q", b1c[:], b1.rearrange("(m p) -> p m", p=128), ["L1b1"], ["b1ck"],
          allow_slow_non_contiguous=True)
    pmm = p.ps("pmmk", [128, 1024])
    hb = p.sb("hbk", [128, 512], BF16)
    hmax = p.sb("hmaxk", [128, 1])
    hm1 = p.sb("hm1k", [128, 1])
    first = [True]
    for mt in range(32):
        for ch in range(4):
            sl = ch % 2
            for kt in range(8):
                p.op("t", lambda e, mt=mt, ch=ch, kt=kt, sl=sl:
                     e.matmul(pmm[:, sl*512:(sl+1)*512], w1qT[:, kt*4096+mt*128:kt*4096+(mt+1)*128],
                              xqf[:, kt*TSH+ch*512:kt*TSH+ch*512+512], start=(kt == 0), stop=(kt == 7)),
                     ["w1qT", "xqfk"], [f"pmk{sl}"], "p" if kt else "c")
            p.op("s", lambda e, mt=mt, sl=sl: e.activation(hb[:], pmm[:, sl*512:(sl+1)*512], AF.Gelu,
                                                           bias=b1c[:, mt:mt+1], scale=ssxc[:]),
                 [f"pmk{sl}", "b1ck", "ssxc"], ["hbk"])
            p.op("v", lambda e: e.reduce_max(hm1[:], hb[:], AX.X, apply_absolute_value=True), ["hbk"], ["hm1k"])
            if first[0]:
                p.op("v", lambda e: e.tensor_copy(hmax[:], hm1[:]), ["hm1k"], ["hmaxk"])
                first[0] = False
            else:
                p.op("v", lambda e: e.tensor_max(hmax[:], hmax[:], hm1[:]), ["hmaxk", "hm1k"], ["hmaxk"])
            p.dma("q", hfmo[mt*128:(mt+1)*128, ch*512:(ch+1)*512], hb[:], ["hbk"], ["hfmo"])
    p.op("t", lambda e: e.transpose(pmx[:], hmax[:], sidf[:]), ["hmaxk", "sidf"], ["pmx"])
    p.op("v", lambda e: e.tensor_copy(mxr[:], pmx[:]), ["pmx"], ["mxr"])
    p.op("v", lambda e: e.reduce_max(sw1[:], mxr[:], AX.X), ["mxr"], ["hms"])
    p.dma("q", hmaxo, sw1[:], ["hms"], ["hmaxo"])


def emit_LK2(p):
    hfm = p.din("h_own", [4096, TSH], BF16)
    hmax8 = p.din("hmax8", [8, 1])
    w2 = p.din("L1w2", [D, 4096])
    b2 = p.din("L1b2", [D])
    s1 = p.din("s1own", [D, TSH])
    x1 = p.din("x1own", [TSH, D])
    iden = p.din("iden", [128, 128], BF16)
    idenf = p.din("idenf", [128, 128])
    outo = p.dout("out_tm", [TSH, D])
    sid = p.sb("sid", [128, 128], BF16)
    sidf = p.sb("sidf", [128, 128])
    p.dma("q", sid[:], iden, ["iden"], ["sid"])
    p.dma("q", sidf[:], idenf, ["idenf"], ["sidf"])
    ones1f = p.sb("ones1f", [1, 128])
    p.op("v", lambda e: e.memset(ones1f[:], 1.0), [], ["ones1f"])
    psk = p.ps("psk", [128, 1])
    hm8 = p.sb("hm8", [1, 8])
    p.dma("q", hm8[:], hmax8.rearrange("a b -> b a"), ["hmax8"], ["hm8"])
    shs = p.sb("shs", [1, 1])
    p.op("v", lambda e: e.reduce_max(shs[:], hm8[:], AX.X), ["hm8"], ["shs"])
    p.op("v", lambda e: e.tensor_scalar_max(shs[:], shs[:], 1e-6), ["shs"], ["shs"])
    p.op("v", lambda e: e.tensor_scalar_mul(shs[:], shs[:], 1.0 / 127.0), ["shs"], ["shs"])
    shi = p.sb("shi", [1, 1])
    p.op("v", lambda e: e.reciprocal(shi[:], shs[:]), ["shs"], ["shi"])
    shc = p.sb("shc", [128, 1])
    p.op("t", lambda e: e.matmul(psk[:], ones1f[:], shi[:], start=True, stop=True),
         ["ones1f", "shi"], ["psk"])
    p.op("v", lambda e: e.tensor_copy(shc[:], psk[:]), ["psk"], ["shc"])
    wt = p.sb("wtq", [128, 4096])
    wmax = p.sb("wmaxq", [128, 1]); mxt = p.sb("mxtq", [128, 1])
    for i in range(8):
        p.dma("q", wt[:], w2[i*128:(i+1)*128, :], ["L1w2"], ["wtq"])
        if i == 0:
            p.op("v", lambda e: e.reduce_max(wmax[:], wt[:], AX.X, apply_absolute_value=True), ["wtq"], ["wmaxq"])
        else:
            p.op("v", lambda e: e.reduce_max(mxt[:], wt[:], AX.X, apply_absolute_value=True), ["wtq"], ["mxtq"])
            p.op("v", lambda e: e.tensor_max(wmax[:], wmax[:], mxt[:]), ["wmaxq", "mxtq"], ["wmaxq"])
    pmx = p.ps("pmxq", [1, 128])
    mxr = p.sb("mxrq", [1, 128])
    sw2 = p.sb("sw2", [1, 1])
    p.op("t", lambda e: e.transpose(pmx[:], wmax[:], sidf[:]), ["wmaxq", "sidf"], ["pmxq"])
    p.op("v", lambda e: e.tensor_copy(mxr[:], pmx[:]), ["pmxq"], ["mxrq"])
    p.op("v", lambda e: e.reduce_max(sw2[:], mxr[:], AX.X), ["mxrq"], ["sw2"])
    p.op("v", lambda e: e.tensor_scalar_max(sw2[:], sw2[:], 1e-6), ["sw2"], ["sw2"])
    p.op("v", lambda e: e.tensor_scalar_mul(sw2[:], sw2[:], 1.0 / 127.0), ["sw2"], ["sw2"])
    sw2i = p.sb("sw2i", [1, 1])
    p.op("v", lambda e: e.reciprocal(sw2i[:], sw2[:]), ["sw2"], ["sw2i"])
    sw2c = p.sb("sw2c", [128, 1])
    p.op("t", lambda e: e.matmul(psk[:], ones1f[:], sw2i[:], start=True, stop=True),
         ["ones1f", "sw2i"], ["psk"])
    p.op("v", lambda e: e.tensor_copy(sw2c[:], psk[:]), ["psk"], ["sw2c"])
    ssx = p.sb("ssxq", [1, 1])
    p.op("v", lambda e: e.tensor_mul(ssx[:], sw2[:], shs[:]), ["sw2", "shs"], ["ssxq"])
    ssxc = p.sb("ssxcq", [128, 1])
    p.op("t", lambda e: e.matmul(psk[:], ones1f[:], ssx[:], start=True, stop=True),
         ["ones1f", "ssxq"], ["psk"])
    p.op("v", lambda e: e.tensor_copy(ssxc[:], psk[:]), ["psk"], ["ssxcq"])
    b2c = p.sb("b2cq", [128, 8])
    p.dma("q", b2c[:], b2.rearrange("(m p) -> p m", p=128), ["L1b2"], ["b2cq"],
          allow_slow_non_contiguous=True)
    b2s = p.sb("b2sq", [128, 8])
    p.op("v", lambda e: e.tensor_scalar(b2s[:], b2c[:], ssxc[:], None, ALU.mult), ["b2cq", "ssxcq"], ["b2sq"])
    w2qT = p.sb("w2qT", [128, 32 * D], BF16)
    wqb = p.sb("wqbq", [128, 4096], BF16)
    ptb = p.ps("ptbq", [128, 128], BF16)
    for i in range(8):
        p.dma("q", wt[:], w2[i*128:(i+1)*128, :], ["L1w2"], ["wtq"])
        p.op("v", lambda e: e.tensor_scalar(wt[:], wt[:], sw2c[:], RBIG, ALU.mult, ALU.add), ["wtq", "sw2c"], ["wtq"])
        p.op("v", lambda e: e.tensor_scalar(wt[:], wt[:], RBIG, None, ALU.subtract), ["wtq"], ["wtq"])
        p.op("v", lambda e: e.tensor_copy(wqb[:], wt[:]), ["wtq"], ["wqbq"])
        for f in range(32):
            p.op("t", lambda e, f=f: e.transpose(ptb[:], wqb[:, f*128:(f+1)*128], sid[:]), ["wqbq", "sid"], ["ptq"])
            p.op("v", lambda e, f=f, i=i: e.tensor_copy(w2qT[:, f*D+i*128:f*D+(i+1)*128], ptb[:]),
                 ["ptq"], ["w2qT"])
    hqb = p.sb("hqb", [128, 32 * 512], BF16)
    hq1 = p.sb("hq1", [128, 512])
    pmm = p.ps("pmmq", [128, 1024])
    g1 = p.sb("g1q", [128, 512])
    sgf = p.sb("sgf", [128, 512])
    s1s = p.sb("s1sq", [128, 512])
    sgtm = p.sb("sgtm", [128, 4 * D])
    ptf = p.ps("ptfq", [128, 128])
    xt = p.sb("xtq", [128, D])
    for ch in range(4):
        for mt in range(32):
            p.dma("g", hq1[:], hfm[mt*128:(mt+1)*128, ch*512:(ch+1)*512], ["h_own"], ["hq1"])
            p.op("v", lambda e, mt=mt: e.tensor_scalar(hq1[:], hq1[:], shc[:], RBIG, ALU.mult, ALU.add),
                 ["hq1", "shc"], ["hq1"])
            p.op("v", lambda e: e.tensor_scalar(hq1[:], hq1[:], RBIG, None, ALU.subtract), ["hq1"], ["hq1"])
            p.op("v", lambda e, mt=mt: e.tensor_copy(hqb[:, mt*512:(mt+1)*512], hq1[:]), ["hq1"], ["hqb"])
        for oc in range(8):
            sl = oc % 2
            for mt in range(32):
                p.op("t", lambda e, oc=oc, mt=mt, sl=sl:
                     e.matmul(pmm[:, sl*512:(sl+1)*512], w2qT[:, mt*D+oc*128:mt*D+(oc+1)*128],
                              hqb[:, mt*512:(mt+1)*512], start=(mt == 0), stop=(mt == 31)),
                     ["w2qT", "hqb"], [f"pq{sl}"], "p" if mt else "c")
            p.op("s", lambda e, oc=oc, sl=sl: e.activation(g1[:], pmm[:, sl*512:(sl+1)*512], AF.Silu,
                                                           bias=b2s[:, oc:oc+1], scale=ssxc[:]),
                 [f"pq{sl}", "b2sq", "ssxcq"], ["g1q"])
            p.dma("q", s1s[:], s1[oc*128:(oc+1)*128, ch*512:(ch+1)*512], ["s1own"], ["s1sq"])
            p.op("v", lambda e: e.tensor_mul(sgf[:], g1[:], s1s[:]), ["g1q", "s1sq"], ["sgf"])
            for cc in range(4):
                p.op("t", lambda e, cc=cc: e.transpose(ptf[:], sgf[:, cc*128:(cc+1)*128], sidf[:]),
                     ["sgf", "sidf"], ["ptfq2"])
                p.op("v", lambda e, cc=cc, oc=oc: e.tensor_copy(sgtm[:, cc*D+oc*128:cc*D+(oc+1)*128], ptf[:]),
                     ["ptfq2"], ["sgtm"])
        for cc in range(4):
            tt = ch * 4 + cc
            p.dma("q", xt[:], x1[tt*128:(tt+1)*128, :], ["x1own"], ["xtq"])
            p.op("v", lambda e, cc=cc: e.tensor_add(xt[:], xt[:], sgtm[:, cc*D:(cc+1)*D]),
                 ["xtq", "sgtm"], ["xtq"])
            p.dma("q", outo[tt*128:(tt+1)*128, :], xt[:], ["xtq"], ["outo"])


def _consts():
    import ml_dtypes
    BFn = ml_dtypes.bfloat16
    iden = np.eye(128).astype(BFn)
    idenf = np.eye(128, dtype=np.float32)
    cmask = np.zeros((4, 128, 512), np.float32)
    for m in range(4):
        cmask[m] = (np.arange(512)[None, :] >= (np.arange(128)[:, None] + m * 128))
    utri = np.triu(np.ones((128, 128), np.float32), 1).astype(BFn)
    half = 32
    inv = 1.0 / (10000.0 ** (np.arange(half, dtype=np.float32) / half))
    ang = np.arange(LSEQ, dtype=np.float32)[:, None] * inv[None, :]
    cosq = np.cos(ang).astype(np.float32)
    sinq = np.sin(ang).astype(np.float32)
    cc = np.arange(LSEQ)
    nprime = (cc % 256) * 16 + cc // 256
    cosk = cosq[nprime]
    sink = sinq[nprime]
    return dict(iden=iden, idenf=idenf, cmask=cmask.astype(BFn), utri=utri,
                cosq=cosq, sinq=sinq, cosk=cosk, sink=sink)

def kernel(**inp):
    C = _consts()
    ar = np.ascontiguousarray
    inp.get("L0_ffn_token_mask")
    X2 = ar(np.asarray(inp["L0_attn_x"]).reshape(NTOK, D))
    Wqkv = np.asarray(inp["L0_attn_W_qkv"])
    WL = np.asarray(inp["L0_attn_W_LTM"])
    pars = {k: np.asarray(inp["L0_attn_" + k]) for k in
            ("V_gs", "V_T0", "beta_tau", "beta_gm", "C_ch", "gamma",
             "alpha_ppd", "IC_threshold", "gate")}
    Wo0 = np.asarray(inp["L0_attn_W_o"])
    Wr_ = np.asarray(inp["L0_ffn_Wr"]); br_ = np.asarray(inp["L0_ffn_br"])
    W1m = np.asarray(inp["L0_ffn_W1"]); W2m = np.asarray(inp["L0_ffn_W2"])
    b1m = np.asarray(inp["L0_ffn_b1"]); b2m = np.asarray(inp["L0_ffn_b2"])
    gam0 = np.asarray(inp["L0_ffn_gamma"]); bet0 = np.asarray(inp["L0_ffn_beta"])
    Wq1 = np.asarray(inp["L1_attn_Wqkv_w"]); bq1 = np.asarray(inp["L1_attn_Wqkv_b"])
    U1 = np.asarray(inp["L1_attn_U"]); Wo1 = np.asarray(inp["L1_attn_Wo_w"])
    w1f_ = np.asarray(inp["L1_ffn_w1"]); b1f_ = np.asarray(inp["L1_ffn_b1"])
    w2f_ = np.asarray(inp["L1_ffn_w2"]); b2f_ = np.asarray(inp["L1_ffn_b2"])

    p = P("MK", fused=True)
    p.final_outputs = {"out_tm"}
    RG8 = [list(range(NCORE))]
    RG2 = [[0, 1], [2, 3], [4, 5], [6, 7]]
    AG, RSC = "AllGather", "ReduceScatter"
    I = p.internals

    def stage(pref, fn, wire=None):
        p.wire = dict(wire or {})
        with p.scope(pref):
            fn(p)
        p.wire = {}

    stage("z_", emit_YTZ)
    stage("a_", emit_LA)
    with p.scope("c1_"):
        p.coll(AG, ALU.bypass, "xn_tmb", "xn_tmb_full", [NTOK, D], BF16, RG8)
        p.coll(AG, ALU.bypass, "xn_fmb", "xn_fmb8", [NCORE * D, TSH], BF16, RG8)
    stage("d_", emit_LD, {"xnown": I["xn_fmb"]})
    with p.scope("c2_"):
        for nm in ("e0", "e1", "w0", "w1"):
            p.coll(AG, ALU.bypass, nm, nm + "F", [NTOK], F32, RG8)
    stage("e_", emit_LE, {"e0": I["e0F"], "e1": I["e1F"], "w0": I["w0F"],
                          "w1": I["w1F"], "xn_tmb": I["xn_tmb_full"]})
    stage("ds_", emit_DISPATCH)
    with p.scope("c3_"):
        p.coll(RSC, ALU.add, "xemH", "xeRS", [CAP, 1040], BF16, RG8)
    stage("oh_", emit_ONEHOT)
    stage("f_", emit_LF, {"xem": I["xeRS"]})
    with p.scope("c4_"):
        p.coll(RSC, ALU.add, "ytokP", "ytokRS", [TSH, D], BF16, RG8)
    stage("b1_", emit_LB1)
    with p.scope("b23_"):
        p.wire = {}
        p.prefix = "b2_"
        emit_LB2(p)
        p.prefix = "b3_"
        emit_LB3a(p)
    stage("b4_", emit_LB3b)
    with p.scope("c5_"):
        y0F = p.coll(AG, ALU.bypass, "y0_fmb", "y0F", [D, NTOK], BF16, RG8)
        y0own = p.dram("y0ownD", [D, TSH], BF16)
        p.swdma(lambda e, c: e.dma_start(out=y0own, in_=y0F[:, c*TSH:(c+1)*TSH]),
                ["y0F"], ["y0ownD"])
    stage("lc_", emit_LC, {"y0_fmb": I["y0F"], "y0own": I["y0ownD"]})
    stage("g_", emit_LG, {"xnown": I["xn_tm"], "xown": p.inputs["x"],
                          "s0own": I["s0_tm"], "ytok_own": I["ytokRS"]})
    with p.scope("c6_"):
        p.coll(AG, ALU.bypass, "xmax1c", "xmax8F", [NCORE, 1], F32, RG8)
    stage("h_", emit_LH, {"xqkv": I["xq1_fmb"], "makv": I["ma1"]})
    stage("k1_", emit_LK1, {"xn1own": I["xn1_tm"], "xmax8": I["xmax8F"]})
    with p.scope("c7_"):
        projB = p.coll(AG, ALU.bypass, "projkv", "projB", [2 * D, TSH], F32, RG2)
        qinD = p.dram("qinD", [512, LSEQ], F32)
        p.swdma(lambda e, c: e.dma_start(out=qinD[:, 0:TSH],
                in_=projB[(c % 2)*512:(c % 2)*512+512, :]), ["projB"], ["qinD"])
        p.swdma(lambda e, c: e.dma_start(out=qinD[:, TSH:LSEQ],
                in_=projB[D+(c % 2)*512:D+(c % 2)*512+512, :]), ["projB"], ["qinD"])
    stage("i_", emit_LI, {"qin": I["qinD"], "kvin": I["projkv"]})
    with p.scope("c8_"):
        y1B = p.coll(AG, ALU.bypass, "y1c", "y1B", [NCORE * 512, LSEQ], F32, RG8)
        y1own = p.dram("y1ownD", [D, TSH], F32)
        p.swdma(lambda e, c: e.dma_start(out=y1own[0:512, :],
                in_=y1B[(2*(c//2))*512:(2*(c//2))*512+512, (c % 2)*TSH:(c % 2)*TSH+TSH]),
                ["y1B"], ["y1ownD"])
        p.swdma(lambda e, c: e.dma_start(out=y1own[512:D, :],
                in_=y1B[(2*(c//2)+1)*512:(2*(c//2)+1)*512+512, (c % 2)*TSH:(c % 2)*TSH+TSH]),
                ["y1B"], ["y1ownD"])
    stage("j_", emit_LJ, {"y1own": I["y1ownD"]})
    with p.scope("c9_"):
        p.coll(AG, ALU.bypass, "hmaxc", "hmax8F", [NCORE, 1], F32, RG8)
    stage("k2_", emit_LK2, {"h_own": I["h_fmb"], "hmax8": I["hmax8F"],
                            "s1own": I["s1_fm"], "x1own": I["x1_tm"]})

    ins = []
    for c in range(NCORE):
        ins.append(dict(
            x=X2[c*TSH:(c+1)*TSH],
            iden=C["iden"], idenf=C["idenf"], cmask=C["cmask"], utri=C["utri"],
            cosq=C["cosq"], sinq=C["sinq"], cosk=C["cosk"], sink=C["sink"],
            wq=ar(Wqkv[:, 2*c*64:2*c*64+128]),
            wk=ar(Wqkv[:, D+c*128:D+(c+1)*128]),
            wv=ar(Wqkv[:, 2*D+c*128:2*D+(c+1)*128]),
            wltm=ar(WL[2*c:2*c+2]),
            vgs=ar(pars["V_gs"][2*c:2*c+2]), vt0=ar(pars["V_T0"][2*c:2*c+2]),
            btau=ar(pars["beta_tau"][2*c:2*c+2]), bgm=ar(pars["beta_gm"][2*c:2*c+2]),
            cch=ar(pars["C_ch"][2*c:2*c+2]), gam=ar(pars["gamma"][2*c:2*c+2]),
            alphap=pars["alpha_ppd"], icthr=pars["IC_threshold"],
            gate=ar(pars["gate"][2*c:2*c+2]),
            W_o=Wo0, Wr=Wr_, br=br_,
            w1e=ar(W1m[c]), b1e=ar(b1m[c]), w2e=ar(W2m[c]), b2e=ar(b2m[c]),
            gamma0=gam0, beta0=bet0,
            LHw=Wq1, LHb=bq1, U=U1, LJw=Wo1,
            L1w1=w1f_, L1b1=b1f_, L1w2=w2f_, L1b2=b2f_,
            cbase=np.array([[float(CAP * c)]], np.float32),
        ))
    res = p.run(ins)
    out = np.concatenate([r["out_tm"] for r in res], 0)
    return out.reshape(B, LSEQ, D)
